# version 01
# baseline (speedup 1.0000x reference)
import numpy as np
import ml_dtypes
from contextlib import ExitStack

import concourse.bass as bass
import concourse.mybir as mybir
from concourse.bass_utils import run_bass_kernel_spmd

F32 = mybir.dt.float32
BF16 = mybir.dt.bfloat16
AF = mybir.ActivationFunctionType
ALU = mybir.AluOpType
AX = mybir.AxisListType

NCORES = 8
B, T, D = 2, 8192, 1024
GT = B * T
NTOK = GT // NCORES
NTT = NTOK // 128
FCH = D // 128

ENGINES = ("sync", "tensor", "vector", "scalar", "gpsimd")


class _RecInstr:
    def __init__(self, prog, slot):
        self.prog, self.slot = prog, slot

    def inc(self, sem, n=1):
        h = self.prog.sem(sem)
        self.prog.cnt[sem] += n
        self.slot.append(lambda ins, h=h, n=n: ins.then_inc(h, n))
        return self


class _EngProxy:
    def __init__(self, prog, eng):
        self._p, self._e = prog, eng

    def wait(self, sem, v=None):
        p = self._p
        h = p.sem(sem)
        v = p.cnt[sem] if v is None else v
        if v > 0:
            p.on(self._e, lambda E, h=h, v=v: E.wait_ge(h, v))

    def __getattr__(self, op):
        p, e = self._p, self._e

        def rec(*args, **kwargs):
            slot = []

            def closure(E, op=op, args=args, kwargs=kwargs, slot=slot):
                ins = getattr(E, op)(*args, **kwargs)
                for f in slot:
                    f(ins)

            p.on(e, closure)
            return _RecInstr(p, slot)

        return rec


class Prog:

    def __init__(self):
        self.nc = bass.Bass("TRN2", target_bir_lowering=False)
        self.nc.detect_race_conditions = False
        self.es = ExitStack()
        self.q = {e: [] for e in ENGINES}
        self.sems = {}
        self.cnt = {}
        self._scoped = {}
        self._trgrp = 0
        self.sy = _EngProxy(self, "sync")
        self.te = _EngProxy(self, "tensor")
        self.ve = _EngProxy(self, "vector")
        self.sc = _EngProxy(self, "scalar")
        self.gp = _EngProxy(self, "gpsimd")

    def sb(self, name, shape, dt):
        return self.es.enter_context(self.nc.sbuf_tensor(name, shape, dt))

    def sb_tmp(self, name, shape, dt):
        cm = self.nc.sbuf_tensor(name, shape, dt)
        h = cm.__enter__()
        self._scoped[name] = cm
        return h

    def sb_free(self, name):
        cm = self._scoped.pop(name)
        cm.__exit__(None, None, None)

    def psum(self, name, shape, dt=F32):
        return self.es.enter_context(self.nc.psum_tensor(name, shape, dt))

    def din(self, name, shape, dt=F32):
        return self.nc.dram_tensor(name, shape, dt, kind="ExternalInput")

    def dout(self, name, shape, dt=F32):
        return self.nc.dram_tensor(name, shape, dt, kind="ExternalOutput")

    def dtmp(self, name, shape, dt, addr_space="Local"):
        return self.nc.dram_tensor(name, shape, dt, addr_space=addr_space)

    def use_consts(self, values):
        self.const_values = list(values)
        self._cvec_in = self.nc.dram_tensor("cvec", [128, max(1, len(values))], F32,
                                            kind="ExternalInput")
        self._cbuf = self.sb("cbuf", [128, max(1, len(values))], F32)
        self.sy.dma_start(out=self._cbuf[:], in_=self._cvec_in.ap()).inc("lc", 16)

    def cv(self, value):
        i = self.const_values.index(value)
        return self._cbuf[:, i:i + 1]

    def sem(self, name):
        if name not in self.sems:
            self.sems[name] = self.es.enter_context(self.nc.semaphore(name))
            self.cnt[name] = 0
        return self.sems[name]

    def val(self, name):
        self.sem(name)
        return self.cnt[name]

    def on(self, eng, f):
        self.q[eng].append(f)

    def finish(self):
        blk = self.es.enter_context(self.nc.Block())
        for name in reversed(list(self._scoped)):
            pass
        for e in ENGINES:
            ops = self.q[e]
            if not ops:
                continue
            getattr(blk, e)(lambda E, ops=ops: [f(E) for f in ops])
        blk.__class__
        for name in reversed(list(self._scoped)):
            self._scoped.pop(name).__exit__(None, None, None)
        self.es.close()
        return self.nc


def dap(t, off, dims):
    return bass.AP(t, off, dims)



def emit_ln(p, x_tm, xn_tm, stat, waste, nt, dwidth, eps_ap, gate=None):
    sc, ve = p.sc, p.ve
    if gate:
        sc.wait(*gate)
        ve.wait(*gate)
    for tt in range(nt):
        sc.activation(waste[:], x_tm[:, tt * dwidth:(tt + 1) * dwidth], AF.Square,
                      accum_out=stat[:, nt + tt:nt + tt + 1])
    sc.drain().inc("lns")
    se_sq = p.val("lns")

    ve.wait("lns", se_sq)
    for tt in range(nt):
        ve.tensor_reduce(stat[:, tt:tt + 1], x_tm[:, tt * dwidth:(tt + 1) * dwidth],
                         AX.X, ALU.add)
    ve.drain()
    S = lambda i: stat[:, i * nt:(i + 1) * nt]
    ve.tensor_scalar(S(2), S(0), 1.0 / dwidth, None, ALU.mult)
    ve.tensor_scalar(S(3), S(1), 1.0 / dwidth, None, ALU.mult)
    ve.drain()
    ve.tensor_tensor(S(4), S(2), S(2), ALU.mult)
    ve.drain()
    ve.tensor_tensor(S(5), S(3), S(4), ALU.subtract)
    ve.drain().inc("lnv")
    sc.wait("lnv", p.val("lnv"))
    sc.activation(S(6), S(5), AF.Sqrt, bias=eps_ap)
    sc.drain().inc("lns")
    ve.wait("lns", p.val("lns"))
    ve.reciprocal(S(7), S(6))
    ve.drain()
    for tt in range(nt):
        ve.tensor_scalar(xn_tm[:, tt * dwidth:(tt + 1) * dwidth],
                         x_tm[:, tt * dwidth:(tt + 1) * dwidth],
                         stat[:, 2 * nt + tt:2 * nt + tt + 1],
                         stat[:, 7 * nt + tt:7 * nt + tt + 1],
                         ALU.subtract, ALU.mult)
    ve.drain().inc("lnv")
    return ("lnv", p.val("lnv"))


def emit_transpose(p, groups, ident, pbA, pbB, gate_te=None, evac="vector"):
    te = p.te
    ev = p.ve if evac == "vector" else p.sc
    if gate_te:
        te.wait(*gate_te)
    for srcs, dst, cols in groups:
        pb = pbA if p._trgrp % 2 == 0 else pbB
        if p._trgrp >= 2:
            te.wait("evt", p.val("evt") - 1)
        off = 0
        last = None
        for s in srcs:
            s_ap, w = s[0], s[1]
            idap = s[2] if len(s) > 2 else ident[:]
            last = te.transpose(pb[:, off:off + w], s_ap, idap)
            off += w
        assert off == cols
        last.inc("mmt")
        ev.wait("mmt", p.val("mmt"))
        if evac == "vector":
            ev.tensor_copy(dst, pb[:, 0:cols]).inc("evt")
        else:
            ev.activation(dst, pb[:, 0:cols], AF.Copy).inc("evt")
        p._trgrp += 1
    return ("evt", p.val("evt"))



def build_prep(upto=None):
    p = Prog()
    x_own = p.din("x_own", [NTOK, D], F32)
    iden = p.din("iden", [128, 128], F32)
    wks1 = p.din("wks1", [D, 128], F32)
    wks2 = p.din("wks2", [D, 128], F32)
    wks3 = p.din("wks3", [D, 35], F32)
    wq1 = p.din("wq1", [D, 128], F32)
    wm1 = p.din("wm1", [D, 128], F32)
    wvm1 = p.din("wvm1", [D, 64], F32)
    wq2 = p.din("wq2", [64, D], F32)
    wqs2 = p.din("wqs2", [64, D], F32)
    wm2 = p.din("wm2", [64, D], F32)
    wms2 = p.din("wms2", [64, D], F32)
    wvm2 = p.din("wvm2", [64, D], F32)
    wk2t = p.din("wk2t", [D, 128], F32)

    o_xnT = p.dout("o_xnT", [D, NTOK], BF16)
    o_kside = p.dout("o_kside", [3, 128, NTOK], BF16)
    o_pT = p.dout("o_pT", [128, NTOK], BF16)
    o_mvoT = p.dout("o_mvoT", [D, NTOK], BF16)
    o_gram = p.dout("o_gram", [48, 64], F32)
    o_dbg = p.dout("o_dbg", [128, NTT * D], BF16)
    o_dbg2 = p.dout("o_dbg2", [128, 8 * NTT], F32)

    xnT = p.sb("xnT", [128, FCH * NTOK], BF16)
    stat = p.sb("stat", [128, 8 * NTT], F32)
    waste = p.sb("waste", [128, D], F32)
    identf = p.sb("identf", [128, 128], F32)
    identb = p.sb("identb", [128, 128], BF16)
    wbuf = p.sb("wbuf", [128, 56 * 128], BF16)
    w2buf = p.sb("w2buf", [64, 5 * D], BF16)
    ksd = p.sb("ksd", [128, 3 * NTOK], BF16)
    ph_q = p.sb("ph_q", [64, 2 * NTOK], BF16)
    ph_m = p.sb("ph_m", [64, 2 * NTOK], BF16)
    idb64 = p.sb("idb64", [128, 64], BF16)
    qbuf = p.sb("qbuf", [128, NTOK], BF16)
    pTsb = p.sb("pTsb", [128, NTOK], BF16)
    mvoh = p.sb("mvoh", [64, NTOK], BF16)
    gtm = p.sb("gtm", [128, NTT * 288], BF16)
    gsb = p.sb("gsb", [16, 3 * 64], F32)

    pbA = p.psum("pbA", [128, 1024], BF16)
    pbB = p.psum("pbB", [128, 1024], BF16)
    pg = p.psum("pg", [128, 2048], F32)
    pg2 = p.psum("pg2", [128, 512], F32)
    pgr = p.psum("pgr", [16, 192], F32)

    sy, te, ve, sc, gp = p.sy, p.te, p.ve, p.sc, p.gp
    p.use_consts([1e-5])
    p.sc.wait("lc", 16)
    p.ve.wait("lc", 16)
    x_tm = p.sb_tmp("x_tm", [128, NTT * D], F32)
    xn_tm = p.sb_tmp("xn_tm", [128, NTT * D], BF16)

    sy.dma_start(out=dap(x_tm, 0, [[NTT * D, 128], [D, NTT], [1, D]]),
                 in_=dap(x_own, 0, [[D, 128], [128 * D, NTT], [1, D]])).inc("ld", 16)
    sy.dma_start(out=identf[:], in_=iden.ap()).inc("ld", 16)
    gp.dma_start(out=identb[:], in_=iden.ap()).inc("lw", 16)
    gp.dma_start(out=idb64[64:128, 0:64], in_=iden.ap()[0:64, 0:64]).inc("lw", 16)

    wmap = {}
    nslot = [0]

    def load_w(name, ap, rows, cols):
        nch = rows // 128
        wmap[name] = (nslot[0], nch, cols)
        first = nslot[0]
        nslot[0] += nch
        gp.dma_start(
            out=dap(wbuf, first * 128, [[56 * 128, 128], [128, nch], [1, cols]]),
            in_=dap(ap, 0, [[cols, 128], [128 * cols, nch], [1, cols]]),
        ).inc("lw", 16)

    def wtile(name, c):
        first, nch, cols = wmap[name]
        return wbuf[:, (first + c) * 128:(first + c) * 128 + cols]

    load_w("wks1", wks1, D, 128)
    load_w("wks2", wks2, D, 128)
    load_w("wks3", wks3, D, 35)
    load_w("wq1", wq1, D, 128)
    load_w("wm1", wm1, D, 128)
    load_w("wk2t", wk2t, D, 128)
    load_w("wvm1", wvm1, D, 64)

    w2map = {}
    n2 = [0]

    def load_w2(name, ap):
        w2map[name] = n2[0]
        gp.dma_start(out=dap(w2buf, n2[0] * D, [[5 * D, 64], [1, D]]),
                     in_=dap(ap, 0, [[D, 64], [1, D]])).inc("lw", 16)
        n2[0] += 1

    def w2tile(name, m):
        return w2buf[:, w2map[name] * D + m * 128: w2map[name] * D + (m + 1) * 128]

    for nm, ap in [("wq2", wq2), ("wqs2", wqs2), ("wm2", wm2), ("wms2", wms2),
                   ("wvm2", wvm2)]:
        load_w2(nm, ap)

    ve.memset(ksd[:, 2 * NTOK:3 * NTOK], 0.0).inc("msk")
    sc.wait("msk", p.val("msk"))

    gate_xn = emit_ln(p, x_tm, xn_tm, stat, waste, NTT, D, p.cv(1e-5), gate=("ld", 32))

    def _cut(tag):
        if upto != tag:
            return False
        sy.wait("so", p.val("so"))
        for e in ("lnv", "lns", "evt", "kv", "pv", "qv", "vv", "gv"):
            pass
        p.finish()
        return True

    if upto == "ln":
        sy.wait("lnv", p.val("lnv"))
        sy.dma_start(out=dap(o_dbg, 0, [[NTT * D, 128], [1, NTT * D]]),
                     in_=xn_tm[:]).inc("so", 16)
        sy.dma_start(out=dap(o_dbg2, 0, [[8 * NTT, 128], [1, 8 * NTT]]),
                     in_=stat[:]).inc("so", 16)
        sy.wait("so", p.val("so"))
        p.finish()
        return p

    groups = []
    for fc in range(FCH):
        for g in range(2):
            srcs = [(xn_tm[:, (g * 8 + i) * D + fc * 128:(g * 8 + i) * D + (fc + 1) * 128], 128)
                    for i in range(8)]
            dst = xnT[:, fc * NTOK + g * 1024: fc * NTOK + (g + 1) * 1024]
            groups.append((srcs, dst, 1024))
    te.wait("lw")
    gate_xnT = emit_transpose(p, groups, identb, pbA, pbB, gate_te=gate_xn)
    sy.wait(*gate_xnT)
    sy.dma_start(out=dap(o_xnT, 0, [[NTOK, 128], [NTOK * 128, FCH], [1, NTOK]]),
                 in_=dap(xnT, 0, [[FCH * NTOK, 128], [NTOK, FCH], [1, NTOK]])).inc("so", 16)

    if upto == "xnT":
        sy.wait("so", p.val("so"))
        p.finish()
        return p

    te.wait("lw")
    te.wait("evt", p.val("evt"))
    for mt, (wn, cols) in enumerate([("wks1", 128), ("wks2", 128), ("wks3", 35)]):
        if mt >= 1:
            te.wait("kv", p.val("kv"))
        last = None
        for n in range(4):
            for kc in range(FCH):
                last = te.matmul(pg[0:cols, n * 512:(n + 1) * 512],
                                 wtile(wn, kc),
                                 xnT[:, kc * NTOK + n * 512: kc * NTOK + (n + 1) * 512],
                                 start=(kc == 0), stop=(kc == FCH - 1))
        last.inc("mmk")
        sc.wait("mmk", p.val("mmk"))
        sc.activation(ksd[0:cols, mt * NTOK:(mt + 1) * NTOK], pg[0:cols, :],
                      AF.Copy).inc("kv")
    sy.wait("kv", p.val("kv"))
    sy.dma_start(out=dap(o_kside, 0, [[NTOK, 128], [128 * NTOK, 2], [1, NTOK]]),
                 in_=dap(ksd, 0, [[3 * NTOK, 128], [NTOK, 2], [1, NTOK]])).inc("so", 16)
    sy.dma_start(out=dap(o_kside, 2 * 128 * NTOK, [[NTOK, 35], [1, NTOK]]),
                 in_=dap(ksd, 2 * NTOK, [[3 * NTOK, 35], [1, NTOK]])).inc("so", 16)

    if upto == "kside":
        sy.wait("so", p.val("so"))
        p.finish()
        return p

    te.wait("kv", p.val("kv"))
    for buf in range(2):
        te.wait("pv", p.val("pv"))
        last = None
        for n in range(4):
            for kc in range(FCH):
                te.matmul(pg[0:64, n * 512:(n + 1) * 512],
                          wtile("wq1", kc)[:, buf * 64:buf * 64 + 64],
                          xnT[:, kc * NTOK + n * 512: kc * NTOK + (n + 1) * 512],
                          start=(kc == 0), stop=(kc == FCH - 1),
                          tile_position=(0, 0))
            for kc in range(FCH):
                last = te.matmul(pg[64:128, n * 512:(n + 1) * 512],
                                 wtile("wm1", kc)[:, buf * 64:buf * 64 + 64],
                                 xnT[:, kc * NTOK + n * 512: kc * NTOK + (n + 1) * 512],
                                 start=(kc == 0), stop=(kc == FCH - 1),
                                 tile_position=(0, 64))
        last.inc("mmp")
        sc.wait("mmp", p.val("mmp"))
        sc.activation(ph_q[:, buf * NTOK:(buf + 1) * NTOK], pg[0:64, :], AF.Copy)
        sc.activation(ph_m[:, buf * NTOK:(buf + 1) * NTOK], pg[64:128, :], AF.Copy).inc("pv")

    if upto == "stageA":
        sy.wait("pv", p.val("pv"))
        sy.dma_start(out=dap(o_dbg, 0, [[2 * NTOK, 64], [1, 2 * NTOK]]),
                     in_=ph_q[:]).inc("so", 16)
        sy.dma_start(out=dap(o_dbg, 64 * 2 * NTOK, [[2 * NTOK, 64], [1, 2 * NTOK]]),
                     in_=ph_m[:]).inc("so", 16)
        sy.wait("so", p.val("so"))
        p.finish()
        return p

    p.sb_free("xn_tm")
    p.sb_free("x_tm")
    qm = p.sb_tmp("qm", [128, FCH * NTOK], BF16)
    mvoT = p.sb_tmp("mvoT", [128, 4 * NTOK], BF16)

    te.wait("pv", p.val("pv"))
    for half in range(2):
        qh_rows = ph_q[:, half * NTOK:(half + 1) * NTOK]
        mgh_rows = ph_m[:, half * NTOK:(half + 1) * NTOK]
        wq2n = "wq2" if half == 0 else "wqs2"
        wm2n = "wm2" if half == 0 else "wms2"
        for m in range(FCH):
            te.wait("qv", p.val("qv"))
            last = None
            for n in range(4):
                last = te.matmul(pg[:, n * 512:(n + 1) * 512], w2tile(wq2n, m),
                                 qh_rows[:, n * 512:(n + 1) * 512], start=True, stop=True)
            last.inc("mmq")
            sc.wait("mmq", p.val("mmq"))
            sc.activation(qbuf[:], pg[:], AF.Copy).inc("sq")
            ve.wait("sq", p.val("sq"))
            for n in range(4):
                te.wait("qv2", p.val("qv2"))
                te.matmul(pg2[:], w2tile(wm2n, m),
                          mgh_rows[:, n * 512:(n + 1) * 512],
                          start=True, stop=True).inc("mmq2")
                ve.wait("mmq2", p.val("mmq2"))
                ve.tensor_tensor(qm[:, m * NTOK + n * 512:m * NTOK + (n + 1) * 512],
                                 qbuf[:, n * 512:(n + 1) * 512], pg2[:],
                                 ALU.mult).inc("qv2")
            ve.engine_nop().inc("qv")
        te.wait("qv", p.val("qv"))
        last = None
        for n in range(4):
            for kc in range(FCH):
                last = te.matmul(pg[(half * 64):(half * 64 + 64), n * 512:(n + 1) * 512],
                                 wtile("wk2t", kc)[:, half * 64:half * 64 + 64],
                                 qm[:, kc * NTOK + n * 512: kc * NTOK + (n + 1) * 512],
                                 start=(kc == 0), stop=(kc == FCH - 1),
                                 tile_position=(0, half * 64))
        last.inc("mmq")
        sc.wait("mmq", p.val("mmq"))
        sc.activation(pTsb[(half * 64):(half * 64 + 64), :],
                      pg[(half * 64):(half * 64 + 64), :], AF.Copy).inc("qv")
        te.wait("qv", p.val("qv"))
    sy.wait("qv", p.val("qv"))
    sy.dma_start(out=o_pT.ap(), in_=pTsb[:]).inc("so", 16)
    if upto == "stageB":
        sy.wait("so", p.val("so"))
        p.finish()
        return p

    last = None
    for n in range(4):
        for kc in range(FCH):
            last = te.matmul(pg[0:64, n * 512:(n + 1) * 512], wtile("wvm1", kc),
                             xnT[:, kc * NTOK + n * 512: kc * NTOK + (n + 1) * 512],
                             start=(kc == 0), stop=(kc == FCH - 1))
    last.inc("mmv")
    sc.wait("mmv", p.val("mmv"))
    sc.activation(mvoh[:], pg[0:64, :], AF.Copy).inc("vv")
    if upto == "mvoh":
        sy.wait("vv", p.val("vv"))
        p.finish()
        return p
    te.wait("vv", p.val("vv"))
    for m in range(FCH):
        last = None
        for n in range(4):
            last = te.matmul(pg[:, n * 512:(n + 1) * 512], w2tile("wvm2", m),
                             mvoh[:, n * 512:(n + 1) * 512], start=True, stop=True)
        last.inc("mmv")
        sc.wait("mmv", p.val("mmv"))
        sc.activation(mvoT[:, (m % 4) * NTOK:(m % 4 + 1) * NTOK], pg[:], AF.Copy).inc("vv")
        te.wait("vv", p.val("vv"))
        if m % 4 == 3:
            sy.wait("vv", p.val("vv"))
            sy.dma_start(out=dap(o_mvoT, (m - 3) * 128 * NTOK,
                                 [[NTOK, 128], [NTOK * 128, 4], [1, NTOK]]),
                         in_=dap(mvoT, 0, [[4 * NTOK, 128], [NTOK, 4], [1, NTOK]])).inc("so", 16)
            sc.wait("so", p.val("so"))
    if upto == "mvo":
        sy.wait("so", p.val("so"))
        p.finish()
        return p

    te.wait("kv", p.val("kv"))
    for tt in range(NTT):
        pb = pbA if p._trgrp % 2 == 0 else pbB
        if p._trgrp >= 2:
            te.wait("evt", p.val("evt") - 1)
        for i in range(3):
            last = te.transpose(pb[:, i * 128:(i + 1) * 128],
                                ksd[:, i * NTOK + tt * 128:i * NTOK + (tt + 1) * 128],
                                identb[:])
        last.inc("mmt")
        ve.wait("mmt", p.val("mmt"))
        ve.tensor_copy(gtm[:, tt * 288:tt * 288 + 256], pb[:, 0:256])
        ve.tensor_copy(gtm[:, tt * 288 + 256:tt * 288 + 288], pb[:, 256:288]).inc("evt")
        p._trgrp += 1
    gate_g = ("evt", p.val("evt"))
    if upto == "gtm":
        sy.wait(*gate_g)
        p.finish()
        return p
    te.wait(*gate_g)
    OF = {"Kf": 0, "Kfs": 64, "Vf": 128, "B1": 192, "B1s": 208, "A1": 224,
          "A1s": 240, "A2": 256, "B2": 272}
    TTW = 288
    last = None
    for gi, (wa, wb) in enumerate([("B1", "Kf"), ("B1s", "Kfs"), ("A2", "Vf")]):
        for tt in range(NTT):
            g = lambda nm, w, tt=tt: gtm[:, tt * TTW + OF[nm]: tt * TTW + OF[nm] + w]
            last = te.matmul(pgr[:, gi * 64:(gi + 1) * 64], g(wa, 16), g(wb, 64),
                             start=(tt == 0), stop=(tt == NTT - 1))
    last.inc("mmg")
    ve.wait("mmg", p.val("mmg"))
    ve.tensor_copy(gsb[:], pgr[:]).inc("gv")
    if upto == "gmm":
        sy.wait("gv", p.val("gv"))
        p.finish()
        return p
    sy.wait("gv", p.val("gv"))
    sy.dma_start(out=dap(o_gram, 0, [[64, 16], [16 * 64, 3], [1, 64]]),
                 in_=dap(gsb, 0, [[192, 16], [64, 3], [1, 64]])).inc("so", 16)

    sy.wait("so", p.val("so"))
    p.finish()
    return p



def _f32(a):
    return np.ascontiguousarray(np.asarray(a), dtype=np.float32)


def prep_inmaps(inputs):
    x = _f32(inputs["L0_attn_x"]).reshape(GT, D)
    g = lambda n: _f32(inputs[n])
    cat = lambda *a: np.ascontiguousarray(np.concatenate(a, axis=1))
    shared = dict(
        iden=np.eye(128, dtype=np.float32),
        wks1=cat(g("L0_attn_qk_Wk1"), g("L0_attn_qks_Wk1")),
        wks2=cat(g("L0_attn_vo_Wv1"), g("L0_attn_sm1_Wb"), g("L0_attn_sm1s_Wb"),
                 g("L0_attn_sm1_Wa"), g("L0_attn_sm1s_Wa")),
        wks3=cat(g("L0_attn_sm2_Wa"), g("L0_attn_sm2_Wb"),
                 g("L0_attn_sm1_Wd"), g("L0_attn_sm1s_Wd"), g("L0_attn_sm2_Wd")),
        wq1=cat(g("L0_attn_qk_Wq1"), g("L0_attn_qks_Wq1")),
        wm1=cat(g("L0_attn_qk_Wm1"), g("L0_attn_qks_Wm1")),
        wvm1=g("L0_attn_vo_Wm1"),
        wq2=g("L0_attn_qk_Wq2"), wqs2=g("L0_attn_qks_Wq2"),
        wm2=g("L0_attn_qk_Wm2"), wms2=g("L0_attn_qks_Wm2"),
        wvm2=g("L0_attn_vo_Wm2"),
        wk2t=cat(np.ascontiguousarray(g("L0_attn_qk_Wk2").T),
                 np.ascontiguousarray(g("L0_attn_qks_Wk2").T)),
    )
    shared["cvec"] = np.full((128, 1), 1e-5, dtype=np.float32)
    maps = []
    for c in range(NCORES):
        m = dict(shared)
        m["x_own"] = np.ascontiguousarray(x[c * NTOK:(c + 1) * NTOK])
        maps.append(m)
    return maps





TB = T
NJC = TB // 512
NJT = TB // 128

def build_l0attn():
    p = Prog()
    kside_b = p.din("kside_b", [3, 128, TB], BF16)
    pT_own = p.din("pT_own", [128, NTOK], BF16)
    mvoT_own = p.din("mvoT_own", [D, NTOK], BF16)
    gram4 = p.din("gram4", [4, 48, 64], F32)
    wv2 = p.din("wv2", [64, D], F32)
    wot = p.din("wot", [D, D], F32)
    ones_in = p.din("ones_in", [1, 128], F32)
    iden = p.din("iden", [128, 128], F32)
    o_s0T = p.dout("o_s0T", [D, NTOK], F32)

    sy, te, ve, sc, gp = p.sy, p.te, p.ve, p.sc, p.gp
    p.use_consts([1e-8])
    sc.wait("lc", 16)
    ve.wait("lc", 16)

    keff = p.sb("keff", [128, TB], BF16)
    veff_tm = p.sb("veff_tm", [128, NJT * 128], BF16)
    ctxT = p.sb("ctxT", [64, NTOK], BF16)
    pTsb = p.sb("pTsb", [128, NTOK], BF16)
    wv2b = p.sb("wv2b", [64, D], BF16)
    onesb = p.sb("onesb", [1, 128], BF16)
    identb = p.sb("identb", [128, 128], BF16)
    g1b = p.sb("g1b", [16, 64], BF16)
    g1sb = p.sb("g1sb", [16, 64], BF16)
    n2b = p.sb("n2b", [16, 64], BF16)
    gsum = p.sb("gsum", [16, 3 * 4 * 64], F32)
    nrm = p.sb("nrm", [128, 16], F32)

    pA1 = p.psum("pA1", [128, 1024], F32)
    pA2 = p.psum("pA2", [128, 1024], F32)
    pb = p.psum("pb", [128, 1024], BF16)
    pctx = p.psum("pctx", [128, 512], F32)
    pm1 = p.psum("pm1", [128, 512], F32)
    pm2 = p.psum("pm2", [128, 512], F32)

    ksb = p.sb_tmp("ksb", [128, 3 * TB], BF16)
    vefful = p.sb_tmp("vefful", [128, TB], BF16)
    a1r = p.sb_tmp("a1r", [16, TB], BF16)
    a1sr = p.sb_tmp("a1sr", [16, TB], BF16)
    b2r = p.sb_tmp("b2r", [16, TB], BF16)
    d1r = p.sb_tmp("d1r", [1, TB], BF16)
    d1sr = p.sb_tmp("d1sr", [1, TB], BF16)
    d2r = p.sb_tmp("d2r", [1, TB], BF16)

    sy.dma_start(out=dap(ksb, 0, [[3 * TB, 128], [TB, 3], [1, TB]]),
                 in_=dap(kside_b, 0, [[TB, 128], [128 * TB, 3], [1, TB]])).inc("ld", 16)
    sy.dma_start(out=pTsb[:], in_=pT_own.ap()).inc("ld", 16)
    gp.dma_start(out=identb[:], in_=iden.ap()).inc("lw", 16)
    gp.dma_start(out=onesb[:], in_=ones_in.ap()).inc("lw", 16)
    gp.dma_start(out=wv2b[:], in_=wv2.ap()).inc("lw", 16)

    for gi in range(3):
        sy.dma_start(out=dap(gsum, gi * 256, [[3 * 4 * 64, 16], [64, 4], [1, 64]]),
                     in_=dap(gram4, gi * 16 * 64, [[64, 16], [48 * 64, 4], [1, 64]])).inc("ld", 16)
    ve.wait("ld", p.val("ld"))
    for gi, dst in enumerate([g1b, g1sb, n2b]):
        b0 = gi * 256
        ve.tensor_tensor(gsum[:, b0:b0 + 64], gsum[:, b0:b0 + 64],
                         gsum[:, b0 + 64:b0 + 128], ALU.add)
        ve.tensor_tensor(gsum[:, b0 + 128:b0 + 192], gsum[:, b0 + 128:b0 + 192],
                         gsum[:, b0 + 192:b0 + 256], ALU.add)
        ve.drain()
        ve.tensor_tensor(dst[:], gsum[:, b0:b0 + 64], gsum[:, b0 + 128:b0 + 192],
                         ALU.add)
    ve.drain().inc("gv")

    gp.wait("ld", p.val("ld"))
    gp.dma_start(out=a1r[:], in_=ksb[96:112, 1 * TB:2 * TB]).inc("lw", 16)
    gp.dma_start(out=a1sr[:], in_=ksb[112:128, 1 * TB:2 * TB]).inc("lw", 16)
    gp.dma_start(out=b2r[:], in_=ksb[16:32, 2 * TB:3 * TB]).inc("lw", 16)
    gp.dma_start(out=d1r[:], in_=ksb[32:33, 2 * TB:3 * TB]).inc("lw", 16)
    gp.dma_start(out=d1sr[:], in_=ksb[33:34, 2 * TB:3 * TB]).inc("lw", 16)
    gp.dma_start(out=d2r[:], in_=ksb[34:35, 2 * TB:3 * TB]).inc("lw", 16)

    ve.memset(vefful[64:128, :], 0.0).inc("vz")

    te.wait("lw", p.val("lw"))
    te.wait("gv", p.val("gv"))
    te.wait("ld", p.val("ld"))
    KF = ksb[0:64, 0:TB]
    KFS = ksb[64:128, 0:TB]
    VF = ksb[0:64, TB:2 * TB]
    for n in range(NJC):
        c0 = n * 512
        te.matmul(pm1[0:64, :], g1b[:], a1r[:, c0:c0 + 512], start=True, stop=True,
                  tile_position=(0, 0))
        te.matmul(pm1[64:128, :], g1sb[:], a1sr[:, c0:c0 + 512], start=True, stop=True,
                  tile_position=(0, 64))
        te.matmul(pm2[0:64, :], onesb[0:1, 0:64], d1r[:, c0:c0 + 512],
                  start=True, stop=True, tile_position=(0, 0))
        last = te.matmul(pm2[64:128, :], onesb[0:1, 0:64], d1sr[:, c0:c0 + 512],
                         start=True, stop=True, tile_position=(0, 64))
        last.inc("mk")
        ve.wait("mk", p.val("mk"))
        ve.tensor_tensor(keff[0:64, c0:c0 + 512], KF[:, c0:c0 + 512], pm2[0:64, :],
                         ALU.mult)
        ve.tensor_tensor(keff[64:128, c0:c0 + 512], KFS[:, c0:c0 + 512],
                         pm2[64:128, :], ALU.mult)
        ve.drain()
        ve.tensor_tensor(keff[0:64, c0:c0 + 512], keff[0:64, c0:c0 + 512],
                         pm1[0:64, :], ALU.add)
        ve.tensor_tensor(keff[64:128, c0:c0 + 512], keff[64:128, c0:c0 + 512],
                         pm1[64:128, :], ALU.add)
        ve.drain().inc("kv")
        te.wait("kv", p.val("kv"))
        te.matmul(pm1[0:64, :], n2b[:], b2r[:, c0:c0 + 512], start=True, stop=True,
                  tile_position=(0, 0))
        te.matmul(pm2[0:64, :], onesb[0:1, 0:64], d2r[:, c0:c0 + 512],
                  start=True, stop=True, tile_position=(0, 0)).inc("mk")
        ve.wait("mk", p.val("mk"))
        ve.tensor_tensor(vefful[0:64, c0:c0 + 512], VF[:, c0:c0 + 512], pm2[0:64, :],
                         ALU.mult)
        ve.drain()
        ve.tensor_tensor(vefful[0:64, c0:c0 + 512], vefful[0:64, c0:c0 + 512],
                         pm1[0:64, :], ALU.add)
        ve.drain().inc("kv")
        te.wait("kv", p.val("kv"))

    te.wait("vz", p.val("vz"))
    for g in range(8):
        pbx = pb
        if g >= 1:
            te.wait("evt", p.val("evt"))
        lastt = None
        for i in range(8):
            jt = g * 8 + i
            lastt = te.transpose(pbx[:, i * 128:(i + 1) * 128],
                                 vefful[:, jt * 128:(jt + 1) * 128], identb[:])
        lastt.inc("mmt")
        ve.wait("mmt", p.val("mmt"))
        ve.tensor_copy(veff_tm[:, g * 1024:(g + 1) * 1024], pbx[:]).inc("evt")
    ve.drain().inc("evt")

    for nm in ["d2r", "d1sr", "d1r", "b2r", "a1sr", "a1r", "vefful", "ksb"]:
        p.sb_free(nm)
    hm_sb = p.sb_tmp("hm_sb", [128, TB], BF16)
    sp_sb = p.sb_tmp("sp_sb", [128, TB], BF16)
    h_sb = p.sb_tmp("h_sb", [128, TB], BF16)
    hT_sb = p.sb_tmp("hT_sb", [128, TB], BF16)
    ab_sb = p.sb_tmp("ab_sb", [128, 1024], F32)
    rl_sb = p.sb_tmp("rl_sb", [128, 2 * 1024], F32)

    NBLK = NTOK // 128
    for ib in range(NBLK):
        lhs_p = pTsb[0:64, ib * 128:(ib + 1) * 128]
        lhs_ps = pTsb[64:128, ib * 128:(ib + 1) * 128]
        for cg in range(8):
            pA = pA1 if cg % 2 == 0 else pA2
            if cg >= 2:
                te.wait("hmv", p.val("hmv") - 1)
            elif ib > 0:
                te.wait("spv", p.val("spv"))
            for half in range(2):
                c0 = cg * 1024 + half * 512
                ins = te.matmul(pA[:, half * 512:(half + 1) * 512], lhs_p,
                                keff[0:64, c0:c0 + 512], start=True, stop=True,
                                tile_position=(0, 0))
            ins.inc("mhm")
            sc.wait("mhm", p.val("mhm"))
            sc.activation(hm_sb[:, cg * 1024:(cg + 1) * 1024], pA[:], AF.Copy)
            sc.activation(ab_sb[:], pA[:], AF.Square,
                          accum_out=nrm[:, cg:cg + 1])
            sc.drain().inc("hmv")
        ve.wait("hmv", p.val("hmv"))
        ve.tensor_reduce(nrm[:, 8:9], nrm[:, 0:8], AX.X, ALU.add)
        ve.drain().inc("nv")
        sc.wait("nv", p.val("nv"))
        sc.activation(nrm[:, 9:10], nrm[:, 8:9], AF.Sqrt)
        sc.drain().inc("ns")
        ve.wait("ns", p.val("ns"))
        ve.tensor_scalar(nrm[:, 10:11], nrm[:, 9:10], 1e-8, None, ALU.add)
        ve.drain()
        ve.reciprocal(nrm[:, 11:12], nrm[:, 10:11])
        ve.drain().inc("nv")
        for cg in range(8):
            pA = pA1 if cg % 2 == 0 else pA2
            if cg >= 2:
                te.wait("spv", p.val("spv") - 1)
            else:
                te.wait("hmv", p.val("hmv"))
            for half in range(2):
                c0 = cg * 1024 + half * 512
                ins = te.matmul(pA[:, half * 512:(half + 1) * 512], lhs_ps,
                                keff[64:128, c0:c0 + 512], start=True, stop=True,
                                tile_position=(64, 0))
            ins.inc("mhs")
            sc.wait("mhs", p.val("mhs"))
            sc.activation(ab_sb[:], pA[:], AF.Abs)
            sc.activation(rl_sb[:, 0:1024], pA[:], AF.Relu)
            sc.activation(ab_sb[:], ab_sb[:], AF.Exp, scale=-1.0)
            sc.activation(rl_sb[:, 1024:2048], ab_sb[:], AF.Ln, bias=1.0)
            sc.drain().inc("sps")
            ve.wait("sps", p.val("sps"))
            ve.tensor_tensor(sp_sb[:, cg * 1024:(cg + 1) * 1024], rl_sb[:, 0:1024],
                             rl_sb[:, 1024:2048], ALU.add)
            ve.drain().inc("spv")
        ve.wait("nv", p.val("nv"))
        for cg in range(8):
            ve.tensor_scalar(sp_sb[:, cg * 1024:(cg + 1) * 1024],
                             sp_sb[:, cg * 1024:(cg + 1) * 1024],
                             nrm[:, 11:12], None, ALU.mult)
        ve.drain()
        for cg in range(8):
            ve.scalar_tensor_tensor(h_sb[:, cg * 1024:(cg + 1) * 1024],
                                    hm_sb[:, cg * 1024:(cg + 1) * 1024], 0.0,
                                    sp_sb[:, cg * 1024:(cg + 1) * 1024],
                                    ALU.max, ALU.mult)
        ve.drain().inc("hv")
        te.wait("hv", p.val("hv"))
        for g in range(8):
            te.wait("hevt", p.val("hevt"))
            if ib == 0 and g == 0:
                te.wait("evt", p.val("evt"))
                te.wait("ctv", p.val("ctv"))
            lastt = None
            for i in range(8):
                jt = g * 8 + i
                lastt = te.transpose(pb[:, i * 128:(i + 1) * 128],
                                     h_sb[:, jt * 128:(jt + 1) * 128], identb[:])
            lastt.inc("mmt")
            ve.wait("mmt", p.val("mmt"))
            ve.tensor_copy(hT_sb[:, g * 1024:(g + 1) * 1024], pb[:]).inc("hevt")
        te.wait("hevt", p.val("hevt"))
        lastc = None
        for jt in range(NJT):
            lastc = te.matmul(pctx[:, 0:64], hT_sb[:, jt * 128:(jt + 1) * 128],
                              veff_tm[:, jt * 128:jt * 128 + 64],
                              start=(jt == 0), stop=(jt == NJT - 1))
        lastc.inc("mc")
        ve.wait("mc", p.val("mc"))
        ve.tensor_copy(hm_sb[:, 0:64], pctx[:, 0:64])
        ve.drain().inc("cv")
        te.wait("cv", p.val("cv"))
        te.wait("hevt", p.val("hevt"))
        te.transpose(pb[0:64, 0:128], hm_sb[:, 0:64], identb[:]).inc("mmt")
        ve.wait("mmt", p.val("mmt"))
        ve.tensor_copy(ctxT[0:64, ib * 128:(ib + 1) * 128], pb[0:64, 0:128])
        ve.drain().inc("ctv")
        te.wait("ctv", p.val("ctv"))

    for nm in ["rl_sb", "ab_sb", "hT_sb", "h_sb", "sp_sb", "hm_sb"]:
        p.sb_free(nm)
    mvoT = p.sb_tmp("mvoT", [128, FCH * NTOK], BF16)
    cm_sb = p.sb_tmp("cm_sb", [128, FCH * NTOK], BF16)
    wotb = p.sb_tmp("wotb", [128, FCH * D], BF16)
    s0st = p.sb_tmp("s0st", [128, 2 * NTOK], F32)

    gp.wait("ctv", p.val("ctv"))
    gp.dma_start(out=dap(mvoT, 0, [[FCH * NTOK, 128], [NTOK, FCH], [1, NTOK]]),
                 in_=dap(mvoT_own, 0, [[NTOK, 128], [128 * NTOK, FCH], [1, NTOK]])
                 ).inc("lw2", 16)
    gp.dma_start(out=dap(wotb, 0, [[FCH * D, 128], [D, FCH], [1, D]]),
                 in_=dap(wot, 0, [[D, 128], [128 * D, FCH], [1, D]])).inc("lw2", 16)

    te.wait("lw2", p.val("lw2"))
    for m in range(FCH):
        for g in range(2):
            pA = pA1 if g % 2 == 0 else pA2
            te.wait("cfv", p.val("cfv") - (1 if (m * 2 + g) >= 2 else 0))                 if (m * 2 + g) >= 2 else None
            for half in range(2):
                n0 = g * 1024 + half * 512
                te.matmul(pA[:, half * 512:(half + 1) * 512],
                          wv2b[:, m * 128:(m + 1) * 128],
                          ctxT[:, n0:n0 + 512], start=True, stop=True).inc(
                              "mcf" if half == 1 else "mcf0")
            ve.wait("mcf", p.val("mcf"))
            ve.tensor_tensor(cm_sb[:, m * NTOK + g * 1024:m * NTOK + (g + 1) * 1024],
                             mvoT[:, m * NTOK + g * 1024:m * NTOK + (g + 1) * 1024],
                             pA[:], ALU.mult)
            ve.drain().inc("cfv")
    for m in range(FCH):
        for g in range(2):
            pA = pA1 if g % 2 == 0 else pA2
            te.wait("s0v", p.val("s0v") - (1 if (m * 2 + g) >= 2 else 0))                 if (m * 2 + g) >= 2 else None
            if m == 0 and g < 2:
                te.wait("cfv", p.val("cfv"))
            lastm = None
            for half in range(2):
                n0 = g * 1024 + half * 512
                for kc in range(FCH):
                    lastm = te.matmul(pA[:, half * 512:(half + 1) * 512],
                                      wotb[:, kc * D + m * 128:kc * D + (m + 1) * 128],
                                      cm_sb[:, kc * NTOK + n0:kc * NTOK + n0 + 512],
                                      start=(kc == 0), stop=(kc == FCH - 1))
            lastm.inc("ms0")
            sc.wait("ms0", p.val("ms0"))
            buf = m % 2
            sc.activation(s0st[:, buf * NTOK + g * 1024:buf * NTOK + (g + 1) * 1024],
                          pA[:], AF.Copy)
            sc.drain().inc("s0v")
        sy.wait("s0v", p.val("s0v"))
        if m >= 2:
            pass
        sy.dma_start(out=dap(o_s0T, m * 128 * NTOK, [[NTOK, 128], [1, NTOK]]),
                     in_=s0st[:, (m % 2) * NTOK:(m % 2 + 1) * NTOK]).inc("so", 16)
        sc.wait("so", p.val("so"))

    sy.wait("so", p.val("so"))
    p.finish()
    return p


def l0attn_inmaps(inputs, prep_outs):
    g = lambda n: _f32(inputs[n])
    kside_full = np.concatenate(prep_outs["kside"], axis=2)
    gram_all = np.stack(prep_outs["gram"])
    shared = dict(
        wv2=g("L0_attn_vo_Wv2"),
        wot=np.ascontiguousarray(g("L0_attn_vo_Wo").T),
        ones_in=np.ones((1, 128), dtype=np.float32),
        iden=np.eye(128, dtype=np.float32),
        cvec=np.full((128, 1), 1e-8, dtype=np.float32),
    )
    maps = []
    for c in range(NCORES):
        b = c // 4
        m = dict(shared)
        m["kside_b"] = np.ascontiguousarray(kside_full[:, :, b * T:(b + 1) * T])
        m["pT_own"] = prep_outs["pT"][c]
        m["mvoT_own"] = prep_outs["mvoT"][c]
        m["gram4"] = np.ascontiguousarray(gram_all[b * 4:(b + 1) * 4])
        maps.append(m)
    return maps



def build_l0ffn():
    p = Prog()
    xnT_own = p.din("xnT_own", [D, NTOK], BF16)
    w1t = p.din("w1t", [D, 4096], F32)
    w2t = p.din("w2t", [D, 4096], F32)
    h1t = p.din("h1t", [4096, 4096], F32)
    h2t = p.din("h2t", [4096, 4096], F32)
    w3t = p.din("w3t", [4096, D], F32)
    sc_in = p.din("sc_in", [128, 4 * 32 + 8], F32)
    o_g0T = p.dout("o_g0T", [D, NTOK], F32)

    sy, te, ve, sc, gp = p.sy, p.te, p.ve, p.sc, p.gp

    xnT = p.sb("xnT", [128, FCH * NTOK], BF16)
    scs = p.sb("scs", [128, 4 * 32 + 8], F32)
    bufA = p.sb("bufA", [128, 32 * 512], BF16)
    bufB = p.sb("bufB", [128, 32 * 512], BF16)
    t1b = p.sb("t1b", [128, 512], BF16)
    t2b = p.sb("t2b", [128, 512], BF16)
    wta = p.sb("wta", [128, 32 * 128], BF16)
    wtb = p.sb("wtb", [128, 32 * 128], BF16)
    stg = p.sb("stg", [128, 2 * 512], F32)

    pA1 = p.psum("pA1", [128, 512], F32)
    pA2 = p.psum("pA2", [128, 512], F32)
    pB1 = p.psum("pB1", [128, 512], F32)
    pB2 = p.psum("pB2", [128, 512], F32)

    sy.dma_start(out=dap(xnT, 0, [[FCH * NTOK, 128], [NTOK, FCH], [1, NTOK]]),
                 in_=dap(xnT_own, 0, [[NTOK, 128], [128 * NTOK, FCH], [1, NTOK]])
                 ).inc("ld", 16)
    sy.dma_start(out=scs[:], in_=sc_in.ap()).inc("ld", 16)
    SC = {"s1": 0, "s2": 32, "sh1": 64, "sh2": 96, "s3": 128}

    def scl(nm, m):
        return scs[:, SC[nm] + m:SC[nm] + m + 1]

    TT4 = 4
    for tp in range(TT4):
        tok0 = tp * 512
        xn_t = lambda kc: xnT[:, kc * NTOK + tok0: kc * NTOK + tok0 + 512]
        mh_base = p.val("mh")
        for m in range(32):
            wt = wta if m % 2 == 0 else wtb
            if m >= 2:
                gp.wait("mh", mh_base + 2 * (m - 1))
            elif p.val("m3") > 0:
                gp.wait("m3", p.val("m3"))
            lsem = "lwfa" if m % 2 == 0 else "lwfb"
            gp.dma_start(out=dap(wt, 0, [[32 * 128, 128], [128, FCH], [1, 128]]),
                         in_=dap(w1t, m * 128, [[4096, 128], [128 * 4096, FCH],
                                                [1, 128]])).inc(lsem, 16)
            gp.dma_start(out=dap(wt, FCH * 128, [[32 * 128, 128], [128, FCH], [1, 128]]),
                         in_=dap(w2t, m * 128, [[4096, 128], [128 * 4096, FCH],
                                                [1, 128]])).inc(lsem, 16)
            te.wait(lsem, p.val(lsem))
            pA = pA1 if m % 2 == 0 else pA2
            pB = pB1 if m % 2 == 0 else pB2
            if p.val("he") >= 2:
                te.wait("he", p.val("he") - 1)
            for kc in range(FCH):
                ins = te.matmul(pA[:], wt[:, kc * 128:(kc + 1) * 128], xn_t(kc),
                                start=(kc == 0), stop=(kc == FCH - 1))
            ins.inc("mh")
            for kc in range(FCH):
                ins = te.matmul(pB[:], wt[:, (FCH + kc) * 128:(FCH + kc + 1) * 128],
                                xn_t(kc), start=(kc == 0), stop=(kc == FCH - 1))
            ins.inc("mh")
            sc.wait("mh", p.val("mh"))
            sc.activation(t1b[:], pA[:], AF.Silu, scale=scl("s1", m))
            sc.activation(t2b[:], pB[:], AF.Copy, scale=scl("s2", m))
            sc.drain().inc("he")
            ve.wait("he", p.val("he"))
            ve.tensor_tensor(bufA[:, m * 512:(m + 1) * 512], t1b[:], t2b[:], ALU.mult)
            ve.drain().inc("hv")
            te.wait("hv", p.val("hv"))
        for stage, (wsrc, snm, src_buf, dst_buf) in enumerate(
                [(h1t, "sh1", bufA, bufB), (h2t, "sh2", bufB, bufA)]):
            mg_base = p.val("mg")
            for m in range(32):
                wt = wta if m % 2 == 0 else wtb
                if m >= 2:
                    gp.wait("mg", mg_base + (m - 1))
                elif stage == 0:
                    gp.wait("mh", p.val("mh"))
                else:
                    gp.wait("mg", mg_base)
                lsem = "lwfa" if m % 2 == 0 else "lwfb"
                gp.dma_start(out=dap(wt, 0, [[32 * 128, 128], [128, 32], [1, 128]]),
                             in_=dap(wsrc, m * 128, [[4096, 128], [128 * 4096, 32],
                                                     [1, 128]])).inc(lsem, 16)
                te.wait(lsem, p.val(lsem))
                pA = pA1 if m % 2 == 0 else pA2
                if p.val("g1e") >= 2:
                    te.wait("g1e", p.val("g1e") - 1)
                for kc in range(32):
                    ins = te.matmul(pA[:], wt[:, kc * 128:(kc + 1) * 128],
                                    src_buf[:, kc * 512:(kc + 1) * 512],
                                    start=(kc == 0), stop=(kc == 31))
                ins.inc("mg")
                sc.wait("mg", p.val("mg"))
                sc.activation(dst_buf[:, m * 512:(m + 1) * 512], pA[:], AF.Gelu,
                              scale=scl(snm, m))
                sc.drain().inc("g1e")
            te.wait("g1e", p.val("g1e"))
        m3_base = p.val("m3")
        for m in range(FCH):
            wt = wta if m % 2 == 0 else wtb
            if m >= 2:
                gp.wait("m3", m3_base + (m - 1))
            else:
                gp.wait("mg", p.val("mg"))
            lsem = "lwfa" if m % 2 == 0 else "lwfb"
            gp.dma_start(out=dap(wt, 0, [[32 * 128, 128], [128, 32], [1, 128]]),
                         in_=dap(w3t, m * 128, [[1024, 128], [128 * 1024, 32],
                                                [1, 128]])).inc(lsem, 16)
            te.wait(lsem, p.val(lsem))
            pA = pA1 if m % 2 == 0 else pA2
            if p.val("g3e") >= 2:
                te.wait("g3e", p.val("g3e") - 1)
            for kc in range(32):
                ins = te.matmul(pA[:], wt[:, kc * 128:(kc + 1) * 128],
                                bufA[:, kc * 512:(kc + 1) * 512],
                                start=(kc == 0), stop=(kc == 31))
            ins.inc("m3")
            sc.wait("m3", p.val("m3"))
            buf = m % 2
            sc.wait("so", p.val("so"))
            sc.activation(stg[:, buf * 512:(buf + 1) * 512], pA[:], AF.Silu,
                          scale=scl("s3", m))
            sc.drain().inc("g3e")
            sy.wait("g3e", p.val("g3e"))
            sy.dma_start(out=dap(o_g0T, m * 128 * NTOK + tok0, [[NTOK, 128], [1, 512]]),
                         in_=stg[:, buf * 512:(buf + 1) * 512]).inc("so", 16)
        te.wait("g3e", p.val("g3e"))
        ve.wait("g3e", p.val("g3e"))

    sy.wait("so", p.val("so"))
    p.finish()
    return p


def l0ffn_inmaps(inputs, prep_outs):
    g = lambda n: _f32(inputs[n])
    tr = lambda a: np.ascontiguousarray(a.T)
    scpack = np.zeros((128, 4 * 32 + 8), np.float32)
    for i, nm in enumerate(["w1_s", "w2_s", "h1_s", "h2_s"]):
        scpack[:, i * 32:(i + 1) * 32] = g("L0_ffn_" + nm).reshape(32, 128).T
    scpack[:, 128:136] = g("L0_ffn_w3_s").reshape(8, 128).T
    shared = dict(
        w1t=tr(g("L0_ffn_w1_w")), w2t=tr(g("L0_ffn_w2_w")),
        h1t=tr(g("L0_ffn_h1_w")), h2t=tr(g("L0_ffn_h2_w")),
        w3t=tr(g("L0_ffn_w3_w")), sc_in=scpack,
    )
    maps = []
    for c in range(NCORES):
        m = dict(shared)
        m["xnT_own"] = prep_outs["xnT"][c]
        maps.append(m)
    return maps



def build_resid1():
    p = Prog()
    x_own = p.din("x_own", [NTOK, D], F32)
    s0T_own = p.din("s0T_own", [D, NTOK], F32)
    g0T_own = p.din("g0T_own", [D, NTOK], F32)
    wattn = p.din("wattn", [D, 3 * D], F32)
    battn = p.din("battn", [128, 24], F32)
    iden = p.din("iden", [128, 128], F32)
    o_x1 = p.dout("o_x1", [NTOK, D], F32)
    o_xn1T = p.dout("o_xn1T", [D, NTOK], BF16)
    o_qT = p.dout("o_qT", [D, NTOK], BF16)
    o_kT = p.dout("o_kT", [D, NTOK], BF16)
    o_vtm = p.dout("o_vtm", [16, NTOK, 64], BF16)

    sy, te, ve, sc, gp = p.sy, p.te, p.ve, p.sc, p.gp
    p.use_consts([1e-5])
    sc.wait("lc", 16)
    ve.wait("lc", 16)

    identf = p.sb("identf", [128, 128], F32)
    identb = p.sb("identb", [128, 128], BF16)
    stat = p.sb("stat", [128, 8 * NTT], F32)
    waste = p.sb("waste", [128, D], F32)
    xn1_tm = p.sb("xn1_tm", [128, NTT * D], BF16)
    battb = p.sb("battb", [128, 24], F32)

    pf1 = p.psum("pf1", [128, 512], F32)
    pf2 = p.psum("pf2", [128, 512], F32)
    pbb = p.psum("pbb", [128, 1024], BF16)
    pbb2 = p.psum("pbb2", [128, 1024], BF16)
    pA1 = p.psum("pA1", [128, 1024], F32)
    pA2 = p.psum("pA2", [128, 1024], F32)

    x_tm = p.sb_tmp("x_tm", [128, NTT * D], F32)
    prod_tm = p.sb_tmp("prod_tm", [128, NTT * D], F32)
    sgch = p.sb_tmp("sgch", [128, 4 * NTOK], F32)

    sy.dma_start(out=dap(x_tm, 0, [[NTT * D, 128], [D, NTT], [1, D]]),
                 in_=dap(x_own, 0, [[D, 128], [128 * D, NTT], [1, D]])).inc("ld", 16)
    sy.dma_start(out=identf[:], in_=iden.ap()).inc("ld", 16)
    sy.dma_start(out=battb[:], in_=battn.ap()).inc("ld", 16)
    gp.dma_start(out=identb[:], in_=iden.ap()).inc("lw", 16)

    for fc in range(FCH):
        buf = fc % 2
        if fc >= 2:
            sy.wait("ptv", p.val("ptv") - 4)
        lsem = "lsga" if buf == 0 else "lsgb"
        sy.dma_start(out=sgch[:, buf * 2 * NTOK:buf * 2 * NTOK + NTOK],
                     in_=dap(s0T_own, fc * 128 * NTOK, [[NTOK, 128], [1, NTOK]])
                     ).inc(lsem, 16)
        sy.dma_start(out=sgch[:, buf * 2 * NTOK + NTOK:buf * 2 * NTOK + 2 * NTOK],
                     in_=dap(g0T_own, fc * 128 * NTOK, [[NTOK, 128], [1, NTOK]])
                     ).inc(lsem, 16)
        ve.wait(lsem, p.val(lsem))
        ve.tensor_tensor(sgch[:, buf * 2 * NTOK:buf * 2 * NTOK + NTOK],
                         sgch[:, buf * 2 * NTOK:buf * 2 * NTOK + NTOK],
                         sgch[:, buf * 2 * NTOK + NTOK:buf * 2 * NTOK + 2 * NTOK],
                         ALU.mult)
        ve.drain().inc("pv")
        te.wait("pv", p.val("pv"))
        for g in range(4):
            pf = pf1 if g % 2 == 0 else pf2
            if fc > 0 or g >= 2:
                te.wait("ptv", p.val("ptv") - 1)
            lastt = None
            for i in range(4):
                tt = g * 4 + i
                lastt = te.transpose(pf[:, i * 128:(i + 1) * 128],
                                     sgch[:, buf * 2 * NTOK + tt * 128:
                                          buf * 2 * NTOK + (tt + 1) * 128],
                                     identf[:])
            lastt.inc("mmp")
            ve.wait("mmp", p.val("mmp"))
            for i in range(4):
                tt = g * 4 + i
                ve.tensor_copy(prod_tm[:, tt * D + fc * 128:tt * D + (fc + 1) * 128],
                               pf[:, i * 128:(i + 1) * 128])
            ve.drain().inc("ptv")

    ve.wait("ld", 48)
    for tt in range(NTT):
        ve.tensor_tensor(prod_tm[:, tt * D:(tt + 1) * D], prod_tm[:, tt * D:(tt + 1) * D],
                         x_tm[:, tt * D:(tt + 1) * D], ALU.add)
    ve.drain().inc("x1v")
    sy.wait("x1v", p.val("x1v"))
    for q4 in range(4):
        sy.dma_start(out=dap(o_x1, q4 * 4 * 128 * D, [[D, 128], [128 * D, 4], [1, D]]),
                     in_=dap(prod_tm, q4 * 4 * D, [[NTT * D, 128], [D, 4], [1, D]])
                     ).inc("sx1", 16)
    x1_so = p.val("sx1")

    gate_xn1 = emit_ln(p, prod_tm, xn1_tm, stat, waste, NTT, D, p.cv(1e-5),
                       gate=("x1v", p.val("x1v")))

    for nm in ["sgch", "prod_tm", "x_tm"]:
        p.sb_free(nm)
    xn1T = p.sb_tmp("xn1T", [128, FCH * NTOK], BF16)
    qkvd = p.sb_tmp("qkvd", [128, 16 * NTOK], BF16)
    v_tm = p.sb_tmp("v_tm", [128, NTT * D], BF16)
    wta = p.sb_tmp("wta", [128, FCH * 128], BF16)
    wtb = p.sb_tmp("wtb", [128, FCH * 128], BF16)
    vstg = p.sb_tmp("vstg", [128, NTOK], BF16)

    gp.wait("lnv", p.val("lnv"))
    gp.wait("sx1", x1_so)
    groups = []
    for fc in range(FCH):
        for g in range(2):
            srcs = [(xn1_tm[:, (g * 8 + i) * D + fc * 128:(g * 8 + i) * D + (fc + 1) * 128], 128)
                    for i in range(8)]
            dst = xn1T[:, fc * NTOK + g * 1024: fc * NTOK + (g + 1) * 1024]
            groups.append((srcs, dst, 1024))
    te.wait("lw", 16)
    gate_x1T = emit_transpose(p, groups, identb, pbb, pbb2, gate_te=gate_xn1)
    sy.wait(*gate_x1T)
    sy.dma_start(out=dap(o_xn1T, 0, [[NTOK, 128], [NTOK * 128, FCH], [1, NTOK]]),
                 in_=dap(xn1T, 0, [[FCH * NTOK, 128], [NTOK, FCH], [1, NTOK]])).inc("so", 16)

    mq_base = p.val("mq")
    for m in range(24):
        wt = wta if m % 2 == 0 else wtb
        if m >= 2:
            gp.wait("mq", mq_base + 2 * (m - 1))
        lsem = "lwfa" if m % 2 == 0 else "lwfb"
        gp.dma_start(out=dap(wt, 0, [[FCH * 128, 128], [128, FCH], [1, 128]]),
                     in_=dap(wattn, m * 128, [[3 * D, 128], [128 * 3 * D, FCH], [1, 128]])
                     ).inc(lsem, 16)
        te.wait(lsem, p.val(lsem))
        te.wait(*gate_x1T)
        pA = pA1 if m % 2 == 0 else pA2
        te.wait("qe", p.val("qe"))
        lastm = None
        for g in range(2):
            for kc in range(FCH):
                lastm = te.matmul(pA[:, g * 512:(g + 1) * 512],
                                  wt[:, kc * 128:(kc + 1) * 128],
                                  xn1T[:, kc * NTOK + g * 512: kc * NTOK + (g + 1) * 512],
                                  start=(kc == 0), stop=(kc == FCH - 1))
        lastm.inc("mq")
        pAo = pA2 if m % 2 == 0 else pA1
        for g in range(2):
            for kc in range(FCH):
                lastm = te.matmul(pAo[:, g * 512:(g + 1) * 512],
                                  wt[:, kc * 128:(kc + 1) * 128],
                                  xn1T[:, kc * NTOK + 1024 + g * 512:
                                       kc * NTOK + 1024 + (g + 1) * 512],
                                  start=(kc == 0), stop=(kc == FCH - 1))
        lastm.inc("mq")
        sc.wait("mq", p.val("mq"))
        if m < 16:
            dst = qkvd[:, m * NTOK:(m + 1) * NTOK]
            sc.activation(dst[:, 0:1024], pA[:], AF.Identity, bias=battb[:, m:m + 1])
            sc.activation(dst[:, 1024:2048], pAo[:], AF.Identity, bias=battb[:, m:m + 1])
            sc.drain().inc("qe")
        else:
            mv = m - 16
            sc.activation(waste[:, 0:1024].bitcast(BF16) if False else
                          qkvd[:, 0:1024], pA[:], AF.Identity, bias=battb[:, m:m + 1])                 if False else None
            sc.activation(vstg[:, 0:1024], pA[:], AF.Identity, bias=battb[:, m:m + 1])
            sc.activation(vstg[:, 1024:2048], (pA2 if m % 2 == 0 else pA1)[:],
                          AF.Identity, bias=battb[:, m:m + 1])
            sc.drain().inc("qe")
            te.wait("qe", p.val("qe"))
            for g in range(4):
                if p.val("vtv") >= 1 or g >= 1:
                    te.wait("vtv", p.val("vtv"))
                lastt = None
                for i in range(4):
                    tt = g * 4 + i
                    lastt = te.transpose(pbb[:, i * 128:(i + 1) * 128],
                                         vstg[:, tt * 128:(tt + 1) * 128], identb[:])
                lastt.inc("mmv")
                ve.wait("mmv", p.val("mmv"))
                for i in range(4):
                    tt = g * 4 + i
                    ve.tensor_copy(v_tm[:, tt * D + mv * 128:tt * D + (mv + 1) * 128],
                                   pbb[:, i * 128:(i + 1) * 128])
                ve.drain().inc("vtv")

    sy.wait("qe", p.val("qe"))
    sy.dma_start(out=dap(o_qT, 0, [[NTOK, 128], [NTOK * 128, FCH], [1, NTOK]]),
                 in_=dap(qkvd, 0, [[16 * NTOK, 128], [NTOK, FCH], [1, NTOK]])).inc("so", 16)
    sy.dma_start(out=dap(o_kT, 0, [[NTOK, 128], [NTOK * 128, FCH], [1, NTOK]]),
                 in_=dap(qkvd, FCH * NTOK, [[16 * NTOK, 128], [NTOK, FCH], [1, NTOK]])
                 ).inc("so", 16)
    sy.wait("vtv", p.val("vtv"))
    for h in range(16):
        sy.dma_start(out=dap(o_vtm, h * NTOK * 64, [[64, 128], [128 * 64, NTT], [1, 64]]),
                     in_=dap(v_tm, (h // 2) * 128 + (h % 2) * 64,
                             [[NTT * D, 128], [D, NTT], [1, 64]])).inc("so", 16)
    sy.wait("so", p.val("so"))
    sy.wait("sx1", p.val("sx1"))
    p.finish()
    return p


def resid1_inmaps(inputs, s0T_list, g0T_list):
    g = lambda n: _f32(inputs[n])
    x = _f32(inputs["L0_attn_x"]).reshape(GT, D)
    batt = np.ascontiguousarray(g("L1_attn_b_attn").reshape(24, 128).T)
    shared = dict(
        wattn=g("L1_attn_W_attn"), battn=batt,
        iden=np.eye(128, dtype=np.float32),
        cvec=np.full((128, 1), 1e-5, dtype=np.float32),
    )
    maps = []
    for c in range(NCORES):
        m = dict(shared)
        m["x_own"] = np.ascontiguousarray(x[c * NTOK:(c + 1) * NTOK])
        m["s0T_own"] = s0T_list[c]
        m["g0T_own"] = g0T_list[c]
        maps.append(m)
    return maps



def build_l1attn(hp0, hp1, do_proj):
    p = Prog()
    qTs = p.din("qTs", [D, NTOK], BF16)
    kTb = p.din("kTb", [D, T], BF16)
    vtmb = p.din("vtmb", [16, T, 64], BF16)
    mask_in = p.din("mask_in", [2, 128, 512], F32)
    gain_col = p.din("gain_col", [128, 1], F32)
    iden = p.din("iden", [128, 128], F32)
    o_yT = p.dout("o_yT", [D, NTOK], BF16)
    if do_proj:
        yT_other = p.din("yT_other", [D, NTOK], BF16)
        wproj = p.din("wproj", [D, D], F32)
        bproj = p.din("bproj", [128, FCH], F32)
        o_s1T = p.dout("o_s1T", [D, NTOK], F32)

    sy, te, ve, sc, gp = p.sy, p.te, p.ve, p.sc, p.gp
    p.use_consts([1e-6])
    sc.wait("lc", 16)
    ve.wait("lc", 16)

    NP = hp1 - hp0
    NIB = 32
    qsb = p.sb("qsb", [128, FCH * NTOK], BF16)
    identb = p.sb("identb", [128, 128], BF16)
    maskb = p.sb("maskb", [128, 2 * 512], BF16)
    gainb = p.sb("gainb", [128, 1], F32)
    kpA = p.sb("kpA", [128, T], BF16)
    kpB = p.sb("kpB", [128, T], BF16)
    vaug = p.sb("vaug", [128, 64 * 130], BF16)
    yA_tm = p.sb("yA_tm", [128, NTT * 64], BF16)
    yB_tm = p.sb("yB_tm", [128, NTT * 64], BF16)
    ysb = p.sb("ysb", [128, 8 * NTOK], BF16)
    st = p.sb("st", [128, 16], F32)
    s_sb = p.sb_tmp("s_sb", [128, T], BF16)
    wb1 = p.sb_tmp("wb1", [128, T], BF16)
    u_sb = p.sb_tmp("u_sb", [128, T], BF16)
    rT_sb = p.sb_tmp("rT_sb", [128, T], BF16)
    waste = p.sb_tmp("waste", [128, T], BF16)

    pS1 = p.psum("pS1", [128, 1024], F32)
    pS2 = p.psum("pS2", [128, 1024], F32)
    pb = p.psum("pb", [128, 1024], BF16)
    pav = p.psum("pav", [128, 136], F32)
    pyt = p.psum("pyt", [128, 256], BF16)

    sy.dma_start(out=dap(qsb, 0, [[FCH * NTOK, 128], [NTOK, FCH], [1, NTOK]]),
                 in_=dap(qTs, 0, [[NTOK, 128], [128 * NTOK, FCH], [1, NTOK]])
                 ).inc("ld", 16)
    gp.dma_start(out=identb[:], in_=iden.ap()).inc("lw", 16)
    gp.dma_start(out=dap(maskb, 0, [[2 * 512, 128], [512, 2], [1, 512]]),
                 in_=dap(mask_in, 0, [[512, 128], [128 * 512, 2], [1, 512]])
                 ).inc("lw", 16)
    sy.dma_start(out=gainb[:], in_=gain_col.ap()).inc("ld", 16)
    ve.memset(vaug[:], 1.0).inc("vo")

    NJT_ = T // 128

    for hpi in range(NP):
        hp = hp0 + hpi
        kp = kpA if hpi % 2 == 0 else kpB
        if hpi >= 2:
            gp.wait("ms1", 8 * NIB * (hpi - 1))
        lsem = "lka" if hpi % 2 == 0 else "lkb"
        gp.dma_start(out=kp[:], in_=dap(kTb, hp * 128 * T, [[T, 128], [1, T]])
                     ).inc(lsem, 16)
        if hpi >= 1:
            gp.wait("mav", p.val("mav"))
        gp.wait("vo", p.val("vo"))
        gp.dma_start(out=dap(vaug, 0, [[64 * 130, 128], [130, 64], [1, 64]]),
                     in_=dap(vtmb, (2 * hp) * T * 64, [[64, 128], [128 * 64, 64], [1, 64]])
                     ).inc("lv", 16)
        gp.dma_start(out=dap(vaug, 65, [[64 * 130, 128], [130, 64], [1, 64]]),
                     in_=dap(vtmb, (2 * hp + 1) * T * 64, [[64, 128], [128 * 64, 64], [1, 64]])
                     ).inc("lv", 16)
        te.wait(lsem, p.val(lsem))
        te.wait("lv", p.val("lv"))
        te.wait("lw", p.val("lw"))
        te.wait("ld", p.val("ld"))
        for ib in range(NIB):
            jd = ib // 2
            nch = jd + 1
            qA = qsb[0:64, hp * NTOK + ib * 64: hp * NTOK + ib * 64 + 64]
            qB = qsb[64:128, hp * NTOK + ib * 64: hp * NTOK + ib * 64 + 64]
            for cg in range(8):
                pS = pS1 if cg % 2 == 0 else pS2
                if p.val("sse") >= 2:
                    te.wait("sse", p.val("sse") - 1)
                lastm = None
                for half in range(2):
                    c0 = cg * 1024 + half * 512
                    te.matmul(pS[0:64, half * 512:(half + 1) * 512], qA,
                              kp[0:64, c0:c0 + 512], start=True, stop=True,
                              tile_position=(0, 0))
                    lastm = te.matmul(pS[64:128, half * 512:(half + 1) * 512], qB,
                                      kp[64:128, c0:c0 + 512], start=True, stop=True,
                                      tile_position=(64, 64))
                lastm.inc("ms1")
                sc.wait("ms1", p.val("ms1"))
                sc.activation(s_sb[:, cg * 1024:(cg + 1) * 1024], pS[:], AF.Identity,
                              accum_out=st[:, cg:cg + 1])
                sc.drain().inc("sse")
            ve.wait("sse", p.val("sse"))
            ve.tensor_reduce(st[:, 8:9], st[:, 0:8], AX.X, ALU.add)
            ve.drain()
            ve.tensor_scalar(st[:, 9:10], st[:, 8:9], 1.0 / T, None, ALU.mult)
            ve.drain()
            ve.tensor_scalar(st[:, 10:11], st[:, 9:10], -1.0, None, ALU.mult)
            ve.drain().inc("muv")
            sc.wait("muv", p.val("muv"))
            sc.activation(waste[:, 0:T], s_sb[:, 0:T], AF.Abs, bias=st[:, 10:11],
                          accum_out=st[:, 11:12])
            sc.drain().inc("mad")
            ve.wait("mad", p.val("mad"))
            ve.tensor_scalar(st[:, 12:13], st[:, 11:12], 1.0 / T, p.cv(1e-6),
                             ALU.mult, ALU.add)
            ve.drain()
            ve.reciprocal(st[:, 13:14], st[:, 12:13])
            ve.drain()
            ve.tensor_tensor(st[:, 14:15], st[:, 13:14], gainb[:], ALU.mult)
            ve.drain()
            ve.tensor_tensor(st[:, 15:16], st[:, 10:11], st[:, 14:15], ALU.mult)
            ve.drain().inc("cv")
            sc.wait("cv", p.val("cv"))
            sc.wait("mmt", p.val("mmt"))
            sc.activation(u_sb[:, 0:nch * 512], s_sb[:, 0:nch * 512], AF.Relu,
                          scale=st[:, 14:15], bias=st[:, 15:16])
            sc.drain().inc("uv")
            ve.wait("uv", p.val("uv"))
            ve.tensor_tensor(wb1[:, 0:nch * 512], u_sb[:, 0:nch * 512],
                             u_sb[:, 0:nch * 512], ALU.mult)
            ve.drain()
            ve.tensor_tensor(u_sb[:, 0:nch * 512], wb1[:, 0:nch * 512],
                             wb1[:, 0:nch * 512], ALU.mult)
            ve.drain()
            ve.tensor_tensor(u_sb[:, jd * 512:(jd + 1) * 512],
                             u_sb[:, jd * 512:(jd + 1) * 512],
                             maskb[:, (ib % 2) * 512:(ib % 2 + 1) * 512], ALU.mult)
            ve.drain().inc("rv")
            te.wait("rv", p.val("rv"))
            ntile = nch * 4
            for g in range((ntile + 7) // 8):
                te.wait("rtv", p.val("rtv"))
                lastt = None
                cnt = min(8, ntile - g * 8)
                for i in range(cnt):
                    jt = g * 8 + i
                    lastt = te.transpose(pb[:, i * 128:(i + 1) * 128],
                                         u_sb[:, jt * 128:(jt + 1) * 128], identb[:])
                lastt.inc("mmt")
                ve.wait("mmt", p.val("mmt"))
                ve.tensor_copy(rT_sb[:, g * 1024:g * 1024 + cnt * 128],
                               pb[:, 0:cnt * 128])
                ve.drain().inc("rtv")
            te.wait("rtv", p.val("rtv"))
            te.wait("yv", p.val("yv"))
            lastm = None
            for jt in range(ntile):
                lastm = te.matmul(pav[:, 0:130], rT_sb[:, jt * 128:(jt + 1) * 128],
                                  vaug[:, jt * 130:(jt + 1) * 130],
                                  start=(jt == 0), stop=(jt == ntile - 1))
            lastm.inc("mav")
            ve.wait("mav", p.val("mav"))
            ve.tensor_copy(st[:, 0:1][0:64, :], pav[0:64, 64:65])
            ve.tensor_copy(st[:, 0:1][64:128, :], pav[64:128, 129:130])
            ve.drain()
            ve.tensor_scalar(st[:, 1:2], st[:, 0:1], 1.0, p.cv(1e-6), ALU.mult, ALU.add)
            ve.drain()
            ve.reciprocal(st[:, 2:3], st[:, 1:2])
            ve.drain()
            row = (ib % 2) * 64
            col = (ib // 2) * 64
            ve.tensor_scalar(yA_tm[row:row + 64, col:col + 64], pav[0:64, 0:64],
                             st[0:64, 2:3], None, ALU.mult)
            ve.tensor_scalar(yB_tm[row:row + 64, col:col + 64], pav[64:128, 65:129],
                             st[64:128, 2:3], None, ALU.mult)
            ve.drain().inc("yv")
        te.wait("yv", p.val("yv"))
        for head in range(2):
            ysrc = yA_tm if head == 0 else yB_tm
            for g in range(4):
                te.wait("ytv", p.val("ytv"))
                lastt = None
                for i in range(4):
                    tt = g * 4 + i
                    lastt = te.transpose(pb[0:64, i * 128:(i + 1) * 128],
                                         ysrc[:, tt * 64:(tt + 1) * 64], identb[:])
                lastt.inc("mmy")
                ve.wait("mmy", p.val("mmy"))
                for i in range(4):
                    tt = g * 4 + i
                    base = 0 if head == 0 else 64
                    ve.tensor_copy(
                        ysb[base:base + 64,
                            hp * NTOK + tt * 128:hp * NTOK + (tt + 1) * 128],
                        pb[0:64, i * 128:(i + 1) * 128])
                ve.drain().inc("ytv")
        te.wait("ytv", p.val("ytv"))

    sy.wait("ytv", p.val("ytv"))
    sy.dma_start(out=dap(o_yT, hp0 * 128 * NTOK, [[NTOK, 128], [128 * NTOK, NP], [1, NTOK]]),
                 in_=dap(ysb, hp0 * NTOK, [[8 * NTOK, 128], [NTOK, NP], [1, NTOK]])
                 ).inc("so", 16)

    if do_proj:
        for nm in ["waste", "rT_sb", "u_sb", "wb1", "s_sb"]:
            p.sb_free(nm)
        gp.wait("ytv", p.val("ytv"))
        gp.wait("uv", p.val("uv"))
        gp.wait("mad", p.val("mad"))
        wpb = p.sb_tmp("wpb", [128, FCH * D], BF16)
        bpb = p.sb_tmp("bpb", [128, FCH], F32)
        s1st = p.sb_tmp("s1st", [128, 2 * NTOK], F32)
        gp.dma_start(out=dap(wpb, 0, [[FCH * D, 128], [D, FCH], [1, D]]),
                     in_=dap(wproj, 0, [[D, 128], [128 * D, FCH], [1, D]])).inc("lp", 16)
        gp.dma_start(out=bpb[:], in_=bproj.ap()).inc("lp", 16)
        oh0 = 0 if hp0 >= 4 else 4
        gp.dma_start(out=dap(ysb, oh0 * NTOK, [[8 * NTOK, 128], [NTOK, 4], [1, NTOK]]),
                     in_=dap(yT_other, oh0 * 128 * NTOK, [[NTOK, 128], [128 * NTOK, 4],
                                                          [1, NTOK]])).inc("lp", 16)
        te.wait("lp", p.val("lp"))
        for m in range(FCH):
            lastm = None
            for g in range(2):
                pS = pS1 if (m * 2 + g) % 2 == 0 else pS2
                if p.val("s1e") >= 2:
                    te.wait("s1e", p.val("s1e") - 1)
                for half in range(2):
                    n0 = g * 1024 + half * 512
                    for kc in range(FCH):
                        lastm = te.matmul(pS[:, half * 512:(half + 1) * 512],
                                          wpb[:, kc * D + m * 128:kc * D + (m + 1) * 128],
                                          ysb[:, kc * NTOK + n0:kc * NTOK + n0 + 512],
                                          start=(kc == 0), stop=(kc == FCH - 1))
                lastm.inc("ms1p")
                sc.wait("ms1p", p.val("ms1p"))
                buf = m % 2
                sc.wait("so", p.val("so"))
                sc.activation(s1st[:, buf * NTOK + g * 1024:buf * NTOK + (g + 1) * 1024],
                              pS[:], AF.Identity, bias=bpb[:, m:m + 1])
                sc.drain().inc("s1e")
            sy.wait("s1e", p.val("s1e"))
            sy.dma_start(out=dap(o_s1T, m * 128 * NTOK, [[NTOK, 128], [1, NTOK]]),
                         in_=s1st[:, (m % 2) * NTOK:(m % 2 + 1) * NTOK]).inc("so", 16)

    sy.wait("so", p.val("so"))
    p.finish()
    return p


def l1attn_inmaps(inputs, r1, half, yT_other_list=None):
    g = lambda n: _f32(inputs[n])
    gain = float(np.asarray(inputs["L1_attn_score_gain"]))
    kT_full = [np.concatenate([r1["kT"][b * 4 + i] for i in range(4)], axis=1)
               for b in range(2)]
    vtm_full = [np.concatenate([r1["vtm"][b * 4 + i] for i in range(4)], axis=1)
                for b in range(2)]
    qT_full = [np.concatenate([r1["qT"][b * 4 + i] for i in range(4)], axis=1)
               for b in range(2)]
    masks = np.zeros((8, 128, 512), np.float32)
    for v in range(8):
        qoff = v * 64 + (np.arange(128) % 64)
        masks[v] = (np.arange(512)[None, :] <= qoff[:, None]).astype(np.float32)
    maps = []
    for c in range(NCORES):
        b, s = c // 4, c % 4
        strip_cols = np.concatenate([np.arange(64) + (4 * ib + s) * 64
                                     for ib in range(32)])
        m = dict(
            qTs=np.ascontiguousarray(qT_full[b][:, strip_cols]),
            kTb=kT_full[b],
            vtmb=vtm_full[b],
            mask_in=np.ascontiguousarray(masks[[s, 4 + s]]),
            gain_col=np.full((128, 1), gain, np.float32),
            iden=np.eye(128, dtype=np.float32),
            cvec=np.full((128, 1), 1e-6, np.float32),
        )
        if half == 1:
            m["yT_other"] = yT_other_list[c]
            m["wproj"] = g("L1_attn_W_proj")
            m["bproj"] = np.ascontiguousarray(g("L1_attn_b_proj").reshape(FCH, 128).T)
        maps.append(m)
    return maps



TIN = 3712
NQK = TIN - 128
NQKT = NQK // 128
NXT = 17
XW = NXT * 128

def build_l1ffn(final, upto=None):
    p = Prog()
    xin = p.din("xin", [D, TIN], BF16)
    wqkv = p.din("wqkv", [D, 1536], F32)
    wo_in = p.din("wo_in", [512, D], F32)
    w1a_in = p.din("w1a_in", [D, 2730], F32)
    w1g_in = p.din("w1g_in", [D, 2730], F32)
    w2_in = p.din("w2_in", [2816, D], F32)
    b1a_in = p.din("b1a_in", [128, 22], F32)
    b1g_in = p.din("b1g_in", [128, 22], F32)
    b2_in = p.din("b2_in", [128, FCH], F32)
    lg_in = p.din("lg_in", [128, FCH], F32)
    lb_in = p.din("lb_in", [128, FCH], F32)
    qsc_in = p.din("qsc_in", [1, 512], F32)
    ksc_in = p.din("ksc_in", [1, 512], F32)
    csq_in = p.din("csq_in", [NQK, 64], F32)
    ssq_in = p.din("ssq_in", [NQK, 64], F32)
    csk_in = p.din("csk_in", [NQK, 64], F32)
    ssk_in = p.din("ssk_in", [NQK, 64], F32)
    masks_in = p.din("masks_in", [128, 1408], F32)
    ksel_in = p.din("ksel_in", [128, 8], F32)
    ones_in = p.din("ones_in", [1, 128], F32)
    iden = p.din("iden", [128, 128], F32)
    if final:
        x1_own = p.din("x1_own", [NTOK, D], F32)
        s1T_own = p.din("s1T_own", [D, NTOK], F32)
        o_out = p.dout("o_out", [NTOK, D], F32)
    else:
        o_xi = p.dout("o_xi", [D, NTOK], BF16)

    sy, te, ve, sc, gp = p.sy, p.te, p.ve, p.sc, p.gp
    p.use_consts([1e-12, 1e-5])
    sc.wait("lc", 16)
    ve.wait("lc", 16)

    identb = p.sb("identb", [128, 128], BF16)
    identf = p.sb("identf", [128, 128], F32)
    onesb = p.sb("onesb", [1, 128], BF16)
    x2T = p.sb("x2T", [128, FCH * XW], BF16)
    lgb = p.sb("lgb", [128, FCH], F32)
    lbb = p.sb("lbb", [128, FCH], F32)
    b2b = p.sb("b2b", [128, FCH], F32)
    kselb = p.sb("kselb", [128, 8], F32)
    st = p.sb("st", [128, 32], F32)

    xinb = p.sb_tmp("xinb", [128, FCH * TIN], BF16)
    kT = p.sb_tmp("kT", [128, 4 * NQK], BF16)
    qT = p.sb_tmp("qT", [128, 4 * XW], BF16)
    v_tm = p.sb_tmp("v_tm", [128, NQKT * 512], BF16)
    att_sb = p.sb_tmp("att_sb", [128, 1024], BF16)
    tabs = p.sb_tmp("tabs", [128, 4 * NQKT * 64], BF16)
    qscB = p.sb_tmp("qscB", [128, 2 * 512], BF16)
    wqb = p.sb_tmp("wqb", [128, FCH * 128], BF16)
    wqb2 = p.sb_tmp("wqb2", [128, FCH * 128], BF16)
    tmw = p.sb_tmp("tmw", [128, 2 * 512], BF16)
    tmw2 = p.sb_tmp("tmw2", [128, 2 * 512], BF16)

    pA1 = p.psum("pA1", [128, 1024], F32)
    pA2 = p.psum("pA2", [128, 1024], F32)
    pb = p.psum("pb", [128, 1024], BF16)
    pf32 = p.psum("pf32", [128, 512], F32)
    pb2 = p.psum("pb2", [128, 1024], BF16)
    pav = p.psum("pav", [128, 128], F32)

    for q4 in range(4):
        sy.dma_start(out=dap(xinb, q4 * 2 * TIN, [[FCH * TIN, 128], [TIN, 2], [1, TIN]]),
                     in_=dap(xin, q4 * 2 * 128 * TIN, [[TIN, 128], [128 * TIN, 2], [1, TIN]])
                     ).inc("ld", 16)
    sy.dma_start(out=identf[:], in_=iden.ap()).inc("ld", 16)
    sy.dma_start(out=lgb[:], in_=lg_in.ap()).inc("ld", 16)
    sy.dma_start(out=lbb[:], in_=lb_in.ap()).inc("ld", 16)
    sy.dma_start(out=b2b[:], in_=b2_in.ap()).inc("ld", 16)
    sy.dma_start(out=kselb[:], in_=ksel_in.ap()).inc("ld", 16)
    gp.dma_start(out=identb[:], in_=iden.ap()).inc("lw", 16)
    gp.dma_start(out=onesb[:], in_=ones_in.ap()).inc("lw", 16)
    for i, tab in enumerate([csq_in, ssq_in, csk_in, ssk_in]):
        gp.dma_start(out=dap(tabs, i * NQKT * 64, [[4 * NQKT * 64, 128], [64, NQKT], [1, 64]]),
                     in_=dap(tab, 0, [[64, 128], [128 * 64, NQKT], [1, 64]])).inc("lw", 16)
    gp.dma_start(out=qscB[0:1, 0:512], in_=qsc_in.ap()).inc("lw", 16)
    gp.dma_start(out=qscB[0:1, 512:1024], in_=ksc_in.ap()).inc("lw", 16)

    def tab_rep(i, tt):
        return dap(tabs, i * NQKT * 64 + tt * 64,
                   [[4 * NQKT * 64, 128], [0, 8], [1, 64]])

    te.wait("lw", p.val("lw"))
    te.matmul(pf32[:, 0:512], onesb[0:1, :], qscB[0:1, 0:512],
              start=True, stop=True).inc("mbc")
    ve.wait("mbc", p.val("mbc"))
    ve.tensor_copy(qscB[:, 0:512], pf32[:, 0:512])
    ve.drain().inc("bcv")
    te.wait("bcv", p.val("bcv"))
    te.matmul(pf32[:, 0:512], onesb[0:1, :], qscB[0:1, 512:1024],
              start=True, stop=True).inc("mbc")
    ve.wait("mbc", p.val("mbc"))
    ve.tensor_copy(qscB[:, 512:1024], pf32[:, 0:512])
    ve.drain().inc("bcv")

    te.wait("ld", p.val("ld"))
    te.wait("bcv", p.val("bcv"))
    NCH7 = NQK // 512
    mq_base = p.val("mqk")
    for m in range(12):
        wt = wqb if m % 2 == 0 else wqb2
        if m >= 2:
            gp.wait("mqk", mq_base + (m - 1) * NCH7)
        lsem = "lwfa" if m % 2 == 0 else "lwfb"
        gp.dma_start(out=dap(wt, 0, [[FCH * 128, 128], [128, FCH], [1, 128]]),
                     in_=dap(wqkv, m * 128, [[1536, 128], [128 * 1536, FCH], [1, 128]])
                     ).inc(lsem, 16)
        te.wait(lsem, p.val(lsem))
        for n in range(NCH7):
            pA = pA1 if (m * NCH7 + n) % 2 == 0 else pA2
            if p.val("qke") >= 2:
                te.wait("qke", p.val("qke") - 1)
            lastm = None
            for kc in range(FCH):
                shiftcol = 0 if kc < 4 else -1
                lastm = te.matmul(pA[:, 0:512], wt[:, kc * 128:(kc + 1) * 128],
                                  xinb[:, kc * TIN + 128 + n * 512 + shiftcol:
                                       kc * TIN + 128 + n * 512 + shiftcol + 512],
                                  start=(kc == 0), stop=(kc == FCH - 1))
            lastm.inc("mqk")
            sc.wait("mqk", p.val("mqk"))
            c0 = 128 + n * 512
            if m < 4:
                lo = max(c0, 1536)
                if lo < c0 + 512:
                    off = lo - c0
                    sc.activation(qT[:, m * XW + (lo - 1536):m * XW + (lo - 1536) + 512 - off],
                                  pA[:, off:512], AF.Copy)
                sc.drain().inc("qke")
            elif m < 8:
                sc.activation(kT[:, (m - 4) * NQK + n * 512:(m - 4) * NQK + (n + 1) * 512],
                              pA[:, 0:512], AF.Copy)
                sc.drain().inc("qke")
            else:
                sc.activation(att_sb[:, 0:512], pA[:, 0:512], AF.Copy)
                sc.drain().inc("qke")
                te.wait("qke", p.val("qke"))
                te.wait("vtv", p.val("vtv"))
                lastt = None
                for i in range(4):
                    lastt = te.transpose(pb[:, i * 128:(i + 1) * 128],
                                         att_sb[:, i * 128:(i + 1) * 128], identb[:])
                lastt.inc("mmv")
                ve.wait("mmv", p.val("mmv"))
                for i in range(4):
                    tt = n * 4 + i
                    ve.tensor_copy(v_tm[:, tt * 512 + (m - 8) * 128:
                                        tt * 512 + (m - 7) * 128],
                                   pb[:, i * 128:(i + 1) * 128])
                ve.drain().inc("vtv")

    if upto == "qkv":
        ve.drain().inc("fin")
        sy.wait("fin", p.val("fin"))
        p.finish()
        return p

    def rot_block(kind):
        ntile = NXT if kind == 0 else NQKT
        srcbuf = qT if kind == 0 else kT
        width = XW if kind == 0 else NQK
        tab_cs = 0 if kind == 0 else 2
        tab_off = 11 if kind == 0 else 0
        scB = qscB[:, 0:512] if kind == 0 else qscB[:, 512:1024]
        for tt in range(ntile):
            te.wait("rtw", p.val("rtw"))
            lastt = None
            for fc in range(4):
                lastt = te.transpose(pb[:, fc * 128:(fc + 1) * 128],
                                     srcbuf[:, fc * width + tt * 128:
                                            fc * width + (tt + 1) * 128], identb[:])
            lastt.inc("mmr")
            ve.wait("mmr", p.val("mmr"))
            ve.tensor_copy(tmw[:, 0:512], pb[:, 0:512])
            ve.drain()
            ve.tensor_tensor(tmw2[:, 0:512], tmw[:, 0:512], tmw[:, 0:512], ALU.mult)
            ve.drain()
            ve.tensor_reduce(st[:, 0:8],
                             dap(tmw2, 0, [[2 * 512, 128], [64, 8], [1, 64]]),
                             AX.X, ALU.add)
            ve.drain().inc("nsv")
            sc.wait("nsv", p.val("nsv"))
            sc.activation(st[:, 8:16], st[:, 0:8], AF.Sqrt)
            sc.drain().inc("nss")
            ve.wait("nss", p.val("nss"))
            ve.tensor_scalar(st[:, 8:16], st[:, 8:16], p.cv(1e-12), None, ALU.max)
            ve.drain()
            ve.reciprocal(st[:, 16:24], st[:, 8:16])
            ve.drain()
            for h in range(8):
                ve.tensor_scalar(tmw[:, h * 64:(h + 1) * 64], tmw[:, h * 64:(h + 1) * 64],
                                 st[:, 16 + h:17 + h], None, ALU.mult)
            ve.drain()
            ve.tensor_tensor(tmw[:, 0:512], tmw[:, 0:512], scB, ALU.mult)
            ve.drain()
            ve.tensor_tensor(tmw2[:, 0:512],
                             dap(tmw, 0, [[2 * 512, 128], [64, 8], [1, 64]]),
                             tab_rep(tab_cs, tab_off + tt), ALU.mult)
            swap = dap(tmw, 32, [[2 * 512, 128], [64, 8], [-32, 2], [1, 32]])
            ve.tensor_tensor(dap(tmw2, 512, [[2 * 512, 128], [64, 8], [1, 64]]),
                             swap, tab_rep(tab_cs + 1, tab_off + tt), ALU.mult)
            ve.drain()
            ve.tensor_tensor(tmw2[:, 0:512], tmw2[:, 0:512], tmw2[:, 512:1024], ALU.add)
            ve.drain().inc("rov")
            te.wait("rov", p.val("rov"))
            lastt = None
            for fc in range(4):
                lastt = te.transpose(pb2[:, fc * 128:(fc + 1) * 128],
                                     tmw2[:, fc * 128:(fc + 1) * 128], identb[:])
            lastt.inc("mmr2")
            ve.wait("mmr2", p.val("mmr2"))
            for fc in range(4):
                ve.tensor_copy(srcbuf[:, fc * width + tt * 128:fc * width + (tt + 1) * 128],
                               pb2[:, fc * 128:(fc + 1) * 128])
            ve.drain().inc("rtw")

    te.wait("qke", p.val("qke"))
    te.wait("vtv", p.val("vtv"))
    rot_block(0)
    rot_block(1)

    if upto == "rot":
        ve.wait("rtw", p.val("rtw"))
        ve.drain().inc("fin")
        sy.wait("fin", p.val("fin"))
        p.finish()
        return p

    te.wait("rtw", p.val("rtw"))
    for nm in ["tmw2", "tmw", "wqb2", "wqb", "qscB", "tabs"]:
        p.sb_free(nm)
    maskb = p.sb_tmp("maskb", [128, 1408], BF16)
    attT_sb = p.sb_tmp("attT_sb", [128, 1024], BF16)
    sm_sb = p.sb_tmp("sm_sb", [128, 1024], F32)
    oT = p.sb_tmp("oT", [128, 4 * XW], BF16)
    wob = p.sb_tmp("wob", [128, 4 * 1024], BF16)
    gp.wait("rtw", p.val("rtw"))
    gp.dma_start(out=maskb[:], in_=masks_in.ap()).inc("lmk", 16)
    for h in range(8):
        hrow = (h % 2) * 64
        hch = h // 2
        for qt in range(NXT):
            qcol = 1536 + qt * 128
            wstart = ((qcol - 128) // 512) * 512 + 128
            it = (qcol - wstart) // 128
            kv0 = wstart - 512
            D_idx = (wstart - 1664) // 512 + 1
            te.wait("ate", p.val("ate"))
            lastm = None
            for half in range(2):
                lastm = te.matmul(
                    pA1[:, half * 512:(half + 1) * 512],
                    qT[hrow:hrow + 64, hch * XW + qt * 128:hch * XW + (qt + 1) * 128],
                    kT[hrow:hrow + 64, hch * NQK + (kv0 - 128) + half * 512:
                       hch * NQK + (kv0 - 128) + (half + 1) * 512],
                    start=True, stop=True, tile_position=(hrow, 0))
            lastm.inc("msim")
            ve.wait("msim", p.val("msim"))
            ve.wait("lmk", p.val("lmk"))
            ve.tensor_tensor(sm_sb[:], pA1[:],
                             maskb[:, (3 - it) * 128:(3 - it) * 128 + 1024], ALU.add)
            ve.drain().inc("ate")
            ve.tensor_scalar(sm_sb[:, 0:512], sm_sb[:, 0:512],
                             kselb[:, D_idx:D_idx + 1], None, ALU.add)
            ve.drain()
            ve.tensor_reduce(st[:, 24:25], sm_sb[:], AX.X, ALU.max)
            ve.drain()
            ve.tensor_scalar(st[:, 25:26], st[:, 24:25], -1.0, None, ALU.mult)
            ve.drain().inc("smx")
            sc.wait("smx", p.val("smx"))
            sc.activation(att_sb[:], sm_sb[:], AF.Exp, bias=st[:, 25:26],
                          accum_out=st[:, 26:27])
            sc.drain().inc("sme")
            ve.wait("sme", p.val("sme"))
            ve.reciprocal(st[:, 27:28], st[:, 26:27])
            ve.drain()
            ve.tensor_scalar(att_sb[:], att_sb[:], st[:, 27:28], None, ALU.mult)
            ve.drain().inc("atv")
            te.wait("atv", p.val("atv"))
            lastt = None
            for i in range(8):
                lastt = te.transpose(pb[:, i * 128:(i + 1) * 128],
                                     att_sb[:, i * 128:(i + 1) * 128], identb[:])
            lastt.inc("mma")
            ve.wait("mma", p.val("mma"))
            ve.tensor_copy(attT_sb[:], pb[:, 0:1024])
            ve.drain().inc("atc")
            te.wait("atc", p.val("atc"))
            te.wait("ocv", p.val("ocv"))
            lastm = None
            for jt in range(8):
                kvt = (kv0 - 128) // 128 + jt
                lastm = te.matmul(pav[:, 0:64], attT_sb[:, jt * 128:(jt + 1) * 128],
                                  v_tm[:, kvt * 512 + h * 64:kvt * 512 + (h + 1) * 64],
                                  start=(jt == 0), stop=(jt == 7))
            lastm.inc("mav")
            ve.wait("mav", p.val("mav"))
            ve.tensor_copy(att_sb[:, 0:64], pav[:, 0:64])
            ve.drain().inc("ocv")
            te.wait("ocv", p.val("ocv"))
            te.wait("otv", p.val("otv"))
            te.transpose(pb2[0:64, 0:128], att_sb[:, 0:64], identb[:]).inc("mmo")
            ve.wait("mmo", p.val("mmo"))
            ve.tensor_copy(oT[hrow:hrow + 64, hch * XW + qt * 128:
                              hch * XW + (qt + 1) * 128],
                           pb2[0:64, 0:128])
            ve.drain().inc("otv")

    if upto == "attn":
        ve.wait("otv", p.val("otv"))
        ve.drain().inc("fin")
        sy.wait("fin", p.val("fin"))
        p.finish()
        return p

    gp.wait("otv", p.val("otv"))
    gp.dma_start(out=dap(wob, 0, [[4 * 1024, 128], [1024, 4], [1, 1024]]),
                 in_=dap(wo_in, 0, [[1024, 128], [128 * 1024, 4], [1, 1024]])
                 ).inc("lwo", 16)
    te.wait("lwo", p.val("lwo"))
    te.wait("otv", p.val("otv"))
    for m in range(FCH):
        for g in range(3):
            pass
        lastm = None
        for n in range((XW + 511) // 512):
            nsz = min(512, XW - n * 512)
            pA = pA1 if (m * 5 + n) % 2 == 0 else pA2
            if p.val("x2e") >= 2:
                te.wait("x2e", p.val("x2e") - 1)
            for kc in range(4):
                lastm = te.matmul(pA[:, 0:nsz], wob[:, kc * 1024 + m * 128:
                                                    kc * 1024 + (m + 1) * 128],
                                  oT[:, kc * XW + n * 512:kc * XW + n * 512 + nsz],
                                  start=(kc == 0), stop=(kc == 3))
            lastm.inc("mx2")
            ve.wait("mx2", p.val("mx2"))
            ve.scalar_tensor_tensor(x2T[:, m * XW + n * 512:m * XW + n * 512 + nsz],
                                    pA[:, 0:nsz], 1.0,
                                    xinb[:, m * TIN + 1536 + n * 512:
                                         m * TIN + 1536 + n * 512 + nsz],
                                    ALU.mult, ALU.add)
            ve.drain().inc("x2e")

    if upto == "x2":
        ve.wait("x2e", p.val("x2e"))
        ve.drain().inc("fin")
        sy.wait("fin", p.val("fin"))
        p.finish()
        return p

    for nm in ["wob", "oT", "sm_sb", "attT_sb", "maskb", "att_sb",
               "v_tm", "qT", "kT", "xinb"]:
        p.sb_free(nm)
    hnT = p.sb_tmp("hnT", [128, FCH * NTOK], BF16)
    ffh = p.sb_tmp("ffh", [128, 22 * 512], BF16)
    w1t = p.sb_tmp("w1t", [128, FCH * 128 * 2], BF16)
    w1t2 = p.sb_tmp("w1t2", [128, FCH * 128 * 2], BF16)
    w2t = p.sb_tmp("w2t", [128, 22 * 128], BF16)
    w2t2 = p.sb_tmp("w2t2", [128, 22 * 128], BF16)
    b1ab = p.sb_tmp("b1ab", [128, 2 * 22], F32)
    stat = p.sb_tmp("stat", [128, 8 * NTT], F32)
    waste2 = p.sb_tmp("waste2", [128, D], F32)
    h_tm = p.sb_tmp("h_tm", [128, NTT * D], BF16)
    hn_tm = p.sb_tmp("hn_tm", [128, NTT * D], BF16)
    hsh = p.sb_tmp("hsh", [128, 4 * NTOK], BF16)

    gp.wait("x2e", p.val("x2e"))
    gp.wait("rtw", p.val("rtw"))
    gp.dma_start(out=b1ab[:, 0:22], in_=b1a_in.ap()).inc("lw2", 16)
    gp.dma_start(out=b1ab[:, 22:44], in_=b1g_in.ap()).inc("lw2", 16)

    ve.wait("x2e", p.val("x2e"))
    for fc in range(4, FCH):
        ve.tensor_copy(hsh[:, (fc - 4) * NTOK:(fc - 3) * NTOK],
                       x2T[:, fc * XW + 127:fc * XW + 127 + NTOK])
    ve.drain().inc("shc")
    te.wait("shc", p.val("shc"))
    for tt in range(NTT):
        te.wait("shv", p.val("shv"))
        lastt = None
        for fc in range(FCH):
            if fc < 4:
                srcap = x2T[:, fc * XW + 128 + tt * 128:fc * XW + 128 + (tt + 1) * 128]
            else:
                srcap = hsh[:, (fc - 4) * NTOK + tt * 128:
                            (fc - 4) * NTOK + (tt + 1) * 128]
            lastt = te.transpose(pb[:, fc * 128:(fc + 1) * 128], srcap, identb[:])
        lastt.inc("mmh")
        ve.wait("mmh", p.val("mmh"))
        ve.tensor_copy(h_tm[:, tt * D:(tt + 1) * D], pb[:])
        ve.drain().inc("shv")
    p.sb_free("hsh")
    gate_ln = emit_ln(p, h_tm, hn_tm, stat, waste2, NTT, D, p.cv(1e-5),
                      gate=("shv", p.val("shv")))
    groups = []
    for fc in range(FCH):
        for g in range(2):
            srcs = [(hn_tm[:, (g * 8 + i) * D + fc * 128:(g * 8 + i) * D + (fc + 1) * 128], 128)
                    for i in range(8)]
            dst = hnT[:, fc * NTOK + g * 1024: fc * NTOK + (g + 1) * 1024]
            groups.append((srcs, dst, 1024))
    gate_hnT = emit_transpose(p, groups, identb, pb, pb2, gate_te=gate_ln)
    ve.wait(*gate_hnT)
    for fc in range(FCH):
        ve.tensor_scalar(hnT[:, fc * NTOK:(fc + 1) * NTOK],
                         hnT[:, fc * NTOK:(fc + 1) * NTOK],
                         lgb[:, fc:fc + 1], lbb[:, fc:fc + 1], ALU.mult, ALU.add)
    ve.drain().inc("hnv")

    if upto == "ln":
        ve.wait("hnv", p.val("hnv"))
        ve.drain().inc("fin")
        sy.wait("fin", p.val("fin"))
        p.finish()
        return p

    p.sb_free("hn_tm")
    p.sb_free("h_tm")
    if final:
        out_tm = p.sb_tmp("out_tm", [128, NTT * D], F32)
        s1buf = p.sb_tmp("s1buf", [128, 2 * 1024], F32)
        gp.wait("hnv", p.val("hnv"))
        sy.wait("hnv", p.val("hnv"))
        for c in range(4):
            sy.dma_start(out=dap(out_tm, c * 4 * D, [[NTT * D, 128], [D, 4], [1, D]]),
                         in_=dap(x1_own, c * 4 * 128 * D, [[D, 128], [128 * D, 4], [1, D]])
                         ).inc("lx1", 16)
    else:
        xi_st = p.sb_tmp("xi_st", [128, 2 * 1024], BF16)

    M1 = 22
    for tp in range(4):
        tok0 = tp * 512
        m1_base = p.val("mf1")
        for m in range(M1):
            wt = w1t if m % 2 == 0 else w1t2
            msz = 128 if m < 21 else 42
            if m >= 2:
                gp.wait("mf1", m1_base + (m - 1))
            elif tp >= 1:
                gp.wait("mxi", p.val("mxi"))
            lsem = "lwf2a" if m % 2 == 0 else "lwf2b"
            gp.dma_start(out=dap(wt, 0, [[2 * FCH * 128, 128], [128, FCH], [1, msz]]),
                         in_=dap(w1a_in, m * 128, [[2730, 128], [128 * 2730, FCH], [1, msz]])
                         ).inc(lsem, 16)
            gp.dma_start(out=dap(wt, FCH * 128, [[2 * FCH * 128, 128], [128, FCH], [1, msz]]),
                         in_=dap(w1g_in, m * 128, [[2730, 128], [128 * 2730, FCH], [1, msz]])
                         ).inc(lsem, 16)
            te.wait(lsem, p.val(lsem))
            te.wait("hnv", p.val("hnv"))
            te.wait("lw2", p.val("lw2"))
            if tp >= 1 and m < 2:
                te.wait("xie", p.val("xie"))
            pA = pA1 if (tp * M1 + m) % 2 == 0 else pA2
            if p.val("f1e") >= 2:
                te.wait("f1e", p.val("f1e") - 1)
            lastm = None
            for kc in range(FCH):
                te.matmul(pA[0:msz, 0:512], wt[:, kc * 128:kc * 128 + msz],
                          hnT[:, kc * NTOK + tok0:kc * NTOK + tok0 + 512],
                          start=(kc == 0), stop=(kc == FCH - 1))
            for kc in range(FCH):
                lastm = te.matmul(pA[0:msz, 512:1024],
                                  wt[:, FCH * 128 + kc * 128:FCH * 128 + kc * 128 + msz],
                                  hnT[:, kc * NTOK + tok0:kc * NTOK + tok0 + 512],
                                  start=(kc == 0), stop=(kc == FCH - 1))
            lastm.inc("mf1")
            sc.wait("mf1", p.val("mf1"))
            sc.activation(waste2[0:msz, 0:512], pA[0:msz, 0:512], AF.Identity,
                          bias=b1ab[0:msz, m:m + 1])
            sc.activation(waste2[0:msz, 512:1024], pA[0:msz, 512:1024], AF.Gelu,
                          bias=b1ab[0:msz, 22 + m:22 + m + 1])
            sc.drain().inc("f1s")
            ve.wait("f1s", p.val("f1s"))
            ve.tensor_tensor(ffh[0:msz, m * 512:(m + 1) * 512],
                             waste2[0:msz, 0:512], waste2[0:msz, 512:1024], ALU.mult)
            ve.drain().inc("f1e")
        m2_base = p.val("mxi")
        for m in range(FCH):
            wt = w2t if m % 2 == 0 else w2t2
            if m >= 2:
                gp.wait("mxi", m2_base + (m - 1))
            else:
                gp.wait("mf1", p.val("mf1"))
            lsem = "lwf2a" if m % 2 == 0 else "lwf2b"
            gp.dma_start(out=dap(wt, 0, [[22 * 128, 128], [128, 22], [1, 128]]),
                         in_=dap(w2_in, m * 128, [[D, 128], [128 * D, 22], [1, 128]])
                         ).inc(lsem, 16)
            te.wait(lsem, p.val(lsem))
            if m < 2:
                te.wait("f1e", p.val("f1e"))
            pA = pA1 if (tp * FCH + m) % 2 == 0 else pA2
            if p.val("xie") >= 2:
                te.wait("xie", p.val("xie") - 1)
            lastm = None
            for kc in range(M1):
                ksz = 128 if kc < 21 else 42
                lastm = te.matmul(pA[:, 0:512],
                                  wt[0:ksz, kc * 128:(kc + 1) * 128],
                                  ffh[0:ksz, kc * 512:(kc + 1) * 512],
                                  start=(kc == 0), stop=(kc == M1 - 1))
            lastm.inc("mxi")
            sc.wait("mxi", p.val("mxi"))
            if final:
                sc.wait("prv0", p.val("prv0"))
                sc.activation(waste2[:, 0:512], pA[:, 0:512], AF.Identity,
                              bias=b2b[:, m:m + 1])
                sc.drain().inc("xis")
                ve.wait("xis", p.val("xis"))
                ve.tensor_tensor(waste2[:, 512:1024], waste2[:, 0:512],
                                 x2T[:, m * XW + 128 + tok0:
                                      m * XW + 128 + tok0 + 512], ALU.add)
                ve.drain().inc("xiv")
                sc.wait("xiv", p.val("xiv"))
                sc.activation(waste2[:, 0:512], waste2[:, 512:1024], AF.Silu)
                sc.drain().inc("g1s")
                buf = (tp * FCH + m) % 2
                sy.wait("prv", max(0, p.val("prv") - 1))
                lsem = "ls1a" if buf == 0 else "ls1b"
                sy.dma_start(out=s1buf[:, buf * 1024:buf * 1024 + 512],
                             in_=dap(s1T_own, m * 128 * NTOK + tok0,
                                     [[NTOK, 128], [1, 512]])).inc(lsem, 16)
                ve.wait(lsem, p.val(lsem))
                ve.wait("g1s", p.val("g1s"))
                ve.tensor_tensor(s1buf[:, buf * 1024:buf * 1024 + 512],
                                 s1buf[:, buf * 1024:buf * 1024 + 512],
                                 waste2[:, 0:512], ALU.mult)
                ve.drain().inc("prv0")
                te.wait("prv0", p.val("prv0"))
                te.wait("prc", p.val("prc"))
                lastt = None
                for i in range(4):
                    lastt = te.transpose(pf32[:, i * 128:(i + 1) * 128],
                                         s1buf[:, buf * 1024 + i * 128:
                                               buf * 1024 + (i + 1) * 128],
                                         identf[:])
                lastt.inc("mpr")
                ve.wait("mpr", p.val("mpr"))
                ve.wait("lx1", 64)
                tt0 = tok0 // 128
                for i in range(4):
                    ve.tensor_tensor(out_tm[:, (tt0 + i) * D + m * 128:
                                            (tt0 + i) * D + (m + 1) * 128],
                                     out_tm[:, (tt0 + i) * D + m * 128:
                                            (tt0 + i) * D + (m + 1) * 128],
                                     pf32[:, i * 128:(i + 1) * 128], ALU.add)
                ve.drain().inc("prc")
                ve.engine_nop().inc("prv")
                ve.engine_nop().inc("xie")
            else:
                buf = (tp * FCH + m) % 2
                sc.wait("so", p.val("so"))
                sc.activation(xi_st[:, buf * 1024:buf * 1024 + 512], pA[:, 0:512],
                              AF.Identity, bias=b2b[:, m:m + 1])
                sc.drain().inc("xis")
                ve.wait("xis", p.val("xis"))
                ve.tensor_tensor(xi_st[:, buf * 1024:buf * 1024 + 512],
                                 xi_st[:, buf * 1024:buf * 1024 + 512],
                                 x2T[:, m * XW + 128 + tok0:
                                      m * XW + 128 + tok0 + 512], ALU.add)
                ve.drain().inc("xiv")
                sy.wait("xiv", p.val("xiv"))
                sy.dma_start(out=dap(o_xi, m * 128 * NTOK + tok0,
                                     [[NTOK, 128], [1, 512]]),
                             in_=xi_st[:, buf * 1024:buf * 1024 + 512]).inc("so", 16)
                ve.engine_nop().inc("xie")
    if final:
        ve.wait("prc", p.val("prc"))
        sy.wait("prc", p.val("prc"))
        for q4 in range(4):
            sy.dma_start(out=dap(o_out, q4 * 4 * 128 * D, [[D, 128], [128 * D, 4], [1, D]]),
                         in_=dap(out_tm, q4 * 4 * D, [[NTT * D, 128], [D, 4], [1, D]])
                         ).inc("so", 16)
    sy.wait("so", p.val("so"))
    p.finish()
    return p


def _rotary_tables(positions):
    half_idx = np.arange(0, 64, 2, dtype=np.float64)
    inv_freq = 1.0 / 10000.0 ** (half_idx / 64.0)
    t = positions.astype(np.float64)
    fr = t[:, None] * inv_freq[None, :]
    freqs = np.concatenate([fr, fr], axis=1)
    base = (half_idx + 0.4 * 64) / (1.4 * 64)
    power = (t - 4096.0) / 256.0
    scale = np.concatenate([base[None, :] ** power[:, None]] * 2, axis=1)
    cos, sin = np.cos(freqs), np.sin(freqs)
    sgn = np.concatenate([-np.ones(32), np.ones(32)])[None, :]
    CSq = (cos * scale).astype(np.float32)
    SSq = (sgn * sin * scale).astype(np.float32)
    CSk = (cos / scale).astype(np.float32)
    SSk = (sgn * sin / scale).astype(np.float32)
    return CSq, SSq, CSk, SSk


def l1ffn_inmaps(inputs, xiT_full, layer, final, x1_list=None, s1T_full=None):
    g = lambda n: _f32(inputs[n])
    wqkv = np.ascontiguousarray(g("L1_ffn_wqkv")[layer])
    wo = np.ascontiguousarray(g("L1_ffn_wo")[layer])
    w1 = g("L1_ffn_w1")[layer]
    w1a = np.ascontiguousarray(w1[:, :2730])
    w1g = np.ascontiguousarray(w1[:, 2730:])
    b1 = g("L1_ffn_b1")[layer]
    b1a = np.zeros((128, 22), np.float32)
    b1g = np.zeros((128, 22), np.float32)
    b1a.T.flat[:2730] = b1[:2730]
    b1g.T.flat[:2730] = b1[2730:]
    b1a = np.zeros((128, 22), np.float32); b1g = np.zeros((128, 22), np.float32)
    tmp = np.zeros(2816, np.float32); tmp[:2730] = b1[:2730]
    b1a[:, :] = tmp.reshape(22, 128).T
    tmp = np.zeros(2816, np.float32); tmp[:2730] = b1[2730:]
    b1g[:, :] = tmp.reshape(22, 128).T
    w2 = np.zeros((2816, D), np.float32)
    w2[:2730] = g("L1_ffn_w2")[layer]
    b2 = np.ascontiguousarray(g("L1_ffn_b2")[layer].reshape(FCH, 128).T)
    lg = np.ascontiguousarray(g("L1_ffn_ln_g")[layer].reshape(FCH, 128).T)
    lb = np.ascontiguousarray(g("L1_ffn_ln_b")[layer].reshape(FCH, 128).T)
    qsc = np.ascontiguousarray(np.tile(g("L1_ffn_q_scale")[layer], 8)[None, :])
    ksc = np.ascontiguousarray(np.tile(g("L1_ffn_k_scale")[layer], 8)[None, :])
    pp = np.arange(128)[:, None]
    uu = np.arange(1408)[None, :]
    maskg = np.where((uu - 384 - pp >= 0) & (uu - 384 - pp <= 512), 0.0, -1e30
                     ).astype(np.float32)
    shared = dict(wqkv=wqkv, wo_in=wo, w1a_in=w1a, w1g_in=w1g, w2_in=w2,
                  b1a_in=b1a, b1g_in=b1g, b2_in=b2, lg_in=lg, lb_in=lb,
                  qsc_in=qsc, ksc_in=ksc, masks_in=maskg,
                  ones_in=np.ones((1, 128), np.float32),
                  iden=np.eye(128, dtype=np.float32),
                  cvec=np.stack([np.full(128, 1e-12, np.float32),
                                 np.full(128, 1e-5, np.float32)], 1))
    maps = []
    bf = ml_dtypes.bfloat16
    for c in range(NCORES):
        b = c // 4
        t0 = (c % 4) * 2048
        g0 = t0 - 1664
        m = dict(shared)
        xin = np.zeros((D, TIN), bf)
        lo = max(0, g0)
        xin[:, lo - g0:] = xiT_full[b][:, lo:g0 + TIN]
        m["xin"] = xin
        pos = np.arange(g0 + 128, g0 + TIN)
        csq, ssq, csk, ssk = _rotary_tables(pos)
        m["csq_in"], m["ssq_in"], m["csk_in"], m["ssk_in"] = csq, ssq, csk, ssk
        ksel = np.zeros((128, 8), np.float32)
        for Dw in range(-1, 4):
            if t0 + 512 * (Dw - 1) < 0:
                ksel[:, Dw + 1] = -1e30
        m["ksel_in"] = ksel
        if final:
            m["x1_own"] = x1_list[c]
            m["s1T_own"] = np.ascontiguousarray(
                s1T_full[b][:, t0:t0 + NTOK])
        maps.append(m)
    return maps


def kernel(**inputs):
    from concourse.bass_utils import run_bass_kernel_spmd
    bf = ml_dtypes.bfloat16
    cores = list(range(NCORES))

    def asbf(a):
        a = np.ascontiguousarray(np.asarray(a))
        return a.view(bf) if a.dtype != bf else a

    res = run_bass_kernel_spmd(build_prep().nc, prep_inmaps(inputs), core_ids=cores)
    prep = {
        "xnT": [np.asarray(res.results[c]["o_xnT"]) for c in cores],
        "kside": [np.asarray(res.results[c]["o_kside"]).reshape(3, 128, NTOK)
                  for c in cores],
        "pT": [np.asarray(res.results[c]["o_pT"]) for c in cores],
        "mvoT": [np.asarray(res.results[c]["o_mvoT"]) for c in cores],
        "gram": [np.asarray(res.results[c]["o_gram"]) for c in cores],
    }

    res = run_bass_kernel_spmd(build_l0attn().nc, l0attn_inmaps(inputs, prep),
                               core_ids=cores)
    s0T = [np.asarray(res.results[c]["o_s0T"]) for c in cores]

    res = run_bass_kernel_spmd(build_l0ffn().nc, l0ffn_inmaps(inputs, prep),
                               core_ids=cores)
    g0T = [np.asarray(res.results[c]["o_g0T"]) for c in cores]

    res = run_bass_kernel_spmd(build_resid1().nc, resid1_inmaps(inputs, s0T, g0T),
                               core_ids=cores)
    x1_list = [np.asarray(res.results[c]["o_x1"]) for c in cores]
    xn1T = [asbf(res.results[c]["o_xn1T"]) for c in cores]
    r1 = {"qT": [asbf(res.results[c]["o_qT"]) for c in cores],
          "kT": [asbf(res.results[c]["o_kT"]) for c in cores],
          "vtm": [asbf(res.results[c]["o_vtm"]) for c in cores]}

    res = run_bass_kernel_spmd(build_l1attn(0, 4, False).nc,
                               l1attn_inmaps(inputs, r1, 0), core_ids=cores)
    yA = [asbf(res.results[c]["o_yT"]) for c in cores]
    res = run_bass_kernel_spmd(build_l1attn(4, 8, True).nc,
                               l1attn_inmaps(inputs, r1, 1, yT_other_list=yA),
                               core_ids=cores)
    s1z = [np.asarray(res.results[c]["o_s1T"]) for c in cores]
    s1T_full = [np.zeros((D, T), np.float32) for _ in range(2)]
    for c in cores:
        b, s = c // 4, c % 4
        strip = np.concatenate([np.arange(64) + (4 * ib + s) * 64 for ib in range(32)])
        s1T_full[b][:, strip] = s1z[c]

    xiT_full = [np.concatenate(xn1T[b * 4:(b + 1) * 4], axis=1) for b in range(2)]
    res = run_bass_kernel_spmd(build_l1ffn(False).nc,
                               l1ffn_inmaps(inputs, xiT_full, 0, False),
                               core_ids=cores)
    xi = [asbf(res.results[c]["o_xi"]) for c in cores]

    xi_full = [np.concatenate(xi[b * 4:(b + 1) * 4], axis=1) for b in range(2)]
    res = run_bass_kernel_spmd(
        build_l1ffn(True).nc,
        l1ffn_inmaps(inputs, xi_full, 1, True, x1_list=x1_list, s1T_full=s1T_full),
        core_ids=cores)
    out = np.empty((GT, D), np.float32)
    for c in cores:
        out[c * NTOK:(c + 1) * NTOK] = np.asarray(res.results[c]["o_out"])
    return out.reshape(2, T, D)


# version 03 (final)
# speedup vs baseline: 1.0564x; 1.0564x over previous
import numpy as np
import ml_dtypes
from contextlib import ExitStack

import concourse.bass as bass
import concourse.mybir as mybir
from concourse.bass_utils import run_bass_kernel_spmd

F32 = mybir.dt.float32
BF16 = mybir.dt.bfloat16
AF = mybir.ActivationFunctionType
ALU = mybir.AluOpType
AX = mybir.AxisListType

NCORES = 8
B, T, D = 2, 8192, 1024
GT = B * T
NTOK = GT // NCORES
NTT = NTOK // 128
FCH = D // 128

ENGINES = ("sync", "tensor", "vector", "scalar", "gpsimd")


class _RecInstr:
    def __init__(self, prog, slot):
        self.prog, self.slot = prog, slot

    def inc(self, sem, n=1):
        h = self.prog.sem(sem)
        self.prog.cnt[sem] += n
        self.slot.append(lambda ins, h=h, n=n: ins.then_inc(h, n))
        return self


class _EngProxy:
    def __init__(self, prog, eng):
        self._p, self._e = prog, eng

    def wait(self, sem, v=None):
        p = self._p
        h = p.sem(sem)
        v = p.cnt[sem] if v is None else v
        if v > 0:
            p.on(self._e, lambda E, h=h, v=v: E.wait_ge(h, v))

    def __getattr__(self, op):
        p, e = self._p, self._e

        def rec(*args, **kwargs):
            slot = []

            def closure(E, op=op, args=args, kwargs=kwargs, slot=slot):
                ins = getattr(E, op)(*args, **kwargs)
                for f in slot:
                    f(ins)

            p.on(e, closure)
            return _RecInstr(p, slot)

        return rec


class Prog:

    def __init__(self):
        self.nc = bass.Bass("TRN2", target_bir_lowering=False)
        self.nc.detect_race_conditions = False
        self.es = ExitStack()
        self.q = {e: [] for e in ENGINES}
        self.sems = {}
        self.cnt = {}
        self._scoped = {}
        self._trgrp = 0
        self.sy = _EngProxy(self, "sync")
        self.te = _EngProxy(self, "tensor")
        self.ve = _EngProxy(self, "vector")
        self.sc = _EngProxy(self, "scalar")
        self.gp = _EngProxy(self, "gpsimd")

    def sb(self, name, shape, dt):
        return self.es.enter_context(self.nc.sbuf_tensor(name, shape, dt))

    def sb_tmp(self, name, shape, dt):
        cm = self.nc.sbuf_tensor(name, shape, dt)
        h = cm.__enter__()
        self._scoped[name] = cm
        return h

    def sb_free(self, name):
        cm = self._scoped.pop(name)
        cm.__exit__(None, None, None)

    def psum(self, name, shape, dt=F32):
        return self.es.enter_context(self.nc.psum_tensor(name, shape, dt))

    def din(self, name, shape, dt=F32):
        return self.nc.dram_tensor(name, shape, dt, kind="ExternalInput")

    def dout(self, name, shape, dt=F32):
        return self.nc.dram_tensor(name, shape, dt, kind="ExternalOutput")

    def dtmp(self, name, shape, dt, addr_space="Local"):
        return self.nc.dram_tensor(name, shape, dt, addr_space=addr_space)

    def use_consts(self, values):
        self.const_values = list(values)
        self._cvec_in = self.nc.dram_tensor("cvec", [128, max(1, len(values))], F32,
                                            kind="ExternalInput")
        self._cbuf = self.sb("cbuf", [128, max(1, len(values))], F32)
        self.sy.dma_start(out=self._cbuf[:], in_=self._cvec_in.ap()).inc("lc", 16)

    def cv(self, value):
        i = self.const_values.index(value)
        return self._cbuf[:, i:i + 1]

    def sem(self, name):
        if name not in self.sems:
            self.sems[name] = self.es.enter_context(self.nc.semaphore(name))
            self.cnt[name] = 0
        return self.sems[name]

    def val(self, name):
        self.sem(name)
        return self.cnt[name]

    def on(self, eng, f):
        self.q[eng].append(f)

    def finish(self):
        blk = self.es.enter_context(self.nc.Block())
        for name in reversed(list(self._scoped)):
            pass
        for e in ENGINES:
            ops = self.q[e]
            if not ops:
                continue
            getattr(blk, e)(lambda E, ops=ops: [f(E) for f in ops])
        blk.__class__
        for name in reversed(list(self._scoped)):
            self._scoped.pop(name).__exit__(None, None, None)
        self.es.close()
        return self.nc


def dap(t, off, dims):
    return bass.AP(t, off, dims)



def emit_ln(p, x_tm, xn_tm, stat, waste, nt, dwidth, eps_ap, gate=None):
    sc, ve = p.sc, p.ve
    if gate:
        sc.wait(*gate)
        ve.wait(*gate)
    for tt in range(nt):
        sc.activation(waste[:], x_tm[:, tt * dwidth:(tt + 1) * dwidth], AF.Square,
                      accum_out=stat[:, nt + tt:nt + tt + 1])
    sc.drain().inc("lns")
    se_sq = p.val("lns")

    ve.wait("lns", se_sq)
    for tt in range(nt):
        ve.tensor_reduce(stat[:, tt:tt + 1], x_tm[:, tt * dwidth:(tt + 1) * dwidth],
                         AX.X, ALU.add)
    ve.drain()
    S = lambda i: stat[:, i * nt:(i + 1) * nt]
    ve.tensor_scalar(S(2), S(0), 1.0 / dwidth, None, ALU.mult)
    ve.tensor_scalar(S(3), S(1), 1.0 / dwidth, None, ALU.mult)
    ve.drain()
    ve.tensor_tensor(S(4), S(2), S(2), ALU.mult)
    ve.drain()
    ve.tensor_tensor(S(5), S(3), S(4), ALU.subtract)
    ve.drain().inc("lnv")
    sc.wait("lnv", p.val("lnv"))
    sc.activation(S(6), S(5), AF.Sqrt, bias=eps_ap)
    sc.drain().inc("lns")
    ve.wait("lns", p.val("lns"))
    ve.reciprocal(S(7), S(6))
    ve.drain()
    for tt in range(nt):
        ve.tensor_scalar(xn_tm[:, tt * dwidth:(tt + 1) * dwidth],
                         x_tm[:, tt * dwidth:(tt + 1) * dwidth],
                         stat[:, 2 * nt + tt:2 * nt + tt + 1],
                         stat[:, 7 * nt + tt:7 * nt + tt + 1],
                         ALU.subtract, ALU.mult)
    ve.drain().inc("lnv")
    return ("lnv", p.val("lnv"))


def emit_transpose(p, groups, ident, pbA, pbB, gate_te=None, evac="vector"):
    te = p.te
    ev = p.ve if evac == "vector" else p.sc
    if gate_te:
        te.wait(*gate_te)
    for srcs, dst, cols in groups:
        pb = pbA if p._trgrp % 2 == 0 else pbB
        if p._trgrp >= 2:
            te.wait("evt", p.val("evt") - 1)
        off = 0
        last = None
        for s in srcs:
            s_ap, w = s[0], s[1]
            idap = s[2] if len(s) > 2 else ident[:]
            last = te.transpose(pb[:, off:off + w], s_ap, idap)
            off += w
        assert off == cols
        last.inc("mmt")
        ev.wait("mmt", p.val("mmt"))
        if evac == "vector":
            ev.tensor_copy(dst, pb[:, 0:cols]).inc("evt")
        else:
            ev.activation(dst, pb[:, 0:cols], AF.Copy).inc("evt")
        p._trgrp += 1
    return ("evt", p.val("evt"))



def build_prep(upto=None):
    p = Prog()
    x_own = p.din("x_own", [NTOK, D], F32)
    iden = p.din("iden", [128, 128], F32)
    wks1 = p.din("wks1", [D, 128], F32)
    wks2 = p.din("wks2", [D, 128], F32)
    wks3 = p.din("wks3", [D, 35], F32)
    wq1 = p.din("wq1", [D, 128], F32)
    wm1 = p.din("wm1", [D, 128], F32)
    wvm1 = p.din("wvm1", [D, 64], F32)
    wq2 = p.din("wq2", [64, D], F32)
    wqs2 = p.din("wqs2", [64, D], F32)
    wm2 = p.din("wm2", [64, D], F32)
    wms2 = p.din("wms2", [64, D], F32)
    wvm2 = p.din("wvm2", [64, D], F32)
    wk2t = p.din("wk2t", [D, 128], F32)

    o_xnT = p.dout("o_xnT", [D, NTOK], BF16)
    o_kside = p.dout("o_kside", [3, 128, NTOK], BF16)
    o_pT = p.dout("o_pT", [128, NTOK], BF16)
    o_mvoT = p.dout("o_mvoT", [D, NTOK], BF16)
    o_gram = p.dout("o_gram", [48, 64], F32)
    o_dbg = p.dout("o_dbg", [128, NTT * D], BF16)
    o_dbg2 = p.dout("o_dbg2", [128, 8 * NTT], F32)

    xnT = p.sb("xnT", [128, FCH * NTOK], BF16)
    stat = p.sb("stat", [128, 8 * NTT], F32)
    waste = p.sb("waste", [128, D], F32)
    identf = p.sb("identf", [128, 128], F32)
    identb = p.sb("identb", [128, 128], BF16)
    wbuf = p.sb("wbuf", [128, 56 * 128], BF16)
    w2buf = p.sb("w2buf", [64, 5 * D], BF16)
    ksd = p.sb("ksd", [128, 3 * NTOK], BF16)
    ph_q = p.sb("ph_q", [64, 2 * NTOK], BF16)
    ph_m = p.sb("ph_m", [64, 2 * NTOK], BF16)
    idb64 = p.sb("idb64", [128, 64], BF16)
    qbuf = p.sb("qbuf", [128, NTOK], BF16)
    pTsb = p.sb("pTsb", [128, NTOK], BF16)
    mvoh = p.sb("mvoh", [64, NTOK], BF16)
    gtm = p.sb("gtm", [128, NTT * 288], BF16)
    gsb = p.sb("gsb", [16, 3 * 64], F32)

    pbA = p.psum("pbA", [128, 1024], BF16)
    pbB = p.psum("pbB", [128, 1024], BF16)
    pg = p.psum("pg", [128, 2048], F32)
    pg2 = p.psum("pg2", [128, 512], F32)
    pgr = p.psum("pgr", [16, 192], F32)

    sy, te, ve, sc, gp = p.sy, p.te, p.ve, p.sc, p.gp
    p.use_consts([1e-5])
    p.sc.wait("lc", 16)
    p.ve.wait("lc", 16)
    x_tm = p.sb_tmp("x_tm", [128, NTT * D], F32)
    xn_tm = p.sb_tmp("xn_tm", [128, NTT * D], BF16)

    sy.dma_start(out=dap(x_tm, 0, [[NTT * D, 128], [D, NTT], [1, D]]),
                 in_=dap(x_own, 0, [[D, 128], [128 * D, NTT], [1, D]])).inc("ld", 16)
    sy.dma_start(out=identf[:], in_=iden.ap()).inc("ld", 16)
    gp.dma_start(out=identb[:], in_=iden.ap()).inc("lw", 16)
    gp.dma_start(out=idb64[64:128, 0:64], in_=iden.ap()[0:64, 0:64]).inc("lw", 16)

    wmap = {}
    nslot = [0]

    def load_w(name, ap, rows, cols):
        nch = rows // 128
        wmap[name] = (nslot[0], nch, cols)
        first = nslot[0]
        nslot[0] += nch
        gp.dma_start(
            out=dap(wbuf, first * 128, [[56 * 128, 128], [128, nch], [1, cols]]),
            in_=dap(ap, 0, [[cols, 128], [128 * cols, nch], [1, cols]]),
        ).inc("lw", 16)

    def wtile(name, c):
        first, nch, cols = wmap[name]
        return wbuf[:, (first + c) * 128:(first + c) * 128 + cols]

    load_w("wks1", wks1, D, 128)
    load_w("wks2", wks2, D, 128)
    load_w("wks3", wks3, D, 35)
    load_w("wq1", wq1, D, 128)
    load_w("wm1", wm1, D, 128)
    load_w("wk2t", wk2t, D, 128)
    load_w("wvm1", wvm1, D, 64)

    w2map = {}
    n2 = [0]

    def load_w2(name, ap):
        w2map[name] = n2[0]
        gp.dma_start(out=dap(w2buf, n2[0] * D, [[5 * D, 64], [1, D]]),
                     in_=dap(ap, 0, [[D, 64], [1, D]])).inc("lw", 16)
        n2[0] += 1

    def w2tile(name, m):
        return w2buf[:, w2map[name] * D + m * 128: w2map[name] * D + (m + 1) * 128]

    for nm, ap in [("wq2", wq2), ("wqs2", wqs2), ("wm2", wm2), ("wms2", wms2),
                   ("wvm2", wvm2)]:
        load_w2(nm, ap)

    ve.memset(ksd[:, 2 * NTOK:3 * NTOK], 0.0).inc("msk")
    sc.wait("msk", p.val("msk"))

    gate_xn = emit_ln(p, x_tm, xn_tm, stat, waste, NTT, D, p.cv(1e-5), gate=("ld", 32))

    def _cut(tag):
        if upto != tag:
            return False
        sy.wait("so", p.val("so"))
        for e in ("lnv", "lns", "evt", "kv", "pv", "qv", "vv", "gv"):
            pass
        p.finish()
        return True

    if upto == "ln":
        sy.wait("lnv", p.val("lnv"))
        sy.dma_start(out=dap(o_dbg, 0, [[NTT * D, 128], [1, NTT * D]]),
                     in_=xn_tm[:]).inc("so", 16)
        sy.dma_start(out=dap(o_dbg2, 0, [[8 * NTT, 128], [1, 8 * NTT]]),
                     in_=stat[:]).inc("so", 16)
        sy.wait("so", p.val("so"))
        p.finish()
        return p

    groups = []
    for fc in range(FCH):
        for g in range(2):
            srcs = [(xn_tm[:, (g * 8 + i) * D + fc * 128:(g * 8 + i) * D + (fc + 1) * 128], 128)
                    for i in range(8)]
            dst = xnT[:, fc * NTOK + g * 1024: fc * NTOK + (g + 1) * 1024]
            groups.append((srcs, dst, 1024))
    te.wait("lw")
    gate_xnT = emit_transpose(p, groups, identb, pbA, pbB, gate_te=gate_xn)
    sy.wait(*gate_xnT)
    sy.dma_start(out=dap(o_xnT, 0, [[NTOK, 128], [NTOK * 128, FCH], [1, NTOK]]),
                 in_=dap(xnT, 0, [[FCH * NTOK, 128], [NTOK, FCH], [1, NTOK]])).inc("so", 16)

    if upto == "xnT":
        sy.wait("so", p.val("so"))
        p.finish()
        return p

    te.wait("lw")
    te.wait("evt", p.val("evt"))
    for mt, (wn, cols) in enumerate([("wks1", 128), ("wks2", 128), ("wks3", 35)]):
        if mt >= 1:
            te.wait("kv", p.val("kv"))
        last = None
        for n in range(4):
            for kc in range(FCH):
                last = te.matmul(pg[0:cols, n * 512:(n + 1) * 512],
                                 wtile(wn, kc),
                                 xnT[:, kc * NTOK + n * 512: kc * NTOK + (n + 1) * 512],
                                 start=(kc == 0), stop=(kc == FCH - 1))
        last.inc("mmk")
        sc.wait("mmk", p.val("mmk"))
        sc.activation(ksd[0:cols, mt * NTOK:(mt + 1) * NTOK], pg[0:cols, :],
                      AF.Copy).inc("kv")
    sy.wait("kv", p.val("kv"))
    sy.dma_start(out=dap(o_kside, 0, [[NTOK, 128], [128 * NTOK, 2], [1, NTOK]]),
                 in_=dap(ksd, 0, [[3 * NTOK, 128], [NTOK, 2], [1, NTOK]])).inc("so", 16)
    sy.dma_start(out=dap(o_kside, 2 * 128 * NTOK, [[NTOK, 35], [1, NTOK]]),
                 in_=dap(ksd, 2 * NTOK, [[3 * NTOK, 35], [1, NTOK]])).inc("so", 16)

    if upto == "kside":
        sy.wait("so", p.val("so"))
        p.finish()
        return p

    te.wait("kv", p.val("kv"))
    for buf in range(2):
        te.wait("pv", p.val("pv"))
        last = None
        for n in range(4):
            for kc in range(FCH):
                te.matmul(pg[0:64, n * 512:(n + 1) * 512],
                          wtile("wq1", kc)[:, buf * 64:buf * 64 + 64],
                          xnT[:, kc * NTOK + n * 512: kc * NTOK + (n + 1) * 512],
                          start=(kc == 0), stop=(kc == FCH - 1),
                          tile_position=(0, 0))
            for kc in range(FCH):
                last = te.matmul(pg[64:128, n * 512:(n + 1) * 512],
                                 wtile("wm1", kc)[:, buf * 64:buf * 64 + 64],
                                 xnT[:, kc * NTOK + n * 512: kc * NTOK + (n + 1) * 512],
                                 start=(kc == 0), stop=(kc == FCH - 1),
                                 tile_position=(0, 64))
        last.inc("mmp")
        sc.wait("mmp", p.val("mmp"))
        sc.activation(ph_q[:, buf * NTOK:(buf + 1) * NTOK], pg[0:64, :], AF.Copy)
        sc.activation(ph_m[:, buf * NTOK:(buf + 1) * NTOK], pg[64:128, :], AF.Copy).inc("pv")

    if upto == "stageA":
        sy.wait("pv", p.val("pv"))
        sy.dma_start(out=dap(o_dbg, 0, [[2 * NTOK, 64], [1, 2 * NTOK]]),
                     in_=ph_q[:]).inc("so", 16)
        sy.dma_start(out=dap(o_dbg, 64 * 2 * NTOK, [[2 * NTOK, 64], [1, 2 * NTOK]]),
                     in_=ph_m[:]).inc("so", 16)
        sy.wait("so", p.val("so"))
        p.finish()
        return p

    p.sb_free("xn_tm")
    p.sb_free("x_tm")
    qm = p.sb_tmp("qm", [128, FCH * NTOK], BF16)
    mvoT = p.sb_tmp("mvoT", [128, 4 * NTOK], BF16)

    te.wait("pv", p.val("pv"))
    for half in range(2):
        qh_rows = ph_q[:, half * NTOK:(half + 1) * NTOK]
        mgh_rows = ph_m[:, half * NTOK:(half + 1) * NTOK]
        wq2n = "wq2" if half == 0 else "wqs2"
        wm2n = "wm2" if half == 0 else "wms2"
        for m in range(FCH):
            te.wait("qv", p.val("qv"))
            last = None
            for n in range(4):
                last = te.matmul(pg[:, n * 512:(n + 1) * 512], w2tile(wq2n, m),
                                 qh_rows[:, n * 512:(n + 1) * 512], start=True, stop=True)
            last.inc("mmq")
            sc.wait("mmq", p.val("mmq"))
            sc.activation(qbuf[:], pg[:], AF.Copy).inc("sq")
            ve.wait("sq", p.val("sq"))
            for n in range(4):
                te.wait("qv2", p.val("qv2"))
                te.matmul(pg2[:], w2tile(wm2n, m),
                          mgh_rows[:, n * 512:(n + 1) * 512],
                          start=True, stop=True).inc("mmq2")
                ve.wait("mmq2", p.val("mmq2"))
                ve.tensor_tensor(qm[:, m * NTOK + n * 512:m * NTOK + (n + 1) * 512],
                                 qbuf[:, n * 512:(n + 1) * 512], pg2[:],
                                 ALU.mult).inc("qv2")
            ve.engine_nop().inc("qv")
        te.wait("qv", p.val("qv"))
        last = None
        for n in range(4):
            for kc in range(FCH):
                last = te.matmul(pg[(half * 64):(half * 64 + 64), n * 512:(n + 1) * 512],
                                 wtile("wk2t", kc)[:, half * 64:half * 64 + 64],
                                 qm[:, kc * NTOK + n * 512: kc * NTOK + (n + 1) * 512],
                                 start=(kc == 0), stop=(kc == FCH - 1),
                                 tile_position=(0, half * 64))
        last.inc("mmq")
        sc.wait("mmq", p.val("mmq"))
        sc.activation(pTsb[(half * 64):(half * 64 + 64), :],
                      pg[(half * 64):(half * 64 + 64), :], AF.Copy).inc("qv")
        te.wait("qv", p.val("qv"))
    sy.wait("qv", p.val("qv"))
    sy.dma_start(out=o_pT.ap(), in_=pTsb[:]).inc("so", 16)
    if upto == "stageB":
        sy.wait("so", p.val("so"))
        p.finish()
        return p

    last = None
    for n in range(4):
        for kc in range(FCH):
            last = te.matmul(pg[0:64, n * 512:(n + 1) * 512], wtile("wvm1", kc),
                             xnT[:, kc * NTOK + n * 512: kc * NTOK + (n + 1) * 512],
                             start=(kc == 0), stop=(kc == FCH - 1))
    last.inc("mmv")
    sc.wait("mmv", p.val("mmv"))
    sc.activation(mvoh[:], pg[0:64, :], AF.Copy).inc("vv")
    if upto == "mvoh":
        sy.wait("vv", p.val("vv"))
        p.finish()
        return p
    te.wait("vv", p.val("vv"))
    for m in range(FCH):
        last = None
        for n in range(4):
            last = te.matmul(pg[:, n * 512:(n + 1) * 512], w2tile("wvm2", m),
                             mvoh[:, n * 512:(n + 1) * 512], start=True, stop=True)
        last.inc("mmv")
        sc.wait("mmv", p.val("mmv"))
        sc.activation(mvoT[:, (m % 4) * NTOK:(m % 4 + 1) * NTOK], pg[:], AF.Copy).inc("vv")
        te.wait("vv", p.val("vv"))
        if m % 4 == 3:
            sy.wait("vv", p.val("vv"))
            sy.dma_start(out=dap(o_mvoT, (m - 3) * 128 * NTOK,
                                 [[NTOK, 128], [NTOK * 128, 4], [1, NTOK]]),
                         in_=dap(mvoT, 0, [[4 * NTOK, 128], [NTOK, 4], [1, NTOK]])).inc("so", 16)
            sc.wait("so", p.val("so"))
    if upto == "mvo":
        sy.wait("so", p.val("so"))
        p.finish()
        return p

    te.wait("kv", p.val("kv"))
    for tt in range(NTT):
        pb = pbA if p._trgrp % 2 == 0 else pbB
        if p._trgrp >= 2:
            te.wait("evt", p.val("evt") - 1)
        for i in range(3):
            last = te.transpose(pb[:, i * 128:(i + 1) * 128],
                                ksd[:, i * NTOK + tt * 128:i * NTOK + (tt + 1) * 128],
                                identb[:])
        last.inc("mmt")
        ve.wait("mmt", p.val("mmt"))
        ve.tensor_copy(gtm[:, tt * 288:tt * 288 + 256], pb[:, 0:256])
        ve.tensor_copy(gtm[:, tt * 288 + 256:tt * 288 + 288], pb[:, 256:288]).inc("evt")
        p._trgrp += 1
    gate_g = ("evt", p.val("evt"))
    if upto == "gtm":
        sy.wait(*gate_g)
        p.finish()
        return p
    te.wait(*gate_g)
    OF = {"Kf": 0, "Kfs": 64, "Vf": 128, "B1": 192, "B1s": 208, "A1": 224,
          "A1s": 240, "A2": 256, "B2": 272}
    TTW = 288
    last = None
    for gi, (wa, wb) in enumerate([("B1", "Kf"), ("B1s", "Kfs"), ("A2", "Vf")]):
        for tt in range(NTT):
            g = lambda nm, w, tt=tt: gtm[:, tt * TTW + OF[nm]: tt * TTW + OF[nm] + w]
            last = te.matmul(pgr[:, gi * 64:(gi + 1) * 64], g(wa, 16), g(wb, 64),
                             start=(tt == 0), stop=(tt == NTT - 1))
    last.inc("mmg")
    ve.wait("mmg", p.val("mmg"))
    ve.tensor_copy(gsb[:], pgr[:]).inc("gv")
    if upto == "gmm":
        sy.wait("gv", p.val("gv"))
        p.finish()
        return p
    sy.wait("gv", p.val("gv"))
    sy.dma_start(out=dap(o_gram, 0, [[64, 16], [16 * 64, 3], [1, 64]]),
                 in_=dap(gsb, 0, [[192, 16], [64, 3], [1, 64]])).inc("so", 16)

    sy.wait("so", p.val("so"))
    p.finish()
    return p



def _f32(a):
    return np.ascontiguousarray(np.asarray(a), dtype=np.float32)


def prep_inmaps(inputs):
    x = _f32(inputs["L0_attn_x"]).reshape(GT, D)
    g = lambda n: _f32(inputs[n])
    cat = lambda *a: np.ascontiguousarray(np.concatenate(a, axis=1))
    shared = dict(
        iden=np.eye(128, dtype=np.float32),
        wks1=cat(g("L0_attn_qk_Wk1"), g("L0_attn_qks_Wk1")),
        wks2=cat(g("L0_attn_vo_Wv1"), g("L0_attn_sm1_Wb"), g("L0_attn_sm1s_Wb"),
                 g("L0_attn_sm1_Wa"), g("L0_attn_sm1s_Wa")),
        wks3=cat(g("L0_attn_sm2_Wa"), g("L0_attn_sm2_Wb"),
                 g("L0_attn_sm1_Wd"), g("L0_attn_sm1s_Wd"), g("L0_attn_sm2_Wd")),
        wq1=cat(g("L0_attn_qk_Wq1"), g("L0_attn_qks_Wq1")),
        wm1=cat(g("L0_attn_qk_Wm1"), g("L0_attn_qks_Wm1")),
        wvm1=g("L0_attn_vo_Wm1"),
        wq2=g("L0_attn_qk_Wq2"), wqs2=g("L0_attn_qks_Wq2"),
        wm2=g("L0_attn_qk_Wm2"), wms2=g("L0_attn_qks_Wm2"),
        wvm2=g("L0_attn_vo_Wm2"),
        wk2t=cat(np.ascontiguousarray(g("L0_attn_qk_Wk2").T),
                 np.ascontiguousarray(g("L0_attn_qks_Wk2").T)),
    )
    shared["cvec"] = np.full((128, 1), 1e-5, dtype=np.float32)
    maps = []
    for c in range(NCORES):
        m = dict(shared)
        m["x_own"] = np.ascontiguousarray(x[c * NTOK:(c + 1) * NTOK])
        maps.append(m)
    return maps





TB = T
NJC = TB // 512
NJT = TB // 128

def build_l0attn():
    p = Prog()
    kside_b = p.din("kside_b", [3, 128, TB], BF16)
    pT_own = p.din("pT_own", [128, NTOK], BF16)
    mvoT_own = p.din("mvoT_own", [D, NTOK], BF16)
    gram4 = p.din("gram4", [4, 48, 64], F32)
    wv2 = p.din("wv2", [64, D], F32)
    wot = p.din("wot", [D, D], F32)
    ones_in = p.din("ones_in", [1, 128], F32)
    iden = p.din("iden", [128, 128], F32)
    o_s0T = p.dout("o_s0T", [D, NTOK], F32)

    sy, te, ve, sc, gp = p.sy, p.te, p.ve, p.sc, p.gp
    p.use_consts([1e-8])
    sc.wait("lc", 16)
    ve.wait("lc", 16)

    keff = p.sb("keff", [128, TB], BF16)
    veff_tm = p.sb("veff_tm", [128, NJT * 128], BF16)
    ctxT = p.sb("ctxT", [64, NTOK], BF16)
    pTsb = p.sb("pTsb", [128, NTOK], BF16)
    wv2b = p.sb("wv2b", [64, D], BF16)
    onesb = p.sb("onesb", [1, 128], BF16)
    identb = p.sb("identb", [128, 128], BF16)
    g1b = p.sb("g1b", [16, 64], BF16)
    g1sb = p.sb("g1sb", [16, 64], BF16)
    n2b = p.sb("n2b", [16, 64], BF16)
    gsum = p.sb("gsum", [16, 3 * 4 * 64], F32)
    nrm = p.sb("nrm", [128, 16], F32)

    pA1 = p.psum("pA1", [128, 1024], F32)
    pA2 = p.psum("pA2", [128, 1024], F32)
    pb = p.psum("pb", [128, 1024], BF16)
    pctx = p.psum("pctx", [128, 512], F32)
    pm1 = p.psum("pm1", [128, 512], F32)
    pm2 = p.psum("pm2", [128, 512], F32)

    ksb = p.sb_tmp("ksb", [128, 3 * TB], BF16)
    vefful = p.sb_tmp("vefful", [128, TB], BF16)
    a1r = p.sb_tmp("a1r", [16, TB], BF16)
    a1sr = p.sb_tmp("a1sr", [16, TB], BF16)
    b2r = p.sb_tmp("b2r", [16, TB], BF16)
    d1r = p.sb_tmp("d1r", [1, TB], BF16)
    d1sr = p.sb_tmp("d1sr", [1, TB], BF16)
    d2r = p.sb_tmp("d2r", [1, TB], BF16)

    sy.dma_start(out=dap(ksb, 0, [[3 * TB, 128], [TB, 3], [1, TB]]),
                 in_=dap(kside_b, 0, [[TB, 128], [128 * TB, 3], [1, TB]])).inc("ld", 16)
    sy.dma_start(out=pTsb[:], in_=pT_own.ap()).inc("ld", 16)
    gp.dma_start(out=identb[:], in_=iden.ap()).inc("lw", 16)
    gp.dma_start(out=onesb[:], in_=ones_in.ap()).inc("lw", 16)
    gp.dma_start(out=wv2b[:], in_=wv2.ap()).inc("lw", 16)

    for gi in range(3):
        sy.dma_start(out=dap(gsum, gi * 256, [[3 * 4 * 64, 16], [64, 4], [1, 64]]),
                     in_=dap(gram4, gi * 16 * 64, [[64, 16], [48 * 64, 4], [1, 64]])).inc("ld", 16)
    ve.wait("ld", p.val("ld"))
    for gi, dst in enumerate([g1b, g1sb, n2b]):
        b0 = gi * 256
        ve.tensor_tensor(gsum[:, b0:b0 + 64], gsum[:, b0:b0 + 64],
                         gsum[:, b0 + 64:b0 + 128], ALU.add)
        ve.tensor_tensor(gsum[:, b0 + 128:b0 + 192], gsum[:, b0 + 128:b0 + 192],
                         gsum[:, b0 + 192:b0 + 256], ALU.add)
        ve.drain()
        ve.tensor_tensor(dst[:], gsum[:, b0:b0 + 64], gsum[:, b0 + 128:b0 + 192],
                         ALU.add)
    ve.drain().inc("gv")

    gp.wait("ld", p.val("ld"))
    gp.dma_start(out=a1r[:], in_=ksb[96:112, 1 * TB:2 * TB]).inc("lw", 16)
    gp.dma_start(out=a1sr[:], in_=ksb[112:128, 1 * TB:2 * TB]).inc("lw", 16)
    gp.dma_start(out=b2r[:], in_=ksb[16:32, 2 * TB:3 * TB]).inc("lw", 16)
    gp.dma_start(out=d1r[:], in_=ksb[32:33, 2 * TB:3 * TB]).inc("lw", 16)
    gp.dma_start(out=d1sr[:], in_=ksb[33:34, 2 * TB:3 * TB]).inc("lw", 16)
    gp.dma_start(out=d2r[:], in_=ksb[34:35, 2 * TB:3 * TB]).inc("lw", 16)

    ve.memset(vefful[64:128, :], 0.0).inc("vz")

    te.wait("lw", p.val("lw"))
    te.wait("gv", p.val("gv"))
    te.wait("ld", p.val("ld"))
    KF = ksb[0:64, 0:TB]
    KFS = ksb[64:128, 0:TB]
    VF = ksb[0:64, TB:2 * TB]
    for n in range(NJC):
        c0 = n * 512
        te.matmul(pm1[0:64, :], g1b[:], a1r[:, c0:c0 + 512], start=True, stop=True,
                  tile_position=(0, 0))
        te.matmul(pm1[64:128, :], g1sb[:], a1sr[:, c0:c0 + 512], start=True, stop=True,
                  tile_position=(0, 64))
        te.matmul(pm2[0:64, :], onesb[0:1, 0:64], d1r[:, c0:c0 + 512],
                  start=True, stop=True, tile_position=(0, 0))
        last = te.matmul(pm2[64:128, :], onesb[0:1, 0:64], d1sr[:, c0:c0 + 512],
                         start=True, stop=True, tile_position=(0, 64))
        last.inc("mk")
        ve.wait("mk", p.val("mk"))
        ve.tensor_tensor(keff[0:64, c0:c0 + 512], KF[:, c0:c0 + 512], pm2[0:64, :],
                         ALU.mult)
        ve.tensor_tensor(keff[64:128, c0:c0 + 512], KFS[:, c0:c0 + 512],
                         pm2[64:128, :], ALU.mult)
        ve.drain()
        ve.tensor_tensor(keff[0:64, c0:c0 + 512], keff[0:64, c0:c0 + 512],
                         pm1[0:64, :], ALU.add)
        ve.tensor_tensor(keff[64:128, c0:c0 + 512], keff[64:128, c0:c0 + 512],
                         pm1[64:128, :], ALU.add)
        ve.drain().inc("kv")
        te.wait("kv", p.val("kv"))
        te.matmul(pm1[0:64, :], n2b[:], b2r[:, c0:c0 + 512], start=True, stop=True,
                  tile_position=(0, 0))
        te.matmul(pm2[0:64, :], onesb[0:1, 0:64], d2r[:, c0:c0 + 512],
                  start=True, stop=True, tile_position=(0, 0)).inc("mk")
        ve.wait("mk", p.val("mk"))
        ve.tensor_tensor(vefful[0:64, c0:c0 + 512], VF[:, c0:c0 + 512], pm2[0:64, :],
                         ALU.mult)
        ve.drain()
        ve.tensor_tensor(vefful[0:64, c0:c0 + 512], vefful[0:64, c0:c0 + 512],
                         pm1[0:64, :], ALU.add)
        ve.drain().inc("kv")
        te.wait("kv", p.val("kv"))

    te.wait("vz", p.val("vz"))
    for g in range(8):
        pbx = pb
        if g >= 1:
            te.wait("evt", p.val("evt"))
        lastt = None
        for i in range(8):
            jt = g * 8 + i
            lastt = te.transpose(pbx[:, i * 128:(i + 1) * 128],
                                 vefful[:, jt * 128:(jt + 1) * 128], identb[:])
        lastt.inc("mmt")
        ve.wait("mmt", p.val("mmt"))
        ve.tensor_copy(veff_tm[:, g * 1024:(g + 1) * 1024], pbx[:]).inc("evt")
    ve.drain().inc("evt")

    for nm in ["d2r", "d1sr", "d1r", "b2r", "a1sr", "a1r", "vefful", "ksb"]:
        p.sb_free(nm)
    hm_sb = p.sb_tmp("hm_sb", [128, TB], BF16)
    sp_sb = p.sb_tmp("sp_sb", [128, TB], BF16)
    h_sb = p.sb_tmp("h_sb", [128, TB], BF16)
    hT_sb = p.sb_tmp("hT_sb", [128, TB], BF16)
    ab_sb = p.sb_tmp("ab_sb", [128, 1024], F32)
    rl_sb = p.sb_tmp("rl_sb", [128, 2 * 1024], F32)

    NBLK = NTOK // 128
    for ib in range(NBLK):
        lhs_p = pTsb[0:64, ib * 128:(ib + 1) * 128]
        lhs_ps = pTsb[64:128, ib * 128:(ib + 1) * 128]
        for cg in range(8):
            pA = pA1 if cg % 2 == 0 else pA2
            if cg >= 2:
                te.wait("hmv", p.val("hmv") - 1)
            elif ib > 0:
                te.wait("spv", p.val("spv"))
            for half in range(2):
                c0 = cg * 1024 + half * 512
                ins = te.matmul(pA[:, half * 512:(half + 1) * 512], lhs_p,
                                keff[0:64, c0:c0 + 512], start=True, stop=True,
                                tile_position=(0, 0))
            ins.inc("mhm")
            sc.wait("mhm", p.val("mhm"))
            sc.activation(hm_sb[:, cg * 1024:(cg + 1) * 1024], pA[:], AF.Copy)
            sc.activation(ab_sb[:], pA[:], AF.Square,
                          accum_out=nrm[:, cg:cg + 1])
            sc.drain().inc("hmv")
        ve.wait("hmv", p.val("hmv"))
        ve.tensor_reduce(nrm[:, 8:9], nrm[:, 0:8], AX.X, ALU.add)
        ve.drain().inc("nv")
        sc.wait("nv", p.val("nv"))
        sc.activation(nrm[:, 9:10], nrm[:, 8:9], AF.Sqrt)
        sc.drain().inc("ns")
        ve.wait("ns", p.val("ns"))
        ve.tensor_scalar(nrm[:, 10:11], nrm[:, 9:10], 1e-8, None, ALU.add)
        ve.drain()
        ve.reciprocal(nrm[:, 11:12], nrm[:, 10:11])
        ve.drain().inc("nv")
        for cg in range(8):
            pA = pA1 if cg % 2 == 0 else pA2
            if cg >= 2:
                te.wait("spv", p.val("spv") - 1)
            else:
                te.wait("hmv", p.val("hmv"))
            for half in range(2):
                c0 = cg * 1024 + half * 512
                ins = te.matmul(pA[:, half * 512:(half + 1) * 512], lhs_ps,
                                keff[64:128, c0:c0 + 512], start=True, stop=True,
                                tile_position=(64, 0))
            ins.inc("mhs")
            sc.wait("mhs", p.val("mhs"))
            sc.activation(ab_sb[:], pA[:], AF.Abs)
            sc.activation(rl_sb[:, 0:1024], pA[:], AF.Relu)
            sc.activation(ab_sb[:], ab_sb[:], AF.Exp, scale=-1.0)
            sc.activation(rl_sb[:, 1024:2048], ab_sb[:], AF.Ln, bias=1.0)
            sc.drain().inc("sps")
            ve.wait("sps", p.val("sps"))
            ve.tensor_tensor(sp_sb[:, cg * 1024:(cg + 1) * 1024], rl_sb[:, 0:1024],
                             rl_sb[:, 1024:2048], ALU.add)
            ve.drain().inc("spv")
        ve.wait("nv", p.val("nv"))
        for cg in range(8):
            ve.tensor_scalar(sp_sb[:, cg * 1024:(cg + 1) * 1024],
                             sp_sb[:, cg * 1024:(cg + 1) * 1024],
                             nrm[:, 11:12], None, ALU.mult)
        ve.drain()
        for cg in range(8):
            ve.scalar_tensor_tensor(h_sb[:, cg * 1024:(cg + 1) * 1024],
                                    hm_sb[:, cg * 1024:(cg + 1) * 1024], 0.0,
                                    sp_sb[:, cg * 1024:(cg + 1) * 1024],
                                    ALU.max, ALU.mult)
        ve.drain().inc("hv")
        te.wait("hv", p.val("hv"))
        for g in range(8):
            te.wait("hevt", p.val("hevt"))
            if ib == 0 and g == 0:
                te.wait("evt", p.val("evt"))
                te.wait("ctv", p.val("ctv"))
            lastt = None
            for i in range(8):
                jt = g * 8 + i
                lastt = te.transpose(pb[:, i * 128:(i + 1) * 128],
                                     h_sb[:, jt * 128:(jt + 1) * 128], identb[:])
            lastt.inc("mmt")
            ve.wait("mmt", p.val("mmt"))
            ve.tensor_copy(hT_sb[:, g * 1024:(g + 1) * 1024], pb[:]).inc("hevt")
        te.wait("hevt", p.val("hevt"))
        lastc = None
        for jt in range(NJT):
            lastc = te.matmul(pctx[:, 0:64], hT_sb[:, jt * 128:(jt + 1) * 128],
                              veff_tm[:, jt * 128:jt * 128 + 64],
                              start=(jt == 0), stop=(jt == NJT - 1))
        lastc.inc("mc")
        ve.wait("mc", p.val("mc"))
        ve.tensor_copy(hm_sb[:, 0:64], pctx[:, 0:64])
        ve.drain().inc("cv")
        te.wait("cv", p.val("cv"))
        te.wait("hevt", p.val("hevt"))
        te.transpose(pb[0:64, 0:128], hm_sb[:, 0:64], identb[:]).inc("mmt")
        ve.wait("mmt", p.val("mmt"))
        ve.tensor_copy(ctxT[0:64, ib * 128:(ib + 1) * 128], pb[0:64, 0:128])
        ve.drain().inc("ctv")
        te.wait("ctv", p.val("ctv"))

    for nm in ["rl_sb", "ab_sb", "hT_sb", "h_sb", "sp_sb", "hm_sb"]:
        p.sb_free(nm)
    mvoT = p.sb_tmp("mvoT", [128, FCH * NTOK], BF16)
    cm_sb = p.sb_tmp("cm_sb", [128, FCH * NTOK], BF16)
    wotb = p.sb_tmp("wotb", [128, FCH * D], BF16)
    s0st = p.sb_tmp("s0st", [128, 2 * NTOK], F32)

    gp.wait("ctv", p.val("ctv"))
    gp.dma_start(out=dap(mvoT, 0, [[FCH * NTOK, 128], [NTOK, FCH], [1, NTOK]]),
                 in_=dap(mvoT_own, 0, [[NTOK, 128], [128 * NTOK, FCH], [1, NTOK]])
                 ).inc("lw2", 16)
    gp.dma_start(out=dap(wotb, 0, [[FCH * D, 128], [D, FCH], [1, D]]),
                 in_=dap(wot, 0, [[D, 128], [128 * D, FCH], [1, D]])).inc("lw2", 16)

    te.wait("lw2", p.val("lw2"))
    for m in range(FCH):
        for g in range(2):
            pA = pA1 if g % 2 == 0 else pA2
            te.wait("cfv", p.val("cfv") - (1 if (m * 2 + g) >= 2 else 0))                 if (m * 2 + g) >= 2 else None
            for half in range(2):
                n0 = g * 1024 + half * 512
                te.matmul(pA[:, half * 512:(half + 1) * 512],
                          wv2b[:, m * 128:(m + 1) * 128],
                          ctxT[:, n0:n0 + 512], start=True, stop=True).inc(
                              "mcf" if half == 1 else "mcf0")
            ve.wait("mcf", p.val("mcf"))
            ve.tensor_tensor(cm_sb[:, m * NTOK + g * 1024:m * NTOK + (g + 1) * 1024],
                             mvoT[:, m * NTOK + g * 1024:m * NTOK + (g + 1) * 1024],
                             pA[:], ALU.mult)
            ve.drain().inc("cfv")
    for m in range(FCH):
        for g in range(2):
            pA = pA1 if g % 2 == 0 else pA2
            te.wait("s0v", p.val("s0v") - (1 if (m * 2 + g) >= 2 else 0))                 if (m * 2 + g) >= 2 else None
            if m == 0 and g < 2:
                te.wait("cfv", p.val("cfv"))
            lastm = None
            for half in range(2):
                n0 = g * 1024 + half * 512
                for kc in range(FCH):
                    lastm = te.matmul(pA[:, half * 512:(half + 1) * 512],
                                      wotb[:, kc * D + m * 128:kc * D + (m + 1) * 128],
                                      cm_sb[:, kc * NTOK + n0:kc * NTOK + n0 + 512],
                                      start=(kc == 0), stop=(kc == FCH - 1))
            lastm.inc("ms0")
            sc.wait("ms0", p.val("ms0"))
            buf = m % 2
            sc.activation(s0st[:, buf * NTOK + g * 1024:buf * NTOK + (g + 1) * 1024],
                          pA[:], AF.Copy)
            sc.drain().inc("s0v")
        sy.wait("s0v", p.val("s0v"))
        if m >= 2:
            pass
        sy.dma_start(out=dap(o_s0T, m * 128 * NTOK, [[NTOK, 128], [1, NTOK]]),
                     in_=s0st[:, (m % 2) * NTOK:(m % 2 + 1) * NTOK]).inc("so", 16)
        sc.wait("so", p.val("so"))

    sy.wait("so", p.val("so"))
    p.finish()
    return p


def l0attn_inmaps(inputs, prep_outs):
    g = lambda n: _f32(inputs[n])
    kside_full = np.concatenate(prep_outs["kside"], axis=2)
    gram_all = np.stack(prep_outs["gram"])
    shared = dict(
        wv2=g("L0_attn_vo_Wv2"),
        wot=np.ascontiguousarray(g("L0_attn_vo_Wo").T),
        ones_in=np.ones((1, 128), dtype=np.float32),
        iden=np.eye(128, dtype=np.float32),
        cvec=np.full((128, 1), 1e-8, dtype=np.float32),
    )
    maps = []
    for c in range(NCORES):
        b = c // 4
        m = dict(shared)
        m["kside_b"] = np.ascontiguousarray(kside_full[:, :, b * T:(b + 1) * T])
        m["pT_own"] = prep_outs["pT"][c]
        m["mvoT_own"] = prep_outs["mvoT"][c]
        m["gram4"] = np.ascontiguousarray(gram_all[b * 4:(b + 1) * 4])
        maps.append(m)
    return maps



def build_l0ffn():
    p = Prog()
    xnT_own = p.din("xnT_own", [D, NTOK], BF16)
    w1t = p.din("w1t", [D, 4096], F32)
    w2t = p.din("w2t", [D, 4096], F32)
    h1t = p.din("h1t", [4096, 4096], F32)
    h2t = p.din("h2t", [4096, 4096], F32)
    w3t = p.din("w3t", [4096, D], F32)
    sc_in = p.din("sc_in", [128, 4 * 32 + 8], F32)
    o_g0T = p.dout("o_g0T", [D, NTOK], F32)

    sy, te, ve, sc, gp = p.sy, p.te, p.ve, p.sc, p.gp

    xnT = p.sb("xnT", [128, FCH * NTOK], BF16)
    scs = p.sb("scs", [128, 4 * 32 + 8], F32)
    bufA = p.sb("bufA", [128, 32 * 1024], BF16)
    bufB = p.sb("bufB", [128, 32 * 1024], BF16)
    t1b = p.sb("t1b", [128, 512], BF16)
    t2b = p.sb("t2b", [128, 512], BF16)
    wta = p.sb("wta", [128, 32 * 128], BF16)
    wtb = p.sb("wtb", [128, 32 * 128], BF16)
    stg = p.sb("stg", [128, 2 * 512], F32)

    pA1 = p.psum("pA1", [128, 512], F32)
    pA2 = p.psum("pA2", [128, 512], F32)
    pB1 = p.psum("pB1", [128, 512], F32)
    pB2 = p.psum("pB2", [128, 512], F32)

    sy.dma_start(out=dap(xnT, 0, [[FCH * NTOK, 128], [NTOK, FCH], [1, NTOK]]),
                 in_=dap(xnT_own, 0, [[NTOK, 128], [128 * NTOK, FCH], [1, NTOK]])
                 ).inc("ld", 16)
    sy.dma_start(out=scs[:], in_=sc_in.ap()).inc("ld", 16)
    SC = {"s1": 0, "s2": 32, "sh1": 64, "sh2": 96, "s3": 128}

    def scl(nm, m):
        return scs[:, SC[nm] + m:SC[nm] + m + 1]

    TT2 = 2
    for tp in range(TT2):
        tok0 = tp * 1024
        xn_t = lambda kc, sub: xnT[:, kc * NTOK + tok0 + sub * 512:
                                   kc * NTOK + tok0 + sub * 512 + 512]
        mh_base = p.val("mh")
        for m in range(32):
            wt = wta if m % 2 == 0 else wtb
            if m >= 2:
                gp.wait("mh", mh_base + 4 * (m - 1))
            elif p.val("m3") > 0:
                gp.wait("m3", p.val("m3"))
            lsem = "lwfa" if m % 2 == 0 else "lwfb"
            gp.dma_start(out=dap(wt, 0, [[32 * 128, 128], [128, FCH], [1, 128]]),
                         in_=dap(w1t, m * 128, [[4096, 128], [128 * 4096, FCH],
                                                [1, 128]])).inc(lsem, 16)
            gp.dma_start(out=dap(wt, FCH * 128, [[32 * 128, 128], [128, FCH], [1, 128]]),
                         in_=dap(w2t, m * 128, [[4096, 128], [128 * 4096, FCH],
                                                [1, 128]])).inc(lsem, 16)
            te.wait(lsem, p.val(lsem))
            for sub in range(2):
                pA = pA1 if sub == 0 else pA2
                pB = pB1 if sub == 0 else pB2
                if p.val("he") >= 2:
                    te.wait("he", p.val("he") - 1)
                for kc in range(FCH):
                    ins = te.matmul(pA[:], wt[:, kc * 128:(kc + 1) * 128],
                                    xn_t(kc, sub),
                                    start=(kc == 0), stop=(kc == FCH - 1))
                ins.inc("mh")
                for kc in range(FCH):
                    ins = te.matmul(pB[:], wt[:, (FCH + kc) * 128:(FCH + kc + 1) * 128],
                                    xn_t(kc, sub), start=(kc == 0), stop=(kc == FCH - 1))
                ins.inc("mh")
                sc.wait("mh", p.val("mh"))
                sc.activation(t1b[:], pA[:], AF.Silu, scale=scl("s1", m))
                sc.activation(t2b[:], pB[:], AF.Copy, scale=scl("s2", m))
                sc.drain().inc("he")
                ve.wait("he", p.val("he"))
                ve.tensor_tensor(bufA[:, m * 1024 + sub * 512:m * 1024 + sub * 512 + 512],
                                 t1b[:], t2b[:], ALU.mult)
                ve.drain().inc("hv")
                te.wait("hv", p.val("hv"))
        for stage, (wsrc, snm, src_buf, dst_buf) in enumerate(
                [(h1t, "sh1", bufA, bufB), (h2t, "sh2", bufB, bufA)]):
            mg_base = p.val("mg")
            for m in range(32):
                wt = wta if m % 2 == 0 else wtb
                if m >= 2:
                    gp.wait("mg", mg_base + 2 * (m - 1))
                elif stage == 0:
                    gp.wait("mh", p.val("mh"))
                else:
                    gp.wait("mg", mg_base)
                lsem = "lwfa" if m % 2 == 0 else "lwfb"
                gp.dma_start(out=dap(wt, 0, [[32 * 128, 128], [128, 32], [1, 128]]),
                             in_=dap(wsrc, m * 128, [[4096, 128], [128 * 4096, 32],
                                                     [1, 128]])).inc(lsem, 16)
                te.wait(lsem, p.val(lsem))
                for sub in range(2):
                    pA = pA1 if sub == 0 else pA2
                    if p.val("g1e") >= 2:
                        te.wait("g1e", p.val("g1e") - 1)
                    for kc in range(32):
                        ins = te.matmul(pA[:], wt[:, kc * 128:(kc + 1) * 128],
                                        src_buf[:, kc * 1024 + sub * 512:
                                                kc * 1024 + sub * 512 + 512],
                                        start=(kc == 0), stop=(kc == 31))
                    ins.inc("mg")
                    sc.wait("mg", p.val("mg"))
                    sc.activation(dst_buf[:, m * 1024 + sub * 512:
                                          m * 1024 + sub * 512 + 512], pA[:], AF.Gelu,
                                  scale=scl(snm, m))
                    sc.drain().inc("g1e")
            te.wait("g1e", p.val("g1e"))
        m3_base = p.val("m3")
        for m in range(FCH):
            wt = wta if m % 2 == 0 else wtb
            if m >= 2:
                gp.wait("m3", m3_base + 2 * (m - 1))
            else:
                gp.wait("mg", p.val("mg"))
            lsem = "lwfa" if m % 2 == 0 else "lwfb"
            gp.dma_start(out=dap(wt, 0, [[32 * 128, 128], [128, 32], [1, 128]]),
                         in_=dap(w3t, m * 128, [[1024, 128], [128 * 1024, 32],
                                                [1, 128]])).inc(lsem, 16)
            te.wait(lsem, p.val(lsem))
            for sub in range(2):
                pA = pA1 if sub == 0 else pA2
                if p.val("g3e") >= 2:
                    te.wait("g3e", p.val("g3e") - 1)
                for kc in range(32):
                    ins = te.matmul(pA[:], wt[:, kc * 128:(kc + 1) * 128],
                                    bufA[:, kc * 1024 + sub * 512:
                                         kc * 1024 + sub * 512 + 512],
                                    start=(kc == 0), stop=(kc == 31))
                ins.inc("m3")
                sc.wait("m3", p.val("m3"))
                buf = sub
                sc.wait("so", p.val("so"))
                sc.activation(stg[:, buf * 512:(buf + 1) * 512], pA[:], AF.Silu,
                              scale=scl("s3", m))
                sc.drain().inc("g3e")
                sy.wait("g3e", p.val("g3e"))
                sy.dma_start(out=dap(o_g0T, m * 128 * NTOK + tok0 + sub * 512,
                                     [[NTOK, 128], [1, 512]]),
                             in_=stg[:, buf * 512:(buf + 1) * 512]).inc("so", 16)
        te.wait("g3e", p.val("g3e"))
        ve.wait("g3e", p.val("g3e"))

    sy.wait("so", p.val("so"))
    p.finish()
    return p


def l0ffn_inmaps(inputs, prep_outs):
    g = lambda n: _f32(inputs[n])
    tr = lambda a: np.ascontiguousarray(a.T)
    scpack = np.zeros((128, 4 * 32 + 8), np.float32)
    for i, nm in enumerate(["w1_s", "w2_s", "h1_s", "h2_s"]):
        scpack[:, i * 32:(i + 1) * 32] = g("L0_ffn_" + nm).reshape(32, 128).T
    scpack[:, 128:136] = g("L0_ffn_w3_s").reshape(8, 128).T
    shared = dict(
        w1t=tr(g("L0_ffn_w1_w")), w2t=tr(g("L0_ffn_w2_w")),
        h1t=tr(g("L0_ffn_h1_w")), h2t=tr(g("L0_ffn_h2_w")),
        w3t=tr(g("L0_ffn_w3_w")), sc_in=scpack,
    )
    maps = []
    for c in range(NCORES):
        m = dict(shared)
        m["xnT_own"] = prep_outs["xnT"][c]
        maps.append(m)
    return maps



def build_resid1():
    p = Prog()
    x_own = p.din("x_own", [NTOK, D], F32)
    s0T_own = p.din("s0T_own", [D, NTOK], F32)
    g0T_own = p.din("g0T_own", [D, NTOK], F32)
    wattn = p.din("wattn", [D, 3 * D], F32)
    battn = p.din("battn", [128, 24], F32)
    iden = p.din("iden", [128, 128], F32)
    o_x1 = p.dout("o_x1", [NTOK, D], F32)
    o_xn1T = p.dout("o_xn1T", [D, NTOK], BF16)
    o_qT = p.dout("o_qT", [D, NTOK], BF16)
    o_kT = p.dout("o_kT", [D, NTOK], BF16)
    o_vtm = p.dout("o_vtm", [16, NTOK, 64], BF16)

    sy, te, ve, sc, gp = p.sy, p.te, p.ve, p.sc, p.gp
    p.use_consts([1e-5])
    sc.wait("lc", 16)
    ve.wait("lc", 16)

    identf = p.sb("identf", [128, 128], F32)
    identb = p.sb("identb", [128, 128], BF16)
    stat = p.sb("stat", [128, 8 * NTT], F32)
    waste = p.sb("waste", [128, D], F32)
    xn1_tm = p.sb("xn1_tm", [128, NTT * D], BF16)
    battb = p.sb("battb", [128, 24], F32)

    pf1 = p.psum("pf1", [128, 512], F32)
    pf2 = p.psum("pf2", [128, 512], F32)
    pbb = p.psum("pbb", [128, 1024], BF16)
    pbb2 = p.psum("pbb2", [128, 1024], BF16)
    pA1 = p.psum("pA1", [128, 1024], F32)
    pA2 = p.psum("pA2", [128, 1024], F32)

    x_tm = p.sb_tmp("x_tm", [128, NTT * D], F32)
    prod_tm = p.sb_tmp("prod_tm", [128, NTT * D], F32)
    sgch = p.sb_tmp("sgch", [128, 4 * NTOK], F32)

    sy.dma_start(out=dap(x_tm, 0, [[NTT * D, 128], [D, NTT], [1, D]]),
                 in_=dap(x_own, 0, [[D, 128], [128 * D, NTT], [1, D]])).inc("ld", 16)
    sy.dma_start(out=identf[:], in_=iden.ap()).inc("ld", 16)
    sy.dma_start(out=battb[:], in_=battn.ap()).inc("ld", 16)
    gp.dma_start(out=identb[:], in_=iden.ap()).inc("lw", 16)

    for fc in range(FCH):
        buf = fc % 2
        if fc >= 2:
            sy.wait("ptv", p.val("ptv") - 4)
        lsem = "lsga" if buf == 0 else "lsgb"
        sy.dma_start(out=sgch[:, buf * 2 * NTOK:buf * 2 * NTOK + NTOK],
                     in_=dap(s0T_own, fc * 128 * NTOK, [[NTOK, 128], [1, NTOK]])
                     ).inc(lsem, 16)
        sy.dma_start(out=sgch[:, buf * 2 * NTOK + NTOK:buf * 2 * NTOK + 2 * NTOK],
                     in_=dap(g0T_own, fc * 128 * NTOK, [[NTOK, 128], [1, NTOK]])
                     ).inc(lsem, 16)
        ve.wait(lsem, p.val(lsem))
        ve.tensor_tensor(sgch[:, buf * 2 * NTOK:buf * 2 * NTOK + NTOK],
                         sgch[:, buf * 2 * NTOK:buf * 2 * NTOK + NTOK],
                         sgch[:, buf * 2 * NTOK + NTOK:buf * 2 * NTOK + 2 * NTOK],
                         ALU.mult)
        ve.drain().inc("pv")
        te.wait("pv", p.val("pv"))
        for g in range(4):
            pf = pf1 if g % 2 == 0 else pf2
            if fc > 0 or g >= 2:
                te.wait("ptv", p.val("ptv") - 1)
            lastt = None
            for i in range(4):
                tt = g * 4 + i
                lastt = te.transpose(pf[:, i * 128:(i + 1) * 128],
                                     sgch[:, buf * 2 * NTOK + tt * 128:
                                          buf * 2 * NTOK + (tt + 1) * 128],
                                     identf[:])
            lastt.inc("mmp")
            ve.wait("mmp", p.val("mmp"))
            for i in range(4):
                tt = g * 4 + i
                ve.tensor_copy(prod_tm[:, tt * D + fc * 128:tt * D + (fc + 1) * 128],
                               pf[:, i * 128:(i + 1) * 128])
            ve.drain().inc("ptv")

    ve.wait("ld", 48)
    for tt in range(NTT):
        ve.tensor_tensor(prod_tm[:, tt * D:(tt + 1) * D], prod_tm[:, tt * D:(tt + 1) * D],
                         x_tm[:, tt * D:(tt + 1) * D], ALU.add)
    ve.drain().inc("x1v")
    sy.wait("x1v", p.val("x1v"))
    for q4 in range(4):
        sy.dma_start(out=dap(o_x1, q4 * 4 * 128 * D, [[D, 128], [128 * D, 4], [1, D]]),
                     in_=dap(prod_tm, q4 * 4 * D, [[NTT * D, 128], [D, 4], [1, D]])
                     ).inc("sx1", 16)
    x1_so = p.val("sx1")

    gate_xn1 = emit_ln(p, prod_tm, xn1_tm, stat, waste, NTT, D, p.cv(1e-5),
                       gate=("x1v", p.val("x1v")))

    for nm in ["sgch", "prod_tm", "x_tm"]:
        p.sb_free(nm)
    xn1T = p.sb_tmp("xn1T", [128, FCH * NTOK], BF16)
    qkvd = p.sb_tmp("qkvd", [128, 16 * NTOK], BF16)
    v_tm = p.sb_tmp("v_tm", [128, NTT * D], BF16)
    wta = p.sb_tmp("wta", [128, FCH * 128], BF16)
    wtb = p.sb_tmp("wtb", [128, FCH * 128], BF16)
    vstg = p.sb_tmp("vstg", [128, NTOK], BF16)

    gp.wait("lnv", p.val("lnv"))
    gp.wait("sx1", x1_so)
    groups = []
    for fc in range(FCH):
        for g in range(2):
            srcs = [(xn1_tm[:, (g * 8 + i) * D + fc * 128:(g * 8 + i) * D + (fc + 1) * 128], 128)
                    for i in range(8)]
            dst = xn1T[:, fc * NTOK + g * 1024: fc * NTOK + (g + 1) * 1024]
            groups.append((srcs, dst, 1024))
    te.wait("lw", 16)
    gate_x1T = emit_transpose(p, groups, identb, pbb, pbb2, gate_te=gate_xn1)
    sy.wait(*gate_x1T)
    sy.dma_start(out=dap(o_xn1T, 0, [[NTOK, 128], [NTOK * 128, FCH], [1, NTOK]]),
                 in_=dap(xn1T, 0, [[FCH * NTOK, 128], [NTOK, FCH], [1, NTOK]])).inc("so", 16)

    mq_base = p.val("mq")
    for m in range(24):
        wt = wta if m % 2 == 0 else wtb
        if m >= 2:
            gp.wait("mq", mq_base + 2 * (m - 1))
        lsem = "lwfa" if m % 2 == 0 else "lwfb"
        gp.dma_start(out=dap(wt, 0, [[FCH * 128, 128], [128, FCH], [1, 128]]),
                     in_=dap(wattn, m * 128, [[3 * D, 128], [128 * 3 * D, FCH], [1, 128]])
                     ).inc(lsem, 16)
        te.wait(lsem, p.val(lsem))
        te.wait(*gate_x1T)
        pA = pA1 if m % 2 == 0 else pA2
        te.wait("qe", p.val("qe"))
        lastm = None
        for g in range(2):
            for kc in range(FCH):
                lastm = te.matmul(pA[:, g * 512:(g + 1) * 512],
                                  wt[:, kc * 128:(kc + 1) * 128],
                                  xn1T[:, kc * NTOK + g * 512: kc * NTOK + (g + 1) * 512],
                                  start=(kc == 0), stop=(kc == FCH - 1))
        lastm.inc("mq")
        pAo = pA2 if m % 2 == 0 else pA1
        for g in range(2):
            for kc in range(FCH):
                lastm = te.matmul(pAo[:, g * 512:(g + 1) * 512],
                                  wt[:, kc * 128:(kc + 1) * 128],
                                  xn1T[:, kc * NTOK + 1024 + g * 512:
                                       kc * NTOK + 1024 + (g + 1) * 512],
                                  start=(kc == 0), stop=(kc == FCH - 1))
        lastm.inc("mq")
        sc.wait("mq", p.val("mq"))
        if m < 16:
            dst = qkvd[:, m * NTOK:(m + 1) * NTOK]
            sc.activation(dst[:, 0:1024], pA[:], AF.Identity, bias=battb[:, m:m + 1])
            sc.activation(dst[:, 1024:2048], pAo[:], AF.Identity, bias=battb[:, m:m + 1])
            sc.drain().inc("qe")
        else:
            mv = m - 16
            sc.activation(waste[:, 0:1024].bitcast(BF16) if False else
                          qkvd[:, 0:1024], pA[:], AF.Identity, bias=battb[:, m:m + 1])                 if False else None
            sc.activation(vstg[:, 0:1024], pA[:], AF.Identity, bias=battb[:, m:m + 1])
            sc.activation(vstg[:, 1024:2048], (pA2 if m % 2 == 0 else pA1)[:],
                          AF.Identity, bias=battb[:, m:m + 1])
            sc.drain().inc("qe")
            te.wait("qe", p.val("qe"))
            for g in range(4):
                if p.val("vtv") >= 1 or g >= 1:
                    te.wait("vtv", p.val("vtv"))
                lastt = None
                for i in range(4):
                    tt = g * 4 + i
                    lastt = te.transpose(pbb[:, i * 128:(i + 1) * 128],
                                         vstg[:, tt * 128:(tt + 1) * 128], identb[:])
                lastt.inc("mmv")
                ve.wait("mmv", p.val("mmv"))
                for i in range(4):
                    tt = g * 4 + i
                    ve.tensor_copy(v_tm[:, tt * D + mv * 128:tt * D + (mv + 1) * 128],
                                   pbb[:, i * 128:(i + 1) * 128])
                ve.drain().inc("vtv")

    sy.wait("qe", p.val("qe"))
    sy.dma_start(out=dap(o_qT, 0, [[NTOK, 128], [NTOK * 128, FCH], [1, NTOK]]),
                 in_=dap(qkvd, 0, [[16 * NTOK, 128], [NTOK, FCH], [1, NTOK]])).inc("so", 16)
    sy.dma_start(out=dap(o_kT, 0, [[NTOK, 128], [NTOK * 128, FCH], [1, NTOK]]),
                 in_=dap(qkvd, FCH * NTOK, [[16 * NTOK, 128], [NTOK, FCH], [1, NTOK]])
                 ).inc("so", 16)
    sy.wait("vtv", p.val("vtv"))
    for h in range(16):
        sy.dma_start(out=dap(o_vtm, h * NTOK * 64, [[64, 128], [128 * 64, NTT], [1, 64]]),
                     in_=dap(v_tm, (h // 2) * 128 + (h % 2) * 64,
                             [[NTT * D, 128], [D, NTT], [1, 64]])).inc("so", 16)
    sy.wait("so", p.val("so"))
    sy.wait("sx1", p.val("sx1"))
    p.finish()
    return p


def resid1_inmaps(inputs, s0T_list, g0T_list):
    g = lambda n: _f32(inputs[n])
    x = _f32(inputs["L0_attn_x"]).reshape(GT, D)
    batt = np.ascontiguousarray(g("L1_attn_b_attn").reshape(24, 128).T)
    shared = dict(
        wattn=g("L1_attn_W_attn"), battn=batt,
        iden=np.eye(128, dtype=np.float32),
        cvec=np.full((128, 1), 1e-5, dtype=np.float32),
    )
    maps = []
    for c in range(NCORES):
        m = dict(shared)
        m["x_own"] = np.ascontiguousarray(x[c * NTOK:(c + 1) * NTOK])
        m["s0T_own"] = s0T_list[c]
        m["g0T_own"] = g0T_list[c]
        maps.append(m)
    return maps



def build_l1attn(hp0, hp1, do_proj):
    p = Prog()
    qTs = p.din("qTs", [D, NTOK], BF16)
    kTb = p.din("kTb", [D, T], BF16)
    vtmb = p.din("vtmb", [16, T, 64], BF16)
    mask_in = p.din("mask_in", [2, 128, 512], F32)
    gain_col = p.din("gain_col", [128, 1], F32)
    iden = p.din("iden", [128, 128], F32)
    o_yT = p.dout("o_yT", [D, NTOK], BF16)
    if do_proj:
        if hp1 - hp0 < 8:
            yT_other = p.din("yT_other", [D, NTOK], BF16)
        wproj = p.din("wproj", [D, D], F32)
        bproj = p.din("bproj", [128, FCH], F32)
        o_s1T = p.dout("o_s1T", [D, NTOK], F32)

    sy, te, ve, sc, gp = p.sy, p.te, p.ve, p.sc, p.gp
    p.use_consts([1e-6])
    sc.wait("lc", 16)
    ve.wait("lc", 16)

    NP = hp1 - hp0
    NIB = 32
    qsb = p.sb("qsb", [128, FCH * NTOK], BF16)
    identb = p.sb("identb", [128, 128], BF16)
    maskb = p.sb("maskb", [128, 2 * 512], BF16)
    gainb = p.sb("gainb", [128, 1], F32)
    kpA = p.sb("kpA", [128, T], BF16)
    kpB = p.sb("kpB", [128, T], BF16)
    vaug = p.sb("vaug", [128, 64 * 130], BF16)
    yA_tm = p.sb("yA_tm", [128, NTT * 64], BF16)
    yB_tm = p.sb("yB_tm", [128, NTT * 64], BF16)
    ysb = p.sb("ysb", [128, 8 * NTOK], BF16)
    st = p.sb("st", [128, 16], F32)
    s_sb = p.sb_tmp("s_sb", [128, T], BF16)
    wb1 = p.sb_tmp("wb1", [128, T], BF16)
    u_sb = p.sb_tmp("u_sb", [128, T], BF16)
    rT_sb = p.sb_tmp("rT_sb", [128, T], BF16)
    waste = p.sb_tmp("waste", [128, T], BF16)

    pS1 = p.psum("pS1", [128, 1024], F32)
    pS2 = p.psum("pS2", [128, 1024], F32)
    pb = p.psum("pb", [128, 1024], BF16)
    pav = p.psum("pav", [128, 136], F32)
    pyt = p.psum("pyt", [128, 256], BF16)

    sy.dma_start(out=dap(qsb, 0, [[FCH * NTOK, 128], [NTOK, FCH], [1, NTOK]]),
                 in_=dap(qTs, 0, [[NTOK, 128], [128 * NTOK, FCH], [1, NTOK]])
                 ).inc("ld", 16)
    gp.dma_start(out=identb[:], in_=iden.ap()).inc("lw", 16)
    gp.dma_start(out=dap(maskb, 0, [[2 * 512, 128], [512, 2], [1, 512]]),
                 in_=dap(mask_in, 0, [[512, 128], [128 * 512, 2], [1, 512]])
                 ).inc("lw", 16)
    sy.dma_start(out=gainb[:], in_=gain_col.ap()).inc("ld", 16)
    ve.memset(vaug[:], 1.0).inc("vo")

    NJT_ = T // 128

    for hpi in range(NP):
        hp = hp0 + hpi
        kp = kpA if hpi % 2 == 0 else kpB
        if hpi >= 2:
            gp.wait("ms1", 8 * NIB * (hpi - 1))
        lsem = "lka" if hpi % 2 == 0 else "lkb"
        gp.dma_start(out=kp[:], in_=dap(kTb, hp * 128 * T, [[T, 128], [1, T]])
                     ).inc(lsem, 16)
        if hpi >= 1:
            gp.wait("mav", p.val("mav"))
        gp.wait("vo", p.val("vo"))
        gp.dma_start(out=dap(vaug, 0, [[64 * 130, 128], [130, 64], [1, 64]]),
                     in_=dap(vtmb, (2 * hp) * T * 64, [[64, 128], [128 * 64, 64], [1, 64]])
                     ).inc("lv", 16)
        gp.dma_start(out=dap(vaug, 65, [[64 * 130, 128], [130, 64], [1, 64]]),
                     in_=dap(vtmb, (2 * hp + 1) * T * 64, [[64, 128], [128 * 64, 64], [1, 64]])
                     ).inc("lv", 16)
        te.wait(lsem, p.val(lsem))
        te.wait("lv", p.val("lv"))
        te.wait("lw", p.val("lw"))
        te.wait("ld", p.val("ld"))
        for ib in range(NIB):
            jd = ib // 2
            nch = jd + 1
            qA = qsb[0:64, hp * NTOK + ib * 64: hp * NTOK + ib * 64 + 64]
            qB = qsb[64:128, hp * NTOK + ib * 64: hp * NTOK + ib * 64 + 64]
            for cg in range(8):
                pS = pS1 if cg % 2 == 0 else pS2
                if p.val("sse") >= 2:
                    te.wait("sse", p.val("sse") - 1)
                lastm = None
                for half in range(2):
                    c0 = cg * 1024 + half * 512
                    te.matmul(pS[0:64, half * 512:(half + 1) * 512], qA,
                              kp[0:64, c0:c0 + 512], start=True, stop=True,
                              tile_position=(0, 0))
                    lastm = te.matmul(pS[64:128, half * 512:(half + 1) * 512], qB,
                                      kp[64:128, c0:c0 + 512], start=True, stop=True,
                                      tile_position=(64, 64))
                lastm.inc("ms1")
                sc.wait("ms1", p.val("ms1"))
                sc.activation(s_sb[:, cg * 1024:(cg + 1) * 1024], pS[:], AF.Identity,
                              accum_out=st[:, cg:cg + 1])
                sc.drain().inc("sse")
            ve.wait("sse", p.val("sse"))
            ve.tensor_reduce(st[:, 8:9], st[:, 0:8], AX.X, ALU.add)
            ve.drain()
            ve.tensor_scalar(st[:, 9:10], st[:, 8:9], 1.0 / T, None, ALU.mult)
            ve.drain()
            ve.tensor_scalar(st[:, 10:11], st[:, 9:10], -1.0, None, ALU.mult)
            ve.drain().inc("muv")
            sc.wait("muv", p.val("muv"))
            sc.activation(waste[:, 0:T], s_sb[:, 0:T], AF.Abs, bias=st[:, 10:11],
                          accum_out=st[:, 11:12])
            sc.drain().inc("mad")
            ve.wait("mad", p.val("mad"))
            ve.tensor_scalar(st[:, 12:13], st[:, 11:12], 1.0 / T, p.cv(1e-6),
                             ALU.mult, ALU.add)
            ve.drain()
            ve.reciprocal(st[:, 13:14], st[:, 12:13])
            ve.drain()
            ve.tensor_tensor(st[:, 14:15], st[:, 13:14], gainb[:], ALU.mult)
            ve.drain()
            ve.tensor_tensor(st[:, 15:16], st[:, 10:11], st[:, 14:15], ALU.mult)
            ve.drain().inc("cv")
            sc.wait("cv", p.val("cv"))
            sc.wait("mmt", p.val("mmt"))
            sc.activation(u_sb[:, 0:nch * 512], s_sb[:, 0:nch * 512], AF.Relu,
                          scale=st[:, 14:15], bias=st[:, 15:16])
            sc.drain().inc("uv")
            ve.wait("uv", p.val("uv"))
            ve.tensor_tensor(wb1[:, 0:nch * 512], u_sb[:, 0:nch * 512],
                             u_sb[:, 0:nch * 512], ALU.mult)
            ve.drain()
            ve.tensor_tensor(u_sb[:, 0:nch * 512], wb1[:, 0:nch * 512],
                             wb1[:, 0:nch * 512], ALU.mult)
            ve.drain()
            ve.tensor_tensor(u_sb[:, jd * 512:(jd + 1) * 512],
                             u_sb[:, jd * 512:(jd + 1) * 512],
                             maskb[:, (ib % 2) * 512:(ib % 2 + 1) * 512], ALU.mult)
            ve.drain().inc("rv")
            te.wait("rv", p.val("rv"))
            ntile = nch * 4
            for g in range((ntile + 7) // 8):
                te.wait("rtv", p.val("rtv"))
                lastt = None
                cnt = min(8, ntile - g * 8)
                for i in range(cnt):
                    jt = g * 8 + i
                    lastt = te.transpose(pb[:, i * 128:(i + 1) * 128],
                                         u_sb[:, jt * 128:(jt + 1) * 128], identb[:])
                lastt.inc("mmt")
                ve.wait("mmt", p.val("mmt"))
                ve.tensor_copy(rT_sb[:, g * 1024:g * 1024 + cnt * 128],
                               pb[:, 0:cnt * 128])
                ve.drain().inc("rtv")
            te.wait("rtv", p.val("rtv"))
            te.wait("yv", p.val("yv"))
            lastm = None
            for jt in range(ntile):
                lastm = te.matmul(pav[:, 0:130], rT_sb[:, jt * 128:(jt + 1) * 128],
                                  vaug[:, jt * 130:(jt + 1) * 130],
                                  start=(jt == 0), stop=(jt == ntile - 1))
            lastm.inc("mav")
            ve.wait("mav", p.val("mav"))
            ve.tensor_copy(st[:, 0:1][0:64, :], pav[0:64, 64:65])
            ve.tensor_copy(st[:, 0:1][64:128, :], pav[64:128, 129:130])
            ve.drain()
            ve.tensor_scalar(st[:, 1:2], st[:, 0:1], 1.0, p.cv(1e-6), ALU.mult, ALU.add)
            ve.drain()
            ve.reciprocal(st[:, 2:3], st[:, 1:2])
            ve.drain()
            row = (ib % 2) * 64
            col = (ib // 2) * 64
            ve.tensor_scalar(yA_tm[row:row + 64, col:col + 64], pav[0:64, 0:64],
                             st[0:64, 2:3], None, ALU.mult)
            ve.tensor_scalar(yB_tm[row:row + 64, col:col + 64], pav[64:128, 65:129],
                             st[64:128, 2:3], None, ALU.mult)
            ve.drain().inc("yv")
        te.wait("yv", p.val("yv"))
        for head in range(2):
            ysrc = yA_tm if head == 0 else yB_tm
            for g in range(4):
                te.wait("ytv", p.val("ytv"))
                lastt = None
                for i in range(4):
                    tt = g * 4 + i
                    lastt = te.transpose(pb[0:64, i * 128:(i + 1) * 128],
                                         ysrc[:, tt * 64:(tt + 1) * 64], identb[:])
                lastt.inc("mmy")
                ve.wait("mmy", p.val("mmy"))
                for i in range(4):
                    tt = g * 4 + i
                    base = 0 if head == 0 else 64
                    ve.tensor_copy(
                        ysb[base:base + 64,
                            hp * NTOK + tt * 128:hp * NTOK + (tt + 1) * 128],
                        pb[0:64, i * 128:(i + 1) * 128])
                ve.drain().inc("ytv")
        te.wait("ytv", p.val("ytv"))

    sy.wait("ytv", p.val("ytv"))
    for c4 in range(0, NP, 4):
        nc4 = min(4, NP - c4)
        sy.dma_start(out=dap(o_yT, (hp0 + c4) * 128 * NTOK,
                             [[NTOK, 128], [128 * NTOK, nc4], [1, NTOK]]),
                     in_=dap(ysb, (hp0 + c4) * NTOK,
                             [[8 * NTOK, 128], [NTOK, nc4], [1, NTOK]])
                     ).inc("so", 16)

    if do_proj:
        for nm in ["waste", "rT_sb", "u_sb", "wb1", "s_sb"]:
            p.sb_free(nm)
        gp.wait("ytv", p.val("ytv"))
        gp.wait("uv", p.val("uv"))
        gp.wait("mad", p.val("mad"))
        wpb = p.sb_tmp("wpb", [128, FCH * D], BF16)
        bpb = p.sb_tmp("bpb", [128, FCH], F32)
        s1st = p.sb_tmp("s1st", [128, 2 * NTOK], F32)
        gp.dma_start(out=dap(wpb, 0, [[FCH * D, 128], [D, FCH], [1, D]]),
                     in_=dap(wproj, 0, [[D, 128], [128 * D, FCH], [1, D]])).inc("lp", 16)
        gp.dma_start(out=bpb[:], in_=bproj.ap()).inc("lp", 16)
        if hp1 - hp0 < 8:
            oh0 = 0 if hp0 >= 4 else 4
            gp.dma_start(out=dap(ysb, oh0 * NTOK, [[8 * NTOK, 128], [NTOK, 4], [1, NTOK]]),
                         in_=dap(yT_other, oh0 * 128 * NTOK, [[NTOK, 128], [128 * NTOK, 4],
                                                              [1, NTOK]])).inc("lp", 16)
        te.wait("lp", p.val("lp"))
        for m in range(FCH):
            lastm = None
            for g in range(2):
                pS = pS1 if (m * 2 + g) % 2 == 0 else pS2
                if p.val("s1e") >= 2:
                    te.wait("s1e", p.val("s1e") - 1)
                for half in range(2):
                    n0 = g * 1024 + half * 512
                    for kc in range(FCH):
                        lastm = te.matmul(pS[:, half * 512:(half + 1) * 512],
                                          wpb[:, kc * D + m * 128:kc * D + (m + 1) * 128],
                                          ysb[:, kc * NTOK + n0:kc * NTOK + n0 + 512],
                                          start=(kc == 0), stop=(kc == FCH - 1))
                lastm.inc("ms1p")
                sc.wait("ms1p", p.val("ms1p"))
                buf = m % 2
                sc.wait("so", p.val("so"))
                sc.activation(s1st[:, buf * NTOK + g * 1024:buf * NTOK + (g + 1) * 1024],
                              pS[:], AF.Identity, bias=bpb[:, m:m + 1])
                sc.drain().inc("s1e")
            sy.wait("s1e", p.val("s1e"))
            sy.dma_start(out=dap(o_s1T, m * 128 * NTOK, [[NTOK, 128], [1, NTOK]]),
                         in_=s1st[:, (m % 2) * NTOK:(m % 2 + 1) * NTOK]).inc("so", 16)

    sy.wait("so", p.val("so"))
    p.finish()
    return p


def l1attn_inmaps(inputs, r1, half, yT_other_list=None):
    g = lambda n: _f32(inputs[n])
    gain = float(np.asarray(inputs["L1_attn_score_gain"]))
    kT_full = [np.concatenate([r1["kT"][b * 4 + i] for i in range(4)], axis=1)
               for b in range(2)]
    vtm_full = [np.concatenate([r1["vtm"][b * 4 + i] for i in range(4)], axis=1)
                for b in range(2)]
    qT_full = [np.concatenate([r1["qT"][b * 4 + i] for i in range(4)], axis=1)
               for b in range(2)]
    masks = np.zeros((8, 128, 512), np.float32)
    for v in range(8):
        qoff = v * 64 + (np.arange(128) % 64)
        masks[v] = (np.arange(512)[None, :] <= qoff[:, None]).astype(np.float32)
    maps = []
    for c in range(NCORES):
        b, s = c // 4, c % 4
        strip_cols = np.concatenate([np.arange(64) + (4 * ib + s) * 64
                                     for ib in range(32)])
        m = dict(
            qTs=np.ascontiguousarray(qT_full[b][:, strip_cols]),
            kTb=kT_full[b],
            vtmb=vtm_full[b],
            mask_in=np.ascontiguousarray(masks[[s, 4 + s]]),
            gain_col=np.full((128, 1), gain, np.float32),
            iden=np.eye(128, dtype=np.float32),
            cvec=np.full((128, 1), 1e-6, np.float32),
        )
        if half == 1:
            if yT_other_list is not None:
                m["yT_other"] = yT_other_list[c]
            m["wproj"] = g("L1_attn_W_proj")
            m["bproj"] = np.ascontiguousarray(g("L1_attn_b_proj").reshape(FCH, 128).T)
        maps.append(m)
    return maps



TIN = 3712
NQK = TIN - 128
NQKT = NQK // 128
NXT = 17
XW = NXT * 128

def build_l1ffn(final, upto=None):
    p = Prog()
    xin = p.din("xin", [D, TIN], BF16)
    wqkv = p.din("wqkv", [D, 1536], F32)
    wo_in = p.din("wo_in", [512, D], F32)
    w1a_in = p.din("w1a_in", [D, 2730], F32)
    w1g_in = p.din("w1g_in", [D, 2730], F32)
    w2_in = p.din("w2_in", [2816, D], F32)
    b1a_in = p.din("b1a_in", [128, 22], F32)
    b1g_in = p.din("b1g_in", [128, 22], F32)
    b2_in = p.din("b2_in", [128, FCH], F32)
    lg_in = p.din("lg_in", [128, FCH], F32)
    lb_in = p.din("lb_in", [128, FCH], F32)
    qsc_in = p.din("qsc_in", [1, 512], F32)
    ksc_in = p.din("ksc_in", [1, 512], F32)
    csq_in = p.din("csq_in", [NQK, 64], F32)
    ssq_in = p.din("ssq_in", [NQK, 64], F32)
    csk_in = p.din("csk_in", [NQK, 64], F32)
    ssk_in = p.din("ssk_in", [NQK, 64], F32)
    masks_in = p.din("masks_in", [128, 1408], F32)
    ksel_in = p.din("ksel_in", [128, 8], F32)
    ones_in = p.din("ones_in", [1, 128], F32)
    iden = p.din("iden", [128, 128], F32)
    if final:
        x1_own = p.din("x1_own", [NTOK, D], F32)
        s1T_own = p.din("s1T_own", [D, NTOK], F32)
        o_out = p.dout("o_out", [NTOK, D], F32)
    else:
        o_xi = p.dout("o_xi", [D, NTOK], BF16)

    sy, te, ve, sc, gp = p.sy, p.te, p.ve, p.sc, p.gp
    p.use_consts([1e-12, 1e-5])
    sc.wait("lc", 16)
    ve.wait("lc", 16)

    identb = p.sb("identb", [128, 128], BF16)
    identf = p.sb("identf", [128, 128], F32)
    onesb = p.sb("onesb", [1, 128], BF16)
    x2T = p.sb("x2T", [128, FCH * XW], BF16)
    lgb = p.sb("lgb", [128, FCH], F32)
    lbb = p.sb("lbb", [128, FCH], F32)
    b2b = p.sb("b2b", [128, FCH], F32)
    kselb = p.sb("kselb", [128, 8], F32)
    st = p.sb("st", [128, 32], F32)

    xinb = p.sb_tmp("xinb", [128, FCH * TIN], BF16)
    kT = p.sb_tmp("kT", [128, 4 * NQK], BF16)
    qT = p.sb_tmp("qT", [128, 4 * XW], BF16)
    v_tm = p.sb_tmp("v_tm", [128, NQKT * 512], BF16)
    att_sb = p.sb_tmp("att_sb", [128, 1024], BF16)
    tabs = p.sb_tmp("tabs", [128, 4 * NQKT * 64], BF16)
    qscB = p.sb_tmp("qscB", [128, 2 * 512], BF16)
    wqb = p.sb_tmp("wqb", [128, FCH * 128], BF16)
    wqb2 = p.sb_tmp("wqb2", [128, FCH * 128], BF16)
    tmw = p.sb_tmp("tmw", [128, 2 * 512], BF16)
    tmw2 = p.sb_tmp("tmw2", [128, 2 * 512], BF16)

    pA1 = p.psum("pA1", [128, 1024], F32)
    pA2 = p.psum("pA2", [128, 1024], F32)
    pb = p.psum("pb", [128, 1024], BF16)
    pf32 = p.psum("pf32", [128, 512], F32)
    pb2 = p.psum("pb2", [128, 1024], BF16)
    pav = p.psum("pav", [128, 128], F32)

    for q4 in range(4):
        sy.dma_start(out=dap(xinb, q4 * 2 * TIN, [[FCH * TIN, 128], [TIN, 2], [1, TIN]]),
                     in_=dap(xin, q4 * 2 * 128 * TIN, [[TIN, 128], [128 * TIN, 2], [1, TIN]])
                     ).inc("ld", 16)
    sy.dma_start(out=identf[:], in_=iden.ap()).inc("ld", 16)
    sy.dma_start(out=lgb[:], in_=lg_in.ap()).inc("ld", 16)
    sy.dma_start(out=lbb[:], in_=lb_in.ap()).inc("ld", 16)
    sy.dma_start(out=b2b[:], in_=b2_in.ap()).inc("ld", 16)
    sy.dma_start(out=kselb[:], in_=ksel_in.ap()).inc("ld", 16)
    gp.dma_start(out=identb[:], in_=iden.ap()).inc("lw", 16)
    gp.dma_start(out=onesb[:], in_=ones_in.ap()).inc("lw", 16)
    for i, tab in enumerate([csq_in, ssq_in, csk_in, ssk_in]):
        gp.dma_start(out=dap(tabs, i * NQKT * 64, [[4 * NQKT * 64, 128], [64, NQKT], [1, 64]]),
                     in_=dap(tab, 0, [[64, 128], [128 * 64, NQKT], [1, 64]])).inc("lw", 16)
    gp.dma_start(out=qscB[0:1, 0:512], in_=qsc_in.ap()).inc("lw", 16)
    gp.dma_start(out=qscB[0:1, 512:1024], in_=ksc_in.ap()).inc("lw", 16)

    def tab_rep(i, tt):
        return dap(tabs, i * NQKT * 64 + tt * 64,
                   [[4 * NQKT * 64, 128], [0, 8], [1, 64]])

    te.wait("lw", p.val("lw"))
    te.matmul(pf32[:, 0:512], onesb[0:1, :], qscB[0:1, 0:512],
              start=True, stop=True).inc("mbc")
    ve.wait("mbc", p.val("mbc"))
    ve.tensor_copy(qscB[:, 0:512], pf32[:, 0:512])
    ve.drain().inc("bcv")
    te.wait("bcv", p.val("bcv"))
    te.matmul(pf32[:, 0:512], onesb[0:1, :], qscB[0:1, 512:1024],
              start=True, stop=True).inc("mbc")
    ve.wait("mbc", p.val("mbc"))
    ve.tensor_copy(qscB[:, 512:1024], pf32[:, 0:512])
    ve.drain().inc("bcv")

    te.wait("ld", p.val("ld"))
    te.wait("bcv", p.val("bcv"))
    NCH7 = NQK // 512
    mq_base = p.val("mqk")
    for m in range(12):
        wt = wqb if m % 2 == 0 else wqb2
        if m >= 2:
            gp.wait("mqk", mq_base + (m - 1) * NCH7)
        lsem = "lwfa" if m % 2 == 0 else "lwfb"
        gp.dma_start(out=dap(wt, 0, [[FCH * 128, 128], [128, FCH], [1, 128]]),
                     in_=dap(wqkv, m * 128, [[1536, 128], [128 * 1536, FCH], [1, 128]])
                     ).inc(lsem, 16)
        te.wait(lsem, p.val(lsem))
        for n in range(NCH7):
            pA = pA1 if (m * NCH7 + n) % 2 == 0 else pA2
            if p.val("qke") >= 2:
                te.wait("qke", p.val("qke") - 1)
            lastm = None
            for kc in range(FCH):
                shiftcol = 0 if kc < 4 else -1
                lastm = te.matmul(pA[:, 0:512], wt[:, kc * 128:(kc + 1) * 128],
                                  xinb[:, kc * TIN + 128 + n * 512 + shiftcol:
                                       kc * TIN + 128 + n * 512 + shiftcol + 512],
                                  start=(kc == 0), stop=(kc == FCH - 1))
            lastm.inc("mqk")
            sc.wait("mqk", p.val("mqk"))
            c0 = 128 + n * 512
            if m < 4:
                lo = max(c0, 1536)
                if lo < c0 + 512:
                    off = lo - c0
                    sc.activation(qT[:, m * XW + (lo - 1536):m * XW + (lo - 1536) + 512 - off],
                                  pA[:, off:512], AF.Copy)
                sc.drain().inc("qke")
            elif m < 8:
                sc.activation(kT[:, (m - 4) * NQK + n * 512:(m - 4) * NQK + (n + 1) * 512],
                              pA[:, 0:512], AF.Copy)
                sc.drain().inc("qke")
            else:
                sc.activation(att_sb[:, 0:512], pA[:, 0:512], AF.Copy)
                sc.drain().inc("qke")
                te.wait("qke", p.val("qke"))
                te.wait("vtv", p.val("vtv"))
                lastt = None
                for i in range(4):
                    lastt = te.transpose(pb[:, i * 128:(i + 1) * 128],
                                         att_sb[:, i * 128:(i + 1) * 128], identb[:])
                lastt.inc("mmv")
                ve.wait("mmv", p.val("mmv"))
                for i in range(4):
                    tt = n * 4 + i
                    ve.tensor_copy(v_tm[:, tt * 512 + (m - 8) * 128:
                                        tt * 512 + (m - 7) * 128],
                                   pb[:, i * 128:(i + 1) * 128])
                ve.drain().inc("vtv")

    if upto == "qkv":
        ve.drain().inc("fin")
        sy.wait("fin", p.val("fin"))
        p.finish()
        return p

    def rot_block(kind):
        ntile = NXT if kind == 0 else NQKT
        srcbuf = qT if kind == 0 else kT
        width = XW if kind == 0 else NQK
        tab_cs = 0 if kind == 0 else 2
        tab_off = 11 if kind == 0 else 0
        scB = qscB[:, 0:512] if kind == 0 else qscB[:, 512:1024]
        for tt in range(ntile):
            te.wait("rtw", p.val("rtw"))
            lastt = None
            for fc in range(4):
                lastt = te.transpose(pb[:, fc * 128:(fc + 1) * 128],
                                     srcbuf[:, fc * width + tt * 128:
                                            fc * width + (tt + 1) * 128], identb[:])
            lastt.inc("mmr")
            ve.wait("mmr", p.val("mmr"))
            ve.tensor_copy(tmw[:, 0:512], pb[:, 0:512])
            ve.drain()
            ve.tensor_tensor(tmw2[:, 0:512], tmw[:, 0:512], tmw[:, 0:512], ALU.mult)
            ve.drain()
            ve.tensor_reduce(st[:, 0:8],
                             dap(tmw2, 0, [[2 * 512, 128], [64, 8], [1, 64]]),
                             AX.X, ALU.add)
            ve.drain().inc("nsv")
            sc.wait("nsv", p.val("nsv"))
            sc.activation(st[:, 8:16], st[:, 0:8], AF.Sqrt)
            sc.drain().inc("nss")
            ve.wait("nss", p.val("nss"))
            ve.tensor_scalar(st[:, 8:16], st[:, 8:16], p.cv(1e-12), None, ALU.max)
            ve.drain()
            ve.reciprocal(st[:, 16:24], st[:, 8:16])
            ve.drain()
            for h in range(8):
                ve.tensor_scalar(tmw[:, h * 64:(h + 1) * 64], tmw[:, h * 64:(h + 1) * 64],
                                 st[:, 16 + h:17 + h], None, ALU.mult)
            ve.drain()
            ve.tensor_tensor(tmw[:, 0:512], tmw[:, 0:512], scB, ALU.mult)
            ve.drain()
            ve.tensor_tensor(tmw2[:, 0:512],
                             dap(tmw, 0, [[2 * 512, 128], [64, 8], [1, 64]]),
                             tab_rep(tab_cs, tab_off + tt), ALU.mult)
            swap = dap(tmw, 32, [[2 * 512, 128], [64, 8], [-32, 2], [1, 32]])
            ve.tensor_tensor(dap(tmw2, 512, [[2 * 512, 128], [64, 8], [1, 64]]),
                             swap, tab_rep(tab_cs + 1, tab_off + tt), ALU.mult)
            ve.drain()
            ve.tensor_tensor(tmw2[:, 0:512], tmw2[:, 0:512], tmw2[:, 512:1024], ALU.add)
            ve.drain().inc("rov")
            te.wait("rov", p.val("rov"))
            lastt = None
            for fc in range(4):
                lastt = te.transpose(pb2[:, fc * 128:(fc + 1) * 128],
                                     tmw2[:, fc * 128:(fc + 1) * 128], identb[:])
            lastt.inc("mmr2")
            ve.wait("mmr2", p.val("mmr2"))
            for fc in range(4):
                ve.tensor_copy(srcbuf[:, fc * width + tt * 128:fc * width + (tt + 1) * 128],
                               pb2[:, fc * 128:(fc + 1) * 128])
            ve.drain().inc("rtw")

    te.wait("qke", p.val("qke"))
    te.wait("vtv", p.val("vtv"))
    rot_block(0)
    rot_block(1)

    if upto == "rot":
        ve.wait("rtw", p.val("rtw"))
        ve.drain().inc("fin")
        sy.wait("fin", p.val("fin"))
        p.finish()
        return p

    te.wait("rtw", p.val("rtw"))
    for nm in ["tmw2", "tmw", "wqb2", "wqb", "qscB", "tabs"]:
        p.sb_free(nm)
    maskb = p.sb_tmp("maskb", [128, 1408], BF16)
    attT_sb = p.sb_tmp("attT_sb", [128, 1024], BF16)
    sm_sb = p.sb_tmp("sm_sb", [128, 1024], F32)
    oT = p.sb_tmp("oT", [128, 4 * XW], BF16)
    wob = p.sb_tmp("wob", [128, 4 * 1024], BF16)
    gp.wait("rtw", p.val("rtw"))
    gp.dma_start(out=maskb[:], in_=masks_in.ap()).inc("lmk", 16)
    for h in range(8):
        hrow = (h % 2) * 64
        hch = h // 2
        for qt in range(NXT):
            qcol = 1536 + qt * 128
            wstart = ((qcol - 128) // 512) * 512 + 128
            it = (qcol - wstart) // 128
            kv0 = wstart - 512
            D_idx = (wstart - 1664) // 512 + 1
            te.wait("ate", p.val("ate"))
            lastm = None
            for half in range(2):
                lastm = te.matmul(
                    pA1[:, half * 512:(half + 1) * 512],
                    qT[hrow:hrow + 64, hch * XW + qt * 128:hch * XW + (qt + 1) * 128],
                    kT[hrow:hrow + 64, hch * NQK + (kv0 - 128) + half * 512:
                       hch * NQK + (kv0 - 128) + (half + 1) * 512],
                    start=True, stop=True, tile_position=(hrow, 0))
            lastm.inc("msim")
            ve.wait("msim", p.val("msim"))
            ve.wait("lmk", p.val("lmk"))
            ve.tensor_tensor(sm_sb[:], pA1[:],
                             maskb[:, (3 - it) * 128:(3 - it) * 128 + 1024], ALU.add)
            ve.drain().inc("ate")
            ve.tensor_scalar(sm_sb[:, 0:512], sm_sb[:, 0:512],
                             kselb[:, D_idx:D_idx + 1], None, ALU.add)
            ve.drain()
            ve.tensor_reduce(st[:, 24:25], sm_sb[:], AX.X, ALU.max)
            ve.drain()
            ve.tensor_scalar(st[:, 25:26], st[:, 24:25], -1.0, None, ALU.mult)
            ve.drain().inc("smx")
            sc.wait("smx", p.val("smx"))
            sc.activation(att_sb[:], sm_sb[:], AF.Exp, bias=st[:, 25:26],
                          accum_out=st[:, 26:27])
            sc.drain().inc("sme")
            ve.wait("sme", p.val("sme"))
            ve.reciprocal(st[:, 27:28], st[:, 26:27])
            ve.drain()
            ve.tensor_scalar(att_sb[:], att_sb[:], st[:, 27:28], None, ALU.mult)
            ve.drain().inc("atv")
            te.wait("atv", p.val("atv"))
            lastt = None
            for i in range(8):
                lastt = te.transpose(pb[:, i * 128:(i + 1) * 128],
                                     att_sb[:, i * 128:(i + 1) * 128], identb[:])
            lastt.inc("mma")
            ve.wait("mma", p.val("mma"))
            ve.tensor_copy(attT_sb[:], pb[:, 0:1024])
            ve.drain().inc("atc")
            te.wait("atc", p.val("atc"))
            te.wait("ocv", p.val("ocv"))
            lastm = None
            for jt in range(8):
                kvt = (kv0 - 128) // 128 + jt
                lastm = te.matmul(pav[:, 0:64], attT_sb[:, jt * 128:(jt + 1) * 128],
                                  v_tm[:, kvt * 512 + h * 64:kvt * 512 + (h + 1) * 64],
                                  start=(jt == 0), stop=(jt == 7))
            lastm.inc("mav")
            ve.wait("mav", p.val("mav"))
            ve.tensor_copy(att_sb[:, 0:64], pav[:, 0:64])
            ve.drain().inc("ocv")
            te.wait("ocv", p.val("ocv"))
            te.wait("otv", p.val("otv"))
            te.transpose(pb2[0:64, 0:128], att_sb[:, 0:64], identb[:]).inc("mmo")
            ve.wait("mmo", p.val("mmo"))
            ve.tensor_copy(oT[hrow:hrow + 64, hch * XW + qt * 128:
                              hch * XW + (qt + 1) * 128],
                           pb2[0:64, 0:128])
            ve.drain().inc("otv")

    if upto == "attn":
        ve.wait("otv", p.val("otv"))
        ve.drain().inc("fin")
        sy.wait("fin", p.val("fin"))
        p.finish()
        return p

    gp.wait("otv", p.val("otv"))
    gp.dma_start(out=dap(wob, 0, [[4 * 1024, 128], [1024, 4], [1, 1024]]),
                 in_=dap(wo_in, 0, [[1024, 128], [128 * 1024, 4], [1, 1024]])
                 ).inc("lwo", 16)
    te.wait("lwo", p.val("lwo"))
    te.wait("otv", p.val("otv"))
    for m in range(FCH):
        for g in range(3):
            pass
        lastm = None
        for n in range((XW + 511) // 512):
            nsz = min(512, XW - n * 512)
            pA = pA1 if (m * 5 + n) % 2 == 0 else pA2
            if p.val("x2e") >= 2:
                te.wait("x2e", p.val("x2e") - 1)
            for kc in range(4):
                lastm = te.matmul(pA[:, 0:nsz], wob[:, kc * 1024 + m * 128:
                                                    kc * 1024 + (m + 1) * 128],
                                  oT[:, kc * XW + n * 512:kc * XW + n * 512 + nsz],
                                  start=(kc == 0), stop=(kc == 3))
            lastm.inc("mx2")
            ve.wait("mx2", p.val("mx2"))
            ve.scalar_tensor_tensor(x2T[:, m * XW + n * 512:m * XW + n * 512 + nsz],
                                    pA[:, 0:nsz], 1.0,
                                    xinb[:, m * TIN + 1536 + n * 512:
                                         m * TIN + 1536 + n * 512 + nsz],
                                    ALU.mult, ALU.add)
            ve.drain().inc("x2e")

    if upto == "x2":
        ve.wait("x2e", p.val("x2e"))
        ve.drain().inc("fin")
        sy.wait("fin", p.val("fin"))
        p.finish()
        return p

    for nm in ["wob", "oT", "sm_sb", "attT_sb", "maskb", "att_sb",
               "v_tm", "qT", "kT", "xinb"]:
        p.sb_free(nm)
    hnT = p.sb_tmp("hnT", [128, FCH * NTOK], BF16)
    ffh = p.sb_tmp("ffh", [128, 22 * 512], BF16)
    w1t = p.sb_tmp("w1t", [128, FCH * 128 * 2], BF16)
    w1t2 = p.sb_tmp("w1t2", [128, FCH * 128 * 2], BF16)
    w2t = p.sb_tmp("w2t", [128, 22 * 128], BF16)
    w2t2 = p.sb_tmp("w2t2", [128, 22 * 128], BF16)
    b1ab = p.sb_tmp("b1ab", [128, 2 * 22], F32)
    stat = p.sb_tmp("stat", [128, 8 * NTT], F32)
    waste2 = p.sb_tmp("waste2", [128, D], F32)
    h_tm = p.sb_tmp("h_tm", [128, NTT * D], BF16)
    hn_tm = p.sb_tmp("hn_tm", [128, NTT * D], BF16)
    hsh = p.sb_tmp("hsh", [128, 4 * NTOK], BF16)

    gp.wait("x2e", p.val("x2e"))
    gp.wait("rtw", p.val("rtw"))
    gp.dma_start(out=b1ab[:, 0:22], in_=b1a_in.ap()).inc("lw2", 16)
    gp.dma_start(out=b1ab[:, 22:44], in_=b1g_in.ap()).inc("lw2", 16)

    ve.wait("x2e", p.val("x2e"))
    for fc in range(4, FCH):
        ve.tensor_copy(hsh[:, (fc - 4) * NTOK:(fc - 3) * NTOK],
                       x2T[:, fc * XW + 127:fc * XW + 127 + NTOK])
    ve.drain().inc("shc")
    te.wait("shc", p.val("shc"))
    for tt in range(NTT):
        te.wait("shv", p.val("shv"))
        lastt = None
        for fc in range(FCH):
            if fc < 4:
                srcap = x2T[:, fc * XW + 128 + tt * 128:fc * XW + 128 + (tt + 1) * 128]
            else:
                srcap = hsh[:, (fc - 4) * NTOK + tt * 128:
                            (fc - 4) * NTOK + (tt + 1) * 128]
            lastt = te.transpose(pb[:, fc * 128:(fc + 1) * 128], srcap, identb[:])
        lastt.inc("mmh")
        ve.wait("mmh", p.val("mmh"))
        ve.tensor_copy(h_tm[:, tt * D:(tt + 1) * D], pb[:])
        ve.drain().inc("shv")
    p.sb_free("hsh")
    gate_ln = emit_ln(p, h_tm, hn_tm, stat, waste2, NTT, D, p.cv(1e-5),
                      gate=("shv", p.val("shv")))
    groups = []
    for fc in range(FCH):
        for g in range(2):
            srcs = [(hn_tm[:, (g * 8 + i) * D + fc * 128:(g * 8 + i) * D + (fc + 1) * 128], 128)
                    for i in range(8)]
            dst = hnT[:, fc * NTOK + g * 1024: fc * NTOK + (g + 1) * 1024]
            groups.append((srcs, dst, 1024))
    gate_hnT = emit_transpose(p, groups, identb, pb, pb2, gate_te=gate_ln)
    ve.wait(*gate_hnT)
    for fc in range(FCH):
        ve.tensor_scalar(hnT[:, fc * NTOK:(fc + 1) * NTOK],
                         hnT[:, fc * NTOK:(fc + 1) * NTOK],
                         lgb[:, fc:fc + 1], lbb[:, fc:fc + 1], ALU.mult, ALU.add)
    ve.drain().inc("hnv")

    if upto == "ln":
        ve.wait("hnv", p.val("hnv"))
        ve.drain().inc("fin")
        sy.wait("fin", p.val("fin"))
        p.finish()
        return p

    p.sb_free("hn_tm")
    p.sb_free("h_tm")
    if final:
        out_tm = p.sb_tmp("out_tm", [128, NTT * D], F32)
        s1buf = p.sb_tmp("s1buf", [128, 2 * 1024], F32)
        gp.wait("hnv", p.val("hnv"))
        sy.wait("hnv", p.val("hnv"))
        for c in range(4):
            sy.dma_start(out=dap(out_tm, c * 4 * D, [[NTT * D, 128], [D, 4], [1, D]]),
                         in_=dap(x1_own, c * 4 * 128 * D, [[D, 128], [128 * D, 4], [1, D]])
                         ).inc("lx1", 16)
    else:
        xi_st = p.sb_tmp("xi_st", [128, 2 * 1024], BF16)

    M1 = 22
    for tp in range(4):
        tok0 = tp * 512
        m1_base = p.val("mf1")
        for m in range(M1):
            wt = w1t if m % 2 == 0 else w1t2
            msz = 128 if m < 21 else 42
            if m >= 2:
                gp.wait("mf1", m1_base + (m - 1))
            elif tp >= 1:
                gp.wait("mxi", p.val("mxi"))
            lsem = "lwf2a" if m % 2 == 0 else "lwf2b"
            gp.dma_start(out=dap(wt, 0, [[2 * FCH * 128, 128], [128, FCH], [1, msz]]),
                         in_=dap(w1a_in, m * 128, [[2730, 128], [128 * 2730, FCH], [1, msz]])
                         ).inc(lsem, 16)
            gp.dma_start(out=dap(wt, FCH * 128, [[2 * FCH * 128, 128], [128, FCH], [1, msz]]),
                         in_=dap(w1g_in, m * 128, [[2730, 128], [128 * 2730, FCH], [1, msz]])
                         ).inc(lsem, 16)
            te.wait(lsem, p.val(lsem))
            te.wait("hnv", p.val("hnv"))
            te.wait("lw2", p.val("lw2"))
            if tp >= 1 and m < 2:
                te.wait("xie", p.val("xie"))
            pA = pA1 if (tp * M1 + m) % 2 == 0 else pA2
            if p.val("f1e") >= 2:
                te.wait("f1e", p.val("f1e") - 1)
            lastm = None
            for kc in range(FCH):
                te.matmul(pA[0:msz, 0:512], wt[:, kc * 128:kc * 128 + msz],
                          hnT[:, kc * NTOK + tok0:kc * NTOK + tok0 + 512],
                          start=(kc == 0), stop=(kc == FCH - 1))
            for kc in range(FCH):
                lastm = te.matmul(pA[0:msz, 512:1024],
                                  wt[:, FCH * 128 + kc * 128:FCH * 128 + kc * 128 + msz],
                                  hnT[:, kc * NTOK + tok0:kc * NTOK + tok0 + 512],
                                  start=(kc == 0), stop=(kc == FCH - 1))
            lastm.inc("mf1")
            sc.wait("mf1", p.val("mf1"))
            sc.activation(waste2[0:msz, 0:512], pA[0:msz, 0:512], AF.Identity,
                          bias=b1ab[0:msz, m:m + 1])
            sc.activation(waste2[0:msz, 512:1024], pA[0:msz, 512:1024], AF.Gelu,
                          bias=b1ab[0:msz, 22 + m:22 + m + 1])
            sc.drain().inc("f1s")
            ve.wait("f1s", p.val("f1s"))
            ve.tensor_tensor(ffh[0:msz, m * 512:(m + 1) * 512],
                             waste2[0:msz, 0:512], waste2[0:msz, 512:1024], ALU.mult)
            ve.drain().inc("f1e")
        m2_base = p.val("mxi")
        for m in range(FCH):
            wt = w2t if m % 2 == 0 else w2t2
            if m >= 2:
                gp.wait("mxi", m2_base + (m - 1))
            else:
                gp.wait("mf1", p.val("mf1"))
            lsem = "lwf2a" if m % 2 == 0 else "lwf2b"
            gp.dma_start(out=dap(wt, 0, [[22 * 128, 128], [128, 22], [1, 128]]),
                         in_=dap(w2_in, m * 128, [[D, 128], [128 * D, 22], [1, 128]])
                         ).inc(lsem, 16)
            te.wait(lsem, p.val(lsem))
            if m < 2:
                te.wait("f1e", p.val("f1e"))
            pA = pA1 if (tp * FCH + m) % 2 == 0 else pA2
            if p.val("xie") >= 2:
                te.wait("xie", p.val("xie") - 1)
            lastm = None
            for kc in range(M1):
                ksz = 128 if kc < 21 else 42
                lastm = te.matmul(pA[:, 0:512],
                                  wt[0:ksz, kc * 128:(kc + 1) * 128],
                                  ffh[0:ksz, kc * 512:(kc + 1) * 512],
                                  start=(kc == 0), stop=(kc == M1 - 1))
            lastm.inc("mxi")
            sc.wait("mxi", p.val("mxi"))
            if final:
                sc.wait("prv0", p.val("prv0"))
                sc.activation(waste2[:, 0:512], pA[:, 0:512], AF.Identity,
                              bias=b2b[:, m:m + 1])
                sc.drain().inc("xis")
                ve.wait("xis", p.val("xis"))
                ve.tensor_tensor(waste2[:, 512:1024], waste2[:, 0:512],
                                 x2T[:, m * XW + 128 + tok0:
                                      m * XW + 128 + tok0 + 512], ALU.add)
                ve.drain().inc("xiv")
                sc.wait("xiv", p.val("xiv"))
                sc.activation(waste2[:, 0:512], waste2[:, 512:1024], AF.Silu)
                sc.drain().inc("g1s")
                buf = (tp * FCH + m) % 2
                sy.wait("prv", max(0, p.val("prv") - 1))
                lsem = "ls1a" if buf == 0 else "ls1b"
                sy.dma_start(out=s1buf[:, buf * 1024:buf * 1024 + 512],
                             in_=dap(s1T_own, m * 128 * NTOK + tok0,
                                     [[NTOK, 128], [1, 512]])).inc(lsem, 16)
                ve.wait(lsem, p.val(lsem))
                ve.wait("g1s", p.val("g1s"))
                ve.tensor_tensor(s1buf[:, buf * 1024:buf * 1024 + 512],
                                 s1buf[:, buf * 1024:buf * 1024 + 512],
                                 waste2[:, 0:512], ALU.mult)
                ve.drain().inc("prv0")
                te.wait("prv0", p.val("prv0"))
                te.wait("prc", p.val("prc"))
                lastt = None
                for i in range(4):
                    lastt = te.transpose(pf32[:, i * 128:(i + 1) * 128],
                                         s1buf[:, buf * 1024 + i * 128:
                                               buf * 1024 + (i + 1) * 128],
                                         identf[:])
                lastt.inc("mpr")
                ve.wait("mpr", p.val("mpr"))
                ve.wait("lx1", 64)
                tt0 = tok0 // 128
                for i in range(4):
                    ve.tensor_tensor(out_tm[:, (tt0 + i) * D + m * 128:
                                            (tt0 + i) * D + (m + 1) * 128],
                                     out_tm[:, (tt0 + i) * D + m * 128:
                                            (tt0 + i) * D + (m + 1) * 128],
                                     pf32[:, i * 128:(i + 1) * 128], ALU.add)
                ve.drain().inc("prc")
                ve.engine_nop().inc("prv")
                ve.engine_nop().inc("xie")
            else:
                buf = (tp * FCH + m) % 2
                sc.wait("so", p.val("so"))
                sc.activation(xi_st[:, buf * 1024:buf * 1024 + 512], pA[:, 0:512],
                              AF.Identity, bias=b2b[:, m:m + 1])
                sc.drain().inc("xis")
                ve.wait("xis", p.val("xis"))
                ve.tensor_tensor(xi_st[:, buf * 1024:buf * 1024 + 512],
                                 xi_st[:, buf * 1024:buf * 1024 + 512],
                                 x2T[:, m * XW + 128 + tok0:
                                      m * XW + 128 + tok0 + 512], ALU.add)
                ve.drain().inc("xiv")
                sy.wait("xiv", p.val("xiv"))
                sy.dma_start(out=dap(o_xi, m * 128 * NTOK + tok0,
                                     [[NTOK, 128], [1, 512]]),
                             in_=xi_st[:, buf * 1024:buf * 1024 + 512]).inc("so", 16)
                ve.engine_nop().inc("xie")
    if final:
        ve.wait("prc", p.val("prc"))
        sy.wait("prc", p.val("prc"))
        for q4 in range(4):
            sy.dma_start(out=dap(o_out, q4 * 4 * 128 * D, [[D, 128], [128 * D, 4], [1, D]]),
                         in_=dap(out_tm, q4 * 4 * D, [[NTT * D, 128], [D, 4], [1, D]])
                         ).inc("so", 16)
    sy.wait("so", p.val("so"))
    p.finish()
    return p


def _rotary_tables(positions):
    half_idx = np.arange(0, 64, 2, dtype=np.float64)
    inv_freq = 1.0 / 10000.0 ** (half_idx / 64.0)
    t = positions.astype(np.float64)
    fr = t[:, None] * inv_freq[None, :]
    freqs = np.concatenate([fr, fr], axis=1)
    base = (half_idx + 0.4 * 64) / (1.4 * 64)
    power = (t - 4096.0) / 256.0
    scale = np.concatenate([base[None, :] ** power[:, None]] * 2, axis=1)
    cos, sin = np.cos(freqs), np.sin(freqs)
    sgn = np.concatenate([-np.ones(32), np.ones(32)])[None, :]
    CSq = (cos * scale).astype(np.float32)
    SSq = (sgn * sin * scale).astype(np.float32)
    CSk = (cos / scale).astype(np.float32)
    SSk = (sgn * sin / scale).astype(np.float32)
    return CSq, SSq, CSk, SSk


def l1ffn_inmaps(inputs, xiT_full, layer, final, x1_list=None, s1T_full=None):
    g = lambda n: _f32(inputs[n])
    wqkv = np.ascontiguousarray(g("L1_ffn_wqkv")[layer])
    wo = np.ascontiguousarray(g("L1_ffn_wo")[layer])
    w1 = g("L1_ffn_w1")[layer]
    w1a = np.ascontiguousarray(w1[:, :2730])
    w1g = np.ascontiguousarray(w1[:, 2730:])
    b1 = g("L1_ffn_b1")[layer]
    b1a = np.zeros((128, 22), np.float32)
    b1g = np.zeros((128, 22), np.float32)
    b1a.T.flat[:2730] = b1[:2730]
    b1g.T.flat[:2730] = b1[2730:]
    b1a = np.zeros((128, 22), np.float32); b1g = np.zeros((128, 22), np.float32)
    tmp = np.zeros(2816, np.float32); tmp[:2730] = b1[:2730]
    b1a[:, :] = tmp.reshape(22, 128).T
    tmp = np.zeros(2816, np.float32); tmp[:2730] = b1[2730:]
    b1g[:, :] = tmp.reshape(22, 128).T
    w2 = np.zeros((2816, D), np.float32)
    w2[:2730] = g("L1_ffn_w2")[layer]
    b2 = np.ascontiguousarray(g("L1_ffn_b2")[layer].reshape(FCH, 128).T)
    lg = np.ascontiguousarray(g("L1_ffn_ln_g")[layer].reshape(FCH, 128).T)
    lb = np.ascontiguousarray(g("L1_ffn_ln_b")[layer].reshape(FCH, 128).T)
    qsc = np.ascontiguousarray(np.tile(g("L1_ffn_q_scale")[layer], 8)[None, :])
    ksc = np.ascontiguousarray(np.tile(g("L1_ffn_k_scale")[layer], 8)[None, :])
    pp = np.arange(128)[:, None]
    uu = np.arange(1408)[None, :]
    maskg = np.where((uu - 384 - pp >= 0) & (uu - 384 - pp <= 512), 0.0, -1e30
                     ).astype(np.float32)
    shared = dict(wqkv=wqkv, wo_in=wo, w1a_in=w1a, w1g_in=w1g, w2_in=w2,
                  b1a_in=b1a, b1g_in=b1g, b2_in=b2, lg_in=lg, lb_in=lb,
                  qsc_in=qsc, ksc_in=ksc, masks_in=maskg,
                  ones_in=np.ones((1, 128), np.float32),
                  iden=np.eye(128, dtype=np.float32),
                  cvec=np.stack([np.full(128, 1e-12, np.float32),
                                 np.full(128, 1e-5, np.float32)], 1))
    maps = []
    bf = ml_dtypes.bfloat16
    for c in range(NCORES):
        b = c // 4
        t0 = (c % 4) * 2048
        g0 = t0 - 1664
        m = dict(shared)
        xin = np.zeros((D, TIN), bf)
        lo = max(0, g0)
        xin[:, lo - g0:] = xiT_full[b][:, lo:g0 + TIN]
        m["xin"] = xin
        pos = np.arange(g0 + 128, g0 + TIN)
        csq, ssq, csk, ssk = _rotary_tables(pos)
        m["csq_in"], m["ssq_in"], m["csk_in"], m["ssk_in"] = csq, ssq, csk, ssk
        ksel = np.zeros((128, 8), np.float32)
        for Dw in range(-1, 4):
            if t0 + 512 * (Dw - 1) < 0:
                ksel[:, Dw + 1] = -1e30
        m["ksel_in"] = ksel
        if final:
            m["x1_own"] = x1_list[c]
            m["s1T_own"] = np.ascontiguousarray(
                s1T_full[b][:, t0:t0 + NTOK])
        maps.append(m)
    return maps


def kernel(**inputs):
    from concourse.bass_utils import run_bass_kernel_spmd
    bf = ml_dtypes.bfloat16
    cores = list(range(NCORES))

    def asbf(a):
        a = np.ascontiguousarray(np.asarray(a))
        return a.view(bf) if a.dtype != bf else a

    res = run_bass_kernel_spmd(build_prep().nc, prep_inmaps(inputs), core_ids=cores)
    prep = {
        "xnT": [np.asarray(res.results[c]["o_xnT"]) for c in cores],
        "kside": [np.asarray(res.results[c]["o_kside"]).reshape(3, 128, NTOK)
                  for c in cores],
        "pT": [np.asarray(res.results[c]["o_pT"]) for c in cores],
        "mvoT": [np.asarray(res.results[c]["o_mvoT"]) for c in cores],
        "gram": [np.asarray(res.results[c]["o_gram"]) for c in cores],
    }

    res = run_bass_kernel_spmd(build_l0attn().nc, l0attn_inmaps(inputs, prep),
                               core_ids=cores)
    s0T = [np.asarray(res.results[c]["o_s0T"]) for c in cores]

    res = run_bass_kernel_spmd(build_l0ffn().nc, l0ffn_inmaps(inputs, prep),
                               core_ids=cores)
    g0T = [np.asarray(res.results[c]["o_g0T"]) for c in cores]

    res = run_bass_kernel_spmd(build_resid1().nc, resid1_inmaps(inputs, s0T, g0T),
                               core_ids=cores)
    x1_list = [np.asarray(res.results[c]["o_x1"]) for c in cores]
    xn1T = [asbf(res.results[c]["o_xn1T"]) for c in cores]
    r1 = {"qT": [asbf(res.results[c]["o_qT"]) for c in cores],
          "kT": [asbf(res.results[c]["o_kT"]) for c in cores],
          "vtm": [asbf(res.results[c]["o_vtm"]) for c in cores]}

    res = run_bass_kernel_spmd(build_l1attn(0, 8, True).nc,
                               l1attn_inmaps(inputs, r1, 1), core_ids=cores)
    s1z = [np.asarray(res.results[c]["o_s1T"]) for c in cores]
    s1T_full = [np.zeros((D, T), np.float32) for _ in range(2)]
    for c in cores:
        b, s = c // 4, c % 4
        strip = np.concatenate([np.arange(64) + (4 * ib + s) * 64 for ib in range(32)])
        s1T_full[b][:, strip] = s1z[c]

    xiT_full = [np.concatenate(xn1T[b * 4:(b + 1) * 4], axis=1) for b in range(2)]
    res = run_bass_kernel_spmd(build_l1ffn(False).nc,
                               l1ffn_inmaps(inputs, xiT_full, 0, False),
                               core_ids=cores)
    xi = [asbf(res.results[c]["o_xi"]) for c in cores]

    xi_full = [np.concatenate(xi[b * 4:(b + 1) * 4], axis=1) for b in range(2)]
    res = run_bass_kernel_spmd(
        build_l1ffn(True).nc,
        l1ffn_inmaps(inputs, xi_full, 1, True, x1_list=x1_list, s1T_full=s1T_full),
        core_ids=cores)
    out = np.empty((GT, D), np.float32)
    for c in cores:
        out[c * NTOK:(c + 1) * NTOK] = np.asarray(res.results[c]["o_out"])
    return out.reshape(2, T, D)
